# version 09
# baseline (speedup 1.0000x reference)
import math
import numpy as np
import ml_dtypes
import concourse.bass as bass
import concourse.mybir as mybir
from concourse.bass_utils import run_bass_kernel_spmd

F32 = mybir.dt.float32
BF16 = mybir.dt.bfloat16
AF = mybir.ActivationFunctionType
ALU = mybir.AluOpType
AX = mybir.AxisListType

B = 2
S = 16384
D = 1024
HD = 128
NT = 512
ST = S // NT
Q = 4096
QT = Q // NT
DC = D // 128
KC = S // 128
NG = KC // 2
RSQ = 1.0 / math.sqrt(128.0)
EPS = 1e-5


class _Ser:

    def __init__(self, eng, sem, st):
        self._e = eng
        self._sem = sem
        self._st = st

    def wait_ge(self, sem, v):
        self._e.wait_ge(sem, v)

    def __getattr__(self, name):
        f = getattr(self._e, name)

        def g(*a, inc=None, edge=True, **kw):
            p = self._st["prev"]
            if p is not None:
                self._e.wait_ge(p[0], p[1])
            ins = f(*a, **kw)
            if inc is None:
                self._st["n"] += 1
                ins.then_inc(self._sem, 1)
                self._st["prev"] = (self._sem, self._st["n"])
            else:
                sem, k, after = inc
                ins.then_inc(sem, k)
                if edge and after is not None:
                    self._st["prev"] = (sem, after)
            return ins
        return g


def build_nc():
    nc = bass.Bass("TRN2", target_bir_lowering=False, dynamic_dma_scratch_size=16384)

    xT = nc.dram_tensor("xT", [128, ST, DC, NT], F32, kind="ExternalInput").ap()
    cs = nc.dram_tensor("cs", [128, S], F32, kind="ExternalInput").ap()
    sc = nc.dram_tensor("sc", [128, S], F32, kind="ExternalInput").ap()
    csq = nc.dram_tensor("csq", [128, Q], F32, kind="ExternalInput").ap()
    scq = nc.dram_tensor("scq", [128, Q], F32, kind="ExternalInput").ap()
    aq = nc.dram_tensor("aq", [128, DC, HD], F32, kind="ExternalInput").ap()
    ak = nc.dram_tensor("ak", [128, DC, HD], F32, kind="ExternalInput").ap()
    av = nc.dram_tensor("av", [128, DC, HD], F32, kind="ExternalInput").ap()
    ao = nc.dram_tensor("ao", [HD, D], F32, kind="ExternalInput").ap()
    fr = nc.dram_tensor("fr", [128, DC, DC, 128], F32, kind="ExternalInput").ap()
    fk = nc.dram_tensor("fk", [128, DC, DC, 128], F32, kind="ExternalInput").ap()
    fv = nc.dram_tensor("fv", [128, DC, DC, 128], F32, kind="ExternalInput").ap()
    mur = nc.dram_tensor("mur", [128, DC], F32, kind="ExternalInput").ap()
    muk = nc.dram_tensor("muk", [128, DC], F32, kind="ExternalInput").ap()
    onesd = nc.dram_tensor("onesd", [128, 128], F32, kind="ExternalInput").ap()
    identd = nc.dram_tensor("identd", [128, 128], F32, kind="ExternalInput").ap()
    halo = nc.dram_tensor("halo", [128, 1], F32, kind="ExternalInput").ap()
    y = nc.dram_tensor("y", [128, QT, DC, NT], F32, kind="ExternalOutput").ap()

    from contextlib import ExitStack
    with ExitStack() as top:
        ec = top.enter_context
        Ksb = ec(nc.sbuf_tensor("Ksb", [128, S], BF16))
        Vsb = ec(nc.sbuf_tensor("Vsb", [128, S], BF16))
        aq_s = ec(nc.sbuf_tensor("aq_s", [128, DC, 128], BF16))
        ak_s = ec(nc.sbuf_tensor("ak_s", [128, DC, 128], BF16))
        av_s = ec(nc.sbuf_tensor("av_s", [128, DC, 128], BF16))
        ao_s = ec(nc.sbuf_tensor("ao_s", [128, D], BF16))
        mur_s = ec(nc.sbuf_tensor("mur_s", [128, DC], F32))
        muk_s = ec(nc.sbuf_tensor("muk_s", [128, DC], F32))
        ones_s = ec(nc.sbuf_tensor("ones_s", [128, 128], BF16))
        id_s = ec(nc.sbuf_tensor("id_s", [128, 128], F32))
        halo_s = ec(nc.sbuf_tensor("halo_s", [128, 1], F32))
        w1k_s = ec(nc.sbuf_tensor("w1k_s", [1, 128], BF16))
        w1v_s = ec(nc.sbuf_tensor("w1v_s", [1, 128], BF16))
        eps_s = ec(nc.sbuf_tensor("eps_s", [1, 1], F32))
        m_sb = ec(nc.sbuf_tensor("m_sb", [1, NT], F32))
        e2_sb = ec(nc.sbuf_tensor("e2_sb", [1, NT], F32))
        var_sb = ec(nc.sbuf_tensor("var_sb", [1, NT], F32))
        r_sb = ec(nc.sbuf_tensor("r_sb", [1, NT], F32))
        mr_sb = ec(nc.sbuf_tensor("mr_sb", [1, NT], BF16))
        rhi_sb = ec(nc.sbuf_tensor("rhi_sb", [1, NT], BF16))
        rlo_sb = ec(nc.sbuf_tensor("rlo_sb", [1, NT], BF16))
        rb_sb = ec(nc.sbuf_tensor("rb_sb", [128, NT], F32))
        sm_sb = ec(nc.sbuf_tensor("sm_sb", [1, 8], F32))
        bbf_sb = ec(nc.sbuf_tensor("bbf_sb", [1, 1], BF16))
        bias_sb = ec(nc.sbuf_tensor("bias_sb", [128, 1], F32))
        rden_sb, dhi_sb, dlo_sb = r_sb, rhi_sb, rlo_sb
        xbf = ec(nc.sbuf_tensor("xbf", [128, 2, DC, NT], BF16))
        x2 = ec(nc.sbuf_tensor("x2", [128, DC, NT], BF16))
        tab = ec(nc.sbuf_tensor("tab", [128, 1, 2, NT], F32))
        A = ec(nc.psum_tensor("A", [128, 1024], F32))
        Bp = ec(nc.psum_tensor("Bp", [128, 1024], F32))
        C = ec(nc.psum_tensor("C", [128, 512], F32))
        Dp = ec(nc.psum_tensor("Dp", [128, 512], F32))
        Fp = ec(nc.psum_tensor("Fp", [128, 512], F32))
        G = ec(nc.psum_tensor("G", [128, 512], F32))

        sems = {}
        for name in ["s_w", "s_misc", "s_w1", "s_w1e", "s_init",
                     "s_x0", "s_x1", "s_tab", "s_sq", "s_serows", "s_st", "s_var",
                     "s_fin", "s_fin2", "s_rb", "s_rbe", "s_csr", "s_tek",
                     "s_tev", "s_kp", "s_k2s", "s_k2m", "s_kmax", "s_vs",
                     "s_vt", "s_ve2", "s_p1d",
                     "s_xf", "s_xb2", "s_tabq", "s_fwk", "s_fwr", "s_fwv",
                     "s_st2", "s_sq2",
                     "s_srow2", "s_var2", "s_fin_2", "s_rows2", "s_rbmm2",
                     "s_xn", "s_q", "s_qr", "s_q2", "s_qn", "s_qmaxd",
                     "s_qb", "s_bmm", "s_bias", "s_qk", "s_exp", "s_av",
                     "s_den", "s_dbm", "s_o", "s_mix", "s_kpre", "s_rl",
                     "s_kh", "s_rpre", "s_g", "s_facc", "s_out", "s_store",
                     "s_vser", "s_sser", "s_yini", "s_xb3"]:
            sems[name] = ec(nc.semaphore(name))
        sm = type("SM", (), sems)
        vst = {"prev": None, "n": 0}
        sst = {"prev": None, "n": 0}

        block = ec(nc.Block())

        with ExitStack() as p1:
            e1 = p1.enter_context
            csr = e1(nc.sbuf_tensor("csr", [128, NT], F32))
            scr = e1(nc.sbuf_tensor("scr", [128, NT], F32))
            p1s = e1(nc.sbuf_tensor("p1s", [128, NT], F32))
            p2s = e1(nc.sbuf_tensor("p2s", [128, NT], F32))
            vt = e1(nc.sbuf_tensor("vt", [128, NT], F32))
            k2 = e1(nc.sbuf_tensor("k2", [128, NT], BF16))

            @block.gpsimd
            def _(g):
                for wi, (dst, srcw) in enumerate([(aq_s, aq), (ak_s, ak), (av_s, av),
                                                  (ao_s, ao), (ones_s, onesd), (id_s, identd)]):
                    if wi >= 2:
                        g.wait_ge(sm.s_w, 16 * wi)
                    g.dma_start(out=dst[:], in_=srcw).then_inc(sm.s_w, 16)
                for t in range(ST):
                    if t >= 2:
                        g.wait_ge(sm.s_st, 2 * (t - 1))
                        g.wait_ge(sm.s_tek, t - 1)
                        g.wait_ge(sm.s_tev, t - 1)
                        g.wait_ge(sm.s_sq, t - 1)
                    g.dma_start(out=xbf[:, t % 2, :, :],
                                in_=xT[:, t, :, :]).then_inc(
                                    sm.s_x0 if t % 2 == 0 else sm.s_x1, 16)
                g.nop().then_inc(sm.s_p1d, 1)

            @block.sync
            def _(s):
                s.dma_start(out=mur_s[:], in_=mur).then_inc(sm.s_misc, 16)
                s.dma_start(out=muk_s[:], in_=muk).then_inc(sm.s_misc, 16)
                s.dma_start(out=halo_s[:], in_=halo).then_inc(sm.s_misc, 16)
                for t in range(ST):
                    if t >= 1:
                        s.wait_ge(sm.s_csr, t)
                    s.dma_start(out=tab[:, 0, 0, :], in_=cs[:, t * NT:(t + 1) * NT]).then_inc(sm.s_tab, 16)
                    s.dma_start(out=tab[:, 0, 1, :], in_=sc[:, t * NT:(t + 1) * NT]).then_inc(sm.s_tab, 16)
                s.nop().then_inc(sm.s_p1d, 1)

            @block.tensor
            def _(t):
                t.wait_ge(sm.s_w, 96)
                for c in range(DC):
                    ins = t.matmul(Dp[0:1, :128], ones_s[:, 0:1], ak_s[:, c, :],
                                   start=(c == 0), stop=(c == DC - 1))
                ins.then_inc(sm.s_w1, 1)
                for c in range(DC):
                    ins = t.matmul(Dp[32:33, :128], ones_s[:, 0:1], av_s[:, c, :],
                                   start=(c == 0), stop=(c == DC - 1))
                ins.then_inc(sm.s_w1, 1)

                def stats_mms(tt):
                    t.wait_ge(sm.s_x0 if tt % 2 == 0 else sm.s_x1, 16 * (tt // 2 + 1))
                    t.wait_ge(sm.s_serows, tt)
                    if tt == 0:
                        t.wait_ge(sm.s_w1e, 1)
                    for c in range(DC):
                        ins = t.matmul(Dp[0:1, :], ones_s[:, 0:1], xbf[:, tt % 2, c, :],
                                       start=(c == 0), stop=(c == DC - 1))
                    ins.then_inc(sm.s_st, 1)
                    t.wait_ge(sm.s_sq, tt + 1)
                    for c in range(DC):
                        ins = t.matmul(Dp[32:33, :], ones_s[:, 0:1], x2[:, c, :],
                                       start=(c == 0), stop=(c == DC - 1))
                    ins.then_inc(sm.s_st, 1)

                def mid_mms(tt):
                    h0 = (tt % 2) * 512
                    t.wait_ge(sm.s_fin2, tt + 1)
                    t.wait_ge(sm.s_rbe, tt)
                    t.matmul(G[:, :], ones_s[0:1, :], rhi_sb[0:1, :], start=True, stop=False)
                    t.matmul(G[:, :], ones_s[0:1, :], rlo_sb[0:1, :], start=False, stop=True).then_inc(sm.s_rb, 1)
                    t.wait_ge(sm.s_kp, max(0, tt - 1))
                    for c in range(DC):
                        t.matmul(A[:, h0:h0 + 512], ak_s[:, c, :], xbf[:, tt % 2, c, :],
                                 start=(c == 0), stop=False)
                    t.matmul(A[:, h0:h0 + 512], w1k_s[0:1, :], mr_sb[0:1, :],
                             start=False, stop=True).then_inc(sm.s_tek, 1)
                    t.wait_ge(sm.s_vs, max(0, tt - 1))
                    for c in range(DC):
                        t.matmul(Bp[:, h0:h0 + 512], av_s[:, c, :], xbf[:, tt % 2, c, :],
                                 start=(c == 0), stop=False)
                    t.matmul(Bp[:, h0:h0 + 512], w1v_s[0:1, :], mr_sb[0:1, :],
                             start=False, stop=True).then_inc(sm.s_tev, 1)

                def tail_mms(tt):
                    t.wait_ge(sm.s_vs, tt + 1)
                    t.wait_ge(sm.s_ve2, tt)
                    for j in range(4):
                        t.transpose(Fp[:, j * 128:(j + 1) * 128],
                                    vt[:, j * 128:(j + 1) * 128],
                                    id_s[:]).then_inc(sm.s_vt, 1)
                    t.wait_ge(sm.s_k2s, tt + 1)
                    t.wait_ge(sm.s_kmax, tt)
                    t.matmul(C[0:1, :], ones_s[:, 0:1], k2[:, :],
                             start=True, stop=True).then_inc(sm.s_k2m, 1)

                for stage in range(ST + 2):
                    if stage < ST:
                        stats_mms(stage)
                    if 1 <= stage <= ST:
                        mid_mms(stage - 1)
                    if stage >= 2:
                        tail_mms(stage - 2)
                t.wait_ge(sm.s_w1, 2)
                t.wait_ge(sm.s_st, 2 * ST)
                t.wait_ge(sm.s_rb, ST)
                t.wait_ge(sm.s_tek, ST)
                t.wait_ge(sm.s_tev, ST)
                t.wait_ge(sm.s_vt, 4 * ST)
                t.wait_ge(sm.s_k2m, ST)
                t.nop().then_inc(sm.s_p1d, 1)

            @block.scalar
            def _(se0):
                se = _Ser(se0, sm.s_sser, sst)
                se.wait_ge(sm.s_init, 1)
                for stage in range(ST + 2):
                    if stage < ST:
                        tt = stage
                        se.wait_ge(sm.s_x0 if tt % 2 == 0 else sm.s_x1, 16 * (tt // 2 + 1))
                        se.wait_ge(sm.s_st, 2 * tt)
                        se.activation(out=x2.ap().rearrange("p c n -> p (c n)"),
                                      in_=xbf[:, tt % 2, :, :].rearrange("p c n -> p (c n)"),
                                      func=AF.Square, inc=(sm.s_sq, 1, tt + 1))
                        se.wait_ge(sm.s_st, 2 * tt + 2)
                        se.wait_ge(sm.s_var, tt)
                        se.wait_ge(sm.s_fin2, tt)
                        se.activation(out=m_sb[0:1, :], in_=Dp[0:1, :], func=AF.Copy,
                                      scale=1.0 / D)
                        se.activation(out=e2_sb[0:1, :], in_=Dp[32:33, :], func=AF.Copy,
                                      scale=1.0 / D, inc=(sm.s_serows, 1, tt + 1))
                        se.wait_ge(sm.s_var, tt + 1)
                        se.activation(out=var_sb[0:1, :], in_=var_sb[0:1, :], func=AF.Ln,
                                      bias=eps_s[0:1, :])
                        se.activation(out=r_sb[0:1, :], in_=var_sb[0:1, :], func=AF.Exp,
                                      scale=-0.5, inc=(sm.s_fin, 1, tt + 1))
                    if stage >= 2:
                        tt = stage - 2
                        se.wait_ge(sm.s_kp, tt + 1)
                        se.wait_ge(sm.s_k2m, tt)
                        se.activation(out=k2[:, :], in_=Ksb[:, tt * NT:(tt + 1) * NT],
                                      func=AF.Square, inc=(sm.s_k2s, 1, tt + 1))
                se.nop(inc=(sm.s_p1d, 1, None))

            @block.vector
            def _(v0):
                v = _Ser(v0, sm.s_vser, vst)
                v.memset(sm_sb[0:1, :], 0.0)
                v.memset(eps_s[0:1, :], EPS, inc=(sm.s_init, 1, 1))
                v.wait_ge(sm.s_w1, 2)
                v.tensor_scalar(out=w1k_s[0:1, :], in0=Dp[0:1, :128], scalar1=-1.0,
                                scalar2=None, op0=ALU.mult)
                v.tensor_scalar(out=w1v_s[0:1, :], in0=Dp[32:33, :128], scalar1=-1.0,
                                scalar2=None, op0=ALU.mult, inc=(sm.s_w1e, 1, 1))
                for stage in range(ST + 2):
                    if stage < ST:
                        tt = stage
                        v.wait_ge(sm.s_serows, tt + 1)
                        v.wait_ge(sm.s_fin, tt)
                        v.tensor_tensor(out=var_sb[0:1, :], in0=m_sb[0:1, :],
                                        in1=m_sb[0:1, :], op=ALU.mult)
                        v.tensor_tensor(out=var_sb[0:1, :], in0=e2_sb[0:1, :],
                                        in1=var_sb[0:1, :], op=ALU.subtract,
                                        inc=(sm.s_var, 1, tt + 1))
                        v.wait_ge(sm.s_fin, tt + 1)
                        v.wait_ge(sm.s_tek, tt)
                        v.wait_ge(sm.s_tev, tt)
                        v.wait_ge(sm.s_rb, tt)
                        v.tensor_copy(mr_sb[0:1, :], m_sb[0:1, :])
                        v.tensor_copy(rhi_sb[0:1, :], r_sb[0:1, :])
                        v.tensor_tensor(out=rlo_sb[0:1, :], in0=r_sb[0:1, :],
                                        in1=rhi_sb[0:1, :], op=ALU.subtract,
                                        inc=(sm.s_fin2, 1, tt + 1))
                    if 1 <= stage <= ST:
                        tt = stage - 1
                        h0 = (tt % 2) * 512
                        v.wait_ge(sm.s_rb, tt + 1)
                        v.tensor_copy(rb_sb[:, :], G[:, :], inc=(sm.s_rbe, 1, tt + 1))
                        v.wait_ge(sm.s_tab, 32 * (tt + 1))
                        v.tensor_tensor(out=csr[:, :], in0=tab[:, 0, 0, :],
                                        in1=rb_sb[:, :], op=ALU.mult)
                        v.tensor_tensor(out=scr[:, :], in0=tab[:, 0, 1, :],
                                        in1=rb_sb[:, :], op=ALU.mult,
                                        inc=(sm.s_csr, 1, tt + 1))
                        v.wait_ge(sm.s_tek, tt + 1)
                        v.tensor_tensor(out=p1s[0:64, :], in0=csr[0:64, :], in1=A[0:64, h0:h0 + 512], op=ALU.mult)
                        v.tensor_tensor(out=p2s[0:64, :], in0=scr[0:64, :], in1=A[64:128, h0:h0 + 512], op=ALU.mult)
                        v.tensor_tensor(out=Ksb[0:64, tt * NT:(tt + 1) * NT],
                                        in0=p1s[0:64, :], in1=p2s[0:64, :], op=ALU.subtract)
                        v.tensor_tensor(out=p1s[64:128, :], in0=csr[64:128, :], in1=A[0:64, h0:h0 + 512], op=ALU.mult)
                        v.tensor_tensor(out=p2s[64:128, :], in0=scr[64:128, :], in1=A[64:128, h0:h0 + 512], op=ALU.mult)
                        v.tensor_tensor(out=Ksb[64:128, tt * NT:(tt + 1) * NT],
                                        in0=p1s[64:128, :], in1=p2s[64:128, :], op=ALU.add,
                                        inc=(sm.s_kp, 1, tt + 1))
                        v.wait_ge(sm.s_tev, tt + 1)
                        v.wait_ge(sm.s_vt, 4 * tt)
                        v.tensor_tensor(out=vt[:, :], in0=Bp[:, h0:h0 + 512], in1=rb_sb[:, :],
                                        op=ALU.mult, inc=(sm.s_vs, 1, tt + 1))
                    if stage >= 2:
                        tt = stage - 2
                        v.wait_ge(sm.s_vt, 4 * (tt + 1))
                        v.tensor_copy(Vsb[:, tt * NT:(tt + 1) * NT], Fp[:, :],
                                      inc=(sm.s_ve2, 1, tt + 1))
                        v.wait_ge(sm.s_k2m, tt + 1)
                        v.reduce_max(sm_sb[0:1, 1:2], C[0:1, :], axis=AX.X)
                        v.tensor_tensor(out=sm_sb[0:1, 0:1], in0=sm_sb[0:1, 0:1],
                                        in1=sm_sb[0:1, 1:2], op=ALU.max,
                                        inc=(sm.s_kmax, 1, tt + 1))
                v.nop(inc=(sm.s_p1d, 1, None))

        with ExitStack() as p2:
            e2 = p2.enter_context
            fbuf = [e2(nc.sbuf_tensor(f"fbuf{i}", [128, DC, DC, 128], BF16)) for i in range(2)]
            xn = e2(nc.sbuf_tensor("xn", [128, 2, DC, NT + 1], BF16))
            q_sb = e2(nc.sbuf_tensor("q_sb", [128, NT], BF16))
            q2_sb = e2(nc.sbuf_tensor("q2_sb", [128, NT], BF16))
            Pr = e2(nc.sbuf_tensor("Pr", [128, 2, 1024], BF16))
            o_sb = e2(nc.sbuf_tensor("o_sb", [128, NT], BF16))
            xr_sb = e2(nc.sbuf_tensor("xr_sb", [128, DC, NT], BF16))
            xk_sb = e2(nc.sbuf_tensor("xk_sb", [128, DC, NT], BF16))
            kh_sb = e2(nc.sbuf_tensor("kh_sb", [128, DC, NT], BF16))
            d_sb = e2(nc.sbuf_tensor("d_sb", [128, NT], BF16))
            tt_sb = e2(nc.sbuf_tensor("tt_sb", [128, 1, NT], BF16))
            t1_sb = e2(nc.sbuf_tensor("t1_sb", [128, NT], F32))
            outs = e2(nc.sbuf_tensor("outs", [128, 1, NT], F32))
            g_sb = e2(nc.sbuf_tensor("g_sb", [128, DC, NT], BF16))
            p1q, p2q = t1_sb, rb_sb

            NL = QT + 1

            @block.gpsimd
            def _(g):
                def xload(e):
                    if e >= 2:
                        g.wait_ge(sm.s_st2, 2 * (e - 1))
                        g.wait_ge(sm.s_sq2, e - 1)
                        g.wait_ge(sm.s_xn, e - 1)
                    g.dma_start(out=xbf[:, e % 2, :, :],
                                in_=xT[:, e, :, :]).then_inc(
                                    sm.s_xb2 if e % 2 == 0 else sm.s_xb3, 16)

                g.wait_ge(sm.s_p1d, 5)
                xload(0)
                xload(1)
                xload(2)
                for ell in range(1, NL):
                    if ell + 2 < NL:
                        xload(ell + 2)
                    if ell >= 1:
                        u = ell - 1
                        for i, (w, fsem) in enumerate([(fk, sm.s_fwk), (fr, sm.s_fwr),
                                                       (fv, sm.s_fwv)]):
                            li = 3 * u + i
                            buf = fbuf[li % 2]
                            if i == 0 and u >= 1:
                                g.wait_ge(sm.s_rpre, 8 * u)
                            if i == 1 and u >= 1:
                                g.wait_ge(sm.s_facc, 16 * u)
                            if i == 1:
                                g.wait_ge(sm.s_fwk, 16 * (u + 1))
                            if i == 2:
                                g.wait_ge(sm.s_kpre, 8 * u + 8)
                                g.wait_ge(sm.s_fwr, 16 * (u + 1))
                            g.dma_start(out=buf[:], in_=w).then_inc(fsem, 16)
                    if ell >= 1:
                        u = ell - 1
                        if ell == 1:
                            g.wait_ge(sm.s_yini, 16)
                        for dc in range(DC):
                            g.wait_ge(sm.s_out, 8 * u + dc + 1)
                            if 8 * u + dc >= 1:
                                g.wait_ge(sm.s_store, 16 * (8 * u + dc))
                            g.dma_start(out=y[:, u, dc, :], in_=outs[:, 0, :],
                                        accum_op=ALU.add).then_inc(sm.s_store, 16)
                g.wait_ge(sm.s_store, 16 * 8 * QT)

            @block.sync
            def _(s):
                s.wait_ge(sm.s_p1d, 5)
                s.dma_start(out=y[:, :, :, :], in_=xT[:, 1:QT + 1, :, :]).then_inc(sm.s_yini, 16)
                for ell in range(NL):
                    u = ell - 1
                    if ell >= 1:
                        s.wait_ge(sm.s_qr, u)
                        s.dma_start(out=tab[:, 0, 0, :], in_=csq[:, u * NT:(u + 1) * NT]).then_inc(sm.s_tabq, 16)
                        s.dma_start(out=tab[:, 0, 1, :], in_=scq[:, u * NT:(u + 1) * NT]).then_inc(sm.s_tabq, 16)

            @block.tensor
            def _(t):
                t.wait_ge(sm.s_p1d, 5)

                def stats_g(e):
                    t.wait_ge(sm.s_xb2 if e % 2 == 0 else sm.s_xb3, 16 * (e // 2 + 1))
                    t.wait_ge(sm.s_srow2, e)
                    for c in range(DC):
                        ins = t.matmul(Dp[32:33, :], ones_s[:, 0:1], xbf[:, e % 2, c, :],
                                       start=(c == 0), stop=(c == DC - 1))
                    ins.then_inc(sm.s_st2, 1)
                    t.wait_ge(sm.s_sq2, e + 1)
                    for c in range(DC):
                        ins = t.matmul(Dp[64:65, :], ones_s[:, 0:1], x2[:, c, :],
                                       start=(c == 0), stop=(c == DC - 1))
                    ins.then_inc(sm.s_st2, 1)

                def bcast_g(e):
                    t.wait_ge(sm.s_rows2, e + 1)
                    t.wait_ge(sm.s_xn, max(0, e - 1))
                    t.wait_ge(sm.s_o, max(0, e - 2))
                    t.wait_ge(sm.s_bias, max(0, e - 1))
                    t.wait_ge(sm.s_out, 8 * max(0, e - 2))
                    t.matmul(G[:, :], ones_s[0:1, :], rhi_sb[0:1, :], start=True, stop=False)
                    t.matmul(G[:, :], ones_s[0:1, :], rlo_sb[0:1, :],
                             start=False, stop=True).then_inc(sm.s_rbmm2, 1)
                    t.matmul(Fp[:, :], ones_s[0:1, :], mr_sb[0:1, :],
                             start=True, stop=True).then_inc(sm.s_rbmm2, 1)

                def qproj_g(uu):
                    t.wait_ge(sm.s_xn, uu + 2)
                    t.wait_ge(sm.s_o, uu)
                    for c in range(DC):
                        ins = t.matmul(C[:, :], aq_s[:, c, :], xn[:, (uu + 1) % 2, c, 1:NT + 1],
                                       start=(c == 0), stop=(c == DC - 1))
                    ins.then_inc(sm.s_q, 1)

                def qnorm_g(uu):
                    t.wait_ge(sm.s_q2, uu + 1)
                    t.wait_ge(sm.s_qmaxd, uu)
                    t.wait_ge(sm.s_den, uu)
                    t.matmul(Dp[0:1, :], ones_s[:, 0:1], q2_sb[:, :],
                             start=True, stop=True).then_inc(sm.s_qn, 1)
                    t.wait_ge(sm.s_qb, uu + 1)
                    t.wait_ge(sm.s_xn, min(uu + 2, NL))
                    t.matmul(G[:, 0:1], ones_s[0:1, :], bbf_sb[0:1, :],
                             start=True, stop=True).then_inc(sm.s_bmm, 1)

                def flash(u, ell):
                    t.wait_ge(sm.s_qr, u + 1)
                    t.wait_ge(sm.s_bias, u + 1)
                    t.wait_ge(sm.s_den, u)
                    if u >= 1:
                        t.wait_ge(sm.s_g, 8 * u)
                        t.wait_ge(sm.s_rl, 8 * u)
                        t.wait_ge(sm.s_out, 8 * u)

                    def den_av(gj):
                        t.wait_ge(sm.s_exp, 64 * u + gj + 1)
                        for h in range(2):
                            kc = 2 * gj + h
                            t.matmul(Dp[0:1, :], ones_s[:, 0:1], Pr[:, gj % 2, h * 512:(h + 1) * 512],
                                     start=(kc == 0), stop=(kc == KC - 1)).then_inc(sm.s_av, 1)
                            t.matmul(C[:, :], Vsb[:, kc * 128:(kc + 1) * 128],
                                     Pr[:, gj % 2, h * 512:(h + 1) * 512],
                                     start=(kc == 0), stop=(kc == KC - 1)).then_inc(sm.s_av, 1)

                    for gi in range(NG):
                        pg = A if gi % 2 == 0 else Bp
                        t.wait_ge(sm.s_exp, max(0, 64 * u + gi - 1))
                        for h in range(2):
                            kc = 2 * gi + h
                            t.matmul(pg[:, h * 512:(h + 1) * 512],
                                     Ksb[:, kc * 128:(kc + 1) * 128], q_sb[:, :],
                                     start=True, stop=True).then_inc(sm.s_qk, 1)
                        if gi == 2 and ell + 1 < NL:
                            stats_g(ell + 1)
                        if gi == 10 and ell + 1 < NL:
                            bcast_g(ell + 1)
                        if gi >= 1:
                            den_av(gi - 1)
                    den_av(NG - 1)

                stats_g(0)
                bcast_g(0)
                stats_g(1)
                bcast_g(1)
                qproj_g(0)
                qnorm_g(0)

                for ell in range(1, NL):
                    u = ell - 1
                    flash(u, ell)
                    t.wait_ge(sm.s_den, u + 1)
                    t.wait_ge(sm.s_xn, min(ell + 2, NL))
                    t.matmul(G[:, :], ones_s[0:1, :], dhi_sb[0:1, :], start=True, stop=False)
                    t.matmul(G[:, :], ones_s[0:1, :], dlo_sb[0:1, :],
                             start=False, stop=True).then_inc(sm.s_dbm, 1)
                    t.wait_ge(sm.s_fwk, 16 * (u + 1))
                    for dc in range(DC):
                        pf = (A if dc % 2 == 0 else Bp)
                        t.wait_ge(sm.s_mix, 8 * u + dc + 1)
                        t.wait_ge(sm.s_rl, max(0, 8 * u + dc - 1))
                        for c in range(DC):
                            ins = t.matmul(pf[:, 0:512], fbuf[(3 * u) % 2][:, c, dc, :],
                                           xk_sb[:, c, :], start=(c == 0), stop=(c == DC - 1))
                        ins.then_inc(sm.s_kpre, 1)
                    if u + 1 <= QT - 1:
                        qproj_g(u + 1)
                    t.wait_ge(sm.s_fwr, 16 * (u + 1))
                    for dc in range(DC):
                        pf = (A if dc % 2 == 0 else Bp)
                        t.wait_ge(sm.s_g, max(0, 8 * u + dc - 1))
                        if dc in (0, 1):
                            t.wait_ge(sm.s_rl, 8 * u + 7 + dc)
                        for c in range(DC):
                            ins = t.matmul(pf[:, 0:512], fbuf[(3 * u + 1) % 2][:, c, dc, :],
                                           xr_sb[:, c, :], start=(c == 0), stop=(c == DC - 1))
                        ins.then_inc(sm.s_rpre, 1)
                    if u + 1 <= QT - 1:
                        qnorm_g(u + 1)
                    t.wait_ge(sm.s_fwv, 16 * (u + 1))
                    t.wait_ge(sm.s_o, u + 1)
                    t.wait_ge(sm.s_kh, 8 * u + 8)
                    t.wait_ge(sm.s_qr, min(u + 2, QT))
                    t.wait_ge(sm.s_bias, min(u + 2, QT))
                    for dc in range(DC):
                        pacc = (A if dc % 2 == 0 else Bp)
                        pff = (C if dc % 2 == 0 else G)
                        t.wait_ge(sm.s_out, max(0, 8 * u + dc - 1))
                        if dc in (0, 1):
                            t.wait_ge(sm.s_g, 8 * u + 7 + dc)
                        t.matmul(pacc[:, 512:1024], ao_s[:, dc * 128:(dc + 1) * 128],
                                 o_sb[:, :], start=True, stop=True).then_inc(sm.s_facc, 1)
                        for c in range(DC):
                            ins = t.matmul(pff[:, 0:512], fbuf[(3 * u + 2) % 2][:, c, dc, :],
                                           kh_sb[:, c, :], start=(c == 0), stop=(c == DC - 1))
                        ins.then_inc(sm.s_facc, 1)

            @block.scalar
            def _(se0):
                se = _Ser(se0, sm.s_sser, sst)
                se.wait_ge(sm.s_p1d, 5)

                def sq_g(e):
                    se.wait_ge(sm.s_xb2 if e % 2 == 0 else sm.s_xb3, 16 * (e // 2 + 1))
                    se.wait_ge(sm.s_st2, 2 * e)
                    se.activation(out=x2.ap().rearrange("p c n -> p (c n)"),
                                  in_=xbf[:, e % 2, :, :].rearrange("p c n -> p (c n)"),
                                  func=AF.Square, inc=(sm.s_sq2, 1, e + 1))

                def rows_g(e):
                    se.wait_ge(sm.s_st2, 2 * (e + 1))
                    se.wait_ge(sm.s_var2, e)
                    se.wait_ge(sm.s_rows2, e)
                    se.wait_ge(sm.s_den, max(0, e - 2))
                    se.activation(out=m_sb[0:1, :], in_=Dp[32:33, :], func=AF.Copy, scale=1.0 / D)
                    se.activation(out=e2_sb[0:1, :], in_=Dp[64:65, :], func=AF.Copy,
                                  scale=1.0 / D, inc=(sm.s_srow2, 1, e + 1))
                    se.wait_ge(sm.s_var2, e + 1)
                    se.activation(out=var_sb[0:1, :], in_=var_sb[0:1, :], func=AF.Ln,
                                  bias=eps_s[0:1, :])
                    se.activation(out=r_sb[0:1, :], in_=var_sb[0:1, :], func=AF.Exp,
                                  scale=-0.5, inc=(sm.s_fin_2, 1, e + 1))

                def q2_g(uu):
                    se.wait_ge(sm.s_qr, uu + 1)
                    se.wait_ge(sm.s_qn, uu)
                    se.activation(out=q2_sb[:, :], in_=q_sb[:, :], func=AF.Square,
                                  inc=(sm.s_q2, 1, uu + 1))

                sq_g(0)
                sq_g(1)
                rows_g(0)
                rows_g(1)
                sq_g(2)
                q2_g(0)

                for ell in range(1, NL):
                    u = ell - 1
                    se.wait_ge(sm.s_bias, u + 1)
                    for gi in range(NG):
                        pg = A if gi % 2 == 0 else Bp
                        se.wait_ge(sm.s_qk, 128 * u + 2 * gi + 2)
                        se.wait_ge(sm.s_av, max(0, 4 * (64 * u + gi - 1)))
                        se.activation(out=Pr[:, gi % 2, :], in_=pg[:, :], func=AF.Exp,
                                      scale=RSQ, bias=bias_sb[:, :],
                                      inc=(sm.s_exp, 1, 64 * u + gi + 1))
                        if gi == 7:
                            if ell + 1 < NL:
                                rows_g(ell + 1)
                            if ell + 2 < NL:
                                sq_g(ell + 2)
                    for dc in range(DC):
                        pf = (A if dc % 2 == 0 else Bp)
                        se.wait_ge(sm.s_kpre, 8 * u + dc + 1)
                        se.wait_ge(sm.s_kh, 8 * u + dc)
                        se.activation(out=tt_sb[:, 0, :], in_=pf[:, 0:512], func=AF.Relu,
                                      inc=(sm.s_rl, 1, 8 * u + dc + 1))
                    se.wait_ge(sm.s_out, 8 * u)
                    for dc in range(DC):
                        pf = (A if dc % 2 == 0 else Bp)
                        se.wait_ge(sm.s_rpre, 8 * u + dc + 1)
                        se.wait_ge(sm.s_kh, 8 * u + 8)
                        se.activation(out=tt_sb[:, 0, :], in_=pf[:, 0:512], func=AF.Tanh, scale=0.5)
                        se.activation(out=g_sb[:, dc, :], in_=tt_sb[:, 0, :], func=AF.Copy,
                                      scale=0.5, bias=0.5, inc=(sm.s_g, 1, 8 * u + dc + 1))
                    if u + 1 <= QT - 1:
                        q2_g(u + 1)

            @block.vector
            def _(v0):
                v = _Ser(v0, sm.s_vser, vst)
                v.wait_ge(sm.s_p1d, 5)

                def rows_v(e):
                    v.wait_ge(sm.s_srow2, e + 1)
                    v.wait_ge(sm.s_fin_2, e)
                    v.tensor_tensor(out=var_sb[0:1, :], in0=m_sb[0:1, :], in1=m_sb[0:1, :], op=ALU.mult)
                    v.tensor_tensor(out=var_sb[0:1, :], in0=e2_sb[0:1, :], in1=var_sb[0:1, :],
                                    op=ALU.subtract, inc=(sm.s_var2, 1, e + 1))
                    v.wait_ge(sm.s_fin_2, e + 1)
                    v.wait_ge(sm.s_rbmm2, 2 * e)
                    v.tensor_tensor(out=mr_sb[0:1, :], in0=m_sb[0:1, :], in1=r_sb[0:1, :], op=ALU.mult)
                    v.tensor_copy(rhi_sb[0:1, :], r_sb[0:1, :])
                    v.tensor_tensor(out=rlo_sb[0:1, :], in0=r_sb[0:1, :], in1=rhi_sb[0:1, :],
                                    op=ALU.subtract, inc=(sm.s_rows2, 1, e + 1))

                def xn_v(e):
                    v.wait_ge(sm.s_rbmm2, 2 * (e + 1))
                    v.wait_ge(sm.s_xb2 if e % 2 == 0 else sm.s_xb3, 16 * (e // 2 + 1))
                    if e >= 2:
                        v.wait_ge(sm.s_mix, 8 * (e - 1))
                    v.tensor_copy(rb_sb[:, :], G[:, :])
                    for dc in range(DC):
                        v.tensor_tensor(out=t1_sb[:, :], in0=xbf[:, e % 2, dc, :],
                                        in1=rb_sb[:, :], op=ALU.mult)
                        v.tensor_tensor(out=xn[:, e % 2, dc, 1:NT + 1], in0=t1_sb[:, :],
                                        in1=Fp[:, :], op=ALU.subtract)
                        if e >= 1:
                            v.tensor_copy(xn[:, e % 2, dc, 0:1], xn[:, (e - 1) % 2, dc, NT:NT + 1])
                        else:
                            v.memset(xn[:, 0, dc, 0:1], 0.0)
                    if e == 0:
                        v.wait_ge(sm.s_misc, 48)
                        for dc in range(DC):
                            v.tensor_scalar(out=xn[:, 0, dc, NT:NT + 1], in0=xn[:, 0, dc, NT:NT + 1],
                                            scalar1=halo_s[:, :], scalar2=None, op0=ALU.mult)
                    v.nop(inc=(sm.s_xn, 1, e + 1))

                def rope_v(uu):
                    v.wait_ge(sm.s_q, uu + 1)
                    v.wait_ge(sm.s_tabq, 32 * (uu + 1))
                    v.tensor_tensor(out=p1q[0:64, :], in0=tab[0:64, 0, 0, :], in1=C[0:64, :], op=ALU.mult)
                    v.tensor_tensor(out=p2q[0:64, :], in0=tab[0:64, 0, 1, :], in1=C[64:128, :], op=ALU.mult)
                    v.tensor_tensor(out=q_sb[0:64, :], in0=p1q[0:64, :], in1=p2q[0:64, :], op=ALU.subtract)
                    v.tensor_tensor(out=p1q[64:128, :], in0=tab[64:128, 0, 0, :], in1=C[0:64, :], op=ALU.mult)
                    v.tensor_tensor(out=p2q[64:128, :], in0=tab[64:128, 0, 1, :], in1=C[64:128, :], op=ALU.mult)
                    v.tensor_tensor(out=q_sb[64:128, :], in0=p1q[64:128, :], in1=p2q[64:128, :],
                                    op=ALU.add, inc=(sm.s_qr, 1, uu + 1))

                def mix_v(uu):
                    e = uu + 1
                    if uu >= 1:
                        v.wait_ge(sm.s_rpre, 8 * uu)
                        v.wait_ge(sm.s_kpre, 8 * uu)
                    for dc in range(DC):
                        v.tensor_tensor(out=d_sb[:, :], in0=xn[:, e % 2, dc, 1:NT + 1],
                                        in1=xn[:, e % 2, dc, 0:NT], op=ALU.subtract)
                        v.scalar_tensor_tensor(out=xr_sb[:, dc, :], in0=d_sb[:, :],
                                               scalar=mur_s[:, dc:dc + 1], in1=xn[:, e % 2, dc, 0:NT],
                                               op0=ALU.mult, op1=ALU.add)
                        v.scalar_tensor_tensor(out=xk_sb[:, dc, :], in0=d_sb[:, :],
                                               scalar=muk_s[:, dc:dc + 1], in1=xn[:, e % 2, dc, 0:NT],
                                               op0=ALU.mult, op1=ALU.add,
                                               inc=(sm.s_mix, 1, 8 * uu + dc + 1))

                def bias_v(uu):
                    v.wait_ge(sm.s_qn, uu + 1)
                    v.reduce_max(sm_sb[0:1, 1:2], Dp[0:1, :], axis=AX.X,
                                 inc=(sm.s_qmaxd, 1, uu + 1))
                    v.tensor_tensor(out=sm_sb[0:1, 2:3], in0=sm_sb[0:1, 1:2], in1=sm_sb[0:1, 0:1], op=ALU.add)
                    v.tensor_scalar(out=sm_sb[0:1, 3:4], in0=sm_sb[0:1, 2:3],
                                    scalar1=-0.5 * RSQ, scalar2=None, op0=ALU.mult)
                    v.tensor_copy(bbf_sb[0:1, :], sm_sb[0:1, 3:4], inc=(sm.s_qb, 1, uu + 1))
                    v.wait_ge(sm.s_bmm, uu + 1)
                    v.tensor_copy(bias_sb[:, :], G[:, 0:1], inc=(sm.s_bias, 1, uu + 1))

                rows_v(0)
                xn_v(0)
                rows_v(1)
                xn_v(1)
                rope_v(0)
                mix_v(0)
                bias_v(0)

                for ell in range(1, NL):
                    u = ell - 1
                    if ell + 1 < NL:
                        rows_v(ell + 1)
                        xn_v(ell + 1)
                    v.wait_ge(sm.s_av, 256 * (u + 1))
                    v.reciprocal(rden_sb[0:1, :], Dp[0:1, :])
                    v.tensor_copy(dhi_sb[0:1, :], rden_sb[0:1, :])
                    v.tensor_tensor(out=dlo_sb[0:1, :], in0=rden_sb[0:1, :], in1=dhi_sb[0:1, :],
                                    op=ALU.subtract, inc=(sm.s_den, 1, u + 1))
                    v.wait_ge(sm.s_dbm, u + 1)
                    v.tensor_copy(rb_sb[:, :], G[:, :])
                    v.tensor_tensor(out=o_sb[:, :], in0=C[:, :], in1=rb_sb[:, :],
                                    op=ALU.mult, inc=(sm.s_o, 1, u + 1))
                    if u >= 1:
                        v.wait_ge(sm.s_facc, 16 * u)
                    for dc in range(DC):
                        v.wait_ge(sm.s_rl, 8 * u + dc + 1)
                        v.tensor_tensor(out=kh_sb[:, dc, :], in0=tt_sb[:, 0, :],
                                        in1=tt_sb[:, 0, :],
                                        op=ALU.mult, inc=(sm.s_kh, 1, 8 * u + dc + 1))
                    if u + 1 <= QT - 1:
                        rope_v(u + 1)
                        mix_v(u + 1)
                        bias_v(u + 1)
                    for dc in range(DC):
                        pacc = (A if dc % 2 == 0 else Bp)
                        pff = (C if dc % 2 == 0 else G)
                        v.wait_ge(sm.s_facc, 16 * u + 2 * dc + 2)
                        v.wait_ge(sm.s_g, 8 * u + dc + 1)
                        if 8 * u + dc >= 1:
                            v.wait_ge(sm.s_store, 16 * (8 * u + dc))
                        v.tensor_tensor(out=t1_sb[:, :], in0=g_sb[:, dc, :], in1=pff[:, 0:512], op=ALU.mult)
                        v.tensor_tensor(out=outs[:, 0, :], in0=t1_sb[:, :], in1=pacc[:, 512:1024],
                                        op=ALU.add, inc=(sm.s_out, 1, 8 * u + dc + 1))
    return nc


_NC_CACHE = None


def kernel(L0_attn_x, L0_attn_wq, L0_attn_wk, L0_attn_wv, L0_attn_wo,
           L0_ffn_W_r, L0_ffn_W_k, L0_ffn_W_v, L0_ffn_mu_r, L0_ffn_mu_k):
    global _NC_CACHE
    in_maps = make_in_maps(L0_attn_x, L0_attn_wq, L0_attn_wk, L0_attn_wv,
                           L0_attn_wo, L0_ffn_W_r, L0_ffn_W_k, L0_ffn_W_v,
                           L0_ffn_mu_r, L0_ffn_mu_k)
    if _NC_CACHE is None:
        _NC_CACHE = build_nc()
    res = run_bass_kernel_spmd(_NC_CACHE, in_maps, core_ids=list(range(8)))

    out = np.empty((B, S, D), dtype=np.float32)
    for core in range(8):
        b, qi = core // 4, core % 4
        yv = res.results[core]["y"]
        out[b, qi * Q:(qi + 1) * Q, :] = yv.transpose(1, 3, 2, 0).reshape(Q, D)
    return out


def make_in_maps(L0_attn_x, L0_attn_wq, L0_attn_wk, L0_attn_wv, L0_attn_wo,
                 L0_ffn_W_r, L0_ffn_W_k, L0_ffn_W_v, L0_ffn_mu_r, L0_ffn_mu_k):
    x = np.asarray(L0_attn_x, dtype=np.float32)
    wq = np.asarray(L0_attn_wq, dtype=np.float32)
    wk = np.asarray(L0_attn_wk, dtype=np.float32)
    wv = np.asarray(L0_attn_wv, dtype=np.float32)
    wo = np.asarray(L0_attn_wo, dtype=np.float32)
    Wr = np.asarray(L0_ffn_W_r, dtype=np.float32)
    Wk = np.asarray(L0_ffn_W_k, dtype=np.float32)
    Wv = np.asarray(L0_ffn_W_v, dtype=np.float32)
    mu_r = np.asarray(L0_ffn_mu_r, dtype=np.float32)
    mu_k = np.asarray(L0_ffn_mu_k, dtype=np.float32)

    try:
        import jax
        import jax.numpy as jnp
        with jax.default_device(jax.devices("cpu")[0]):
            inv_j = 1.0 / 10000.0 ** (jnp.arange(0, HD, 2, dtype=jnp.float32) / HD)
            t_j = jnp.arange(S - 1, dtype=jnp.float32)
            ang_j = t_j[:, None] * inv_j[None, :]
            cosr = np.asarray(jnp.cos(ang_j))
            sinr = np.asarray(jnp.sin(ang_j))
    except Exception:
        inv32 = (1.0 / (np.float32(10000.0) **
                        (np.arange(0, HD, 2, dtype=np.float32) / np.float32(HD)))
                 ).astype(np.float32)
        t32 = np.arange(S - 1, dtype=np.float32)
        ang32 = (t32[:, None] * inv32[None, :]).astype(np.float32)
        cosr = np.cos(ang32).astype(np.float32)
        sinr = np.sin(ang32).astype(np.float32)
    cosf = np.empty((64, S), dtype=np.float32)
    sinf = np.empty((64, S), dtype=np.float32)
    cosf[:, 1:] = cosr.T
    sinf[:, 1:] = sinr.T
    cosf[:, 0] = 1.0
    sinf[:, 0] = 0.0
    CS = np.concatenate([cosf, sinf], axis=0)
    SC = np.concatenate([sinf, cosf], axis=0)

    perm_h = np.concatenate([np.arange(0, HD, 2), np.arange(1, HD, 2)])

    def _wT_pch(w):
        return np.ascontiguousarray(w.T.reshape(DC, 128, HD).transpose(1, 0, 2))

    def _fT_pcdk(w):
        return np.ascontiguousarray(w.T.reshape(DC, 128, DC, 128).transpose(1, 0, 2, 3))

    aq_h = _wT_pch(wq[perm_h])
    ak_h = _wT_pch(wk[perm_h])
    av_h = _wT_pch(wv)
    ao_h = np.ascontiguousarray(wo.T)
    fr_h = _fT_pcdk(Wr)
    fk_h = _fT_pcdk(Wk)
    fv_h = _fT_pcdk(Wv)
    mur_h = np.ascontiguousarray(mu_r.reshape(DC, 128).T)
    muk_h = np.ascontiguousarray(mu_k.reshape(DC, 128).T)
    ones_h = np.ones((128, 128), dtype=np.float32)
    ident_h = np.eye(128, dtype=np.float32)

    in_maps = []
    for core in range(8):
        b, qi = core // 4, core % 4
        q0 = qi * Q
        if qi == 0:
            perm = np.concatenate([np.arange(S - NT, S), np.arange(0, Q), np.arange(Q, S - NT)])
            halo_h = np.zeros((128, 1), dtype=np.float32)
        else:
            perm = np.concatenate([np.arange(q0 - NT, q0 + Q), np.arange(0, q0 - NT),
                                   np.arange(q0 + Q, S)])
            halo_h = np.ones((128, 1), dtype=np.float32)
        xT_h = np.ascontiguousarray(
            x[b].T[:, perm].reshape(DC, 128, ST, NT).transpose(1, 2, 0, 3))
        in_maps.append({
            "xT": xT_h,
            "cs": np.ascontiguousarray(CS[:, perm]),
            "sc": np.ascontiguousarray(SC[:, perm]),
            "csq": np.ascontiguousarray(CS[:, q0:q0 + Q]),
            "scq": np.ascontiguousarray(SC[:, q0:q0 + Q]),
            "aq": aq_h, "ak": ak_h, "av": av_h, "ao": ao_h,
            "fr": fr_h, "fk": fk_h, "fv": fv_h,
            "mur": mur_h, "muk": muk_h,
            "onesd": ones_h, "identd": ident_h, "halo": halo_h,
        })
    return in_maps


# version 10
# speedup vs baseline: 1.0000x; 1.0000x over previous
import math
import numpy as np
import ml_dtypes
import concourse.bass as bass
import concourse.mybir as mybir
from concourse.bass_utils import run_bass_kernel_spmd

F32 = mybir.dt.float32
BF16 = mybir.dt.bfloat16
AF = mybir.ActivationFunctionType
ALU = mybir.AluOpType
AX = mybir.AxisListType

B = 2
S = 16384
D = 1024
HD = 128
NT = 512
ST = S // NT
Q = 4096
QT = Q // NT
DC = D // 128
KC = S // 128
NG = KC // 2
RSQ = 1.0 / math.sqrt(128.0)
EPS = 1e-5


class _Ser:

    def __init__(self, eng, sem, st):
        self._e = eng
        self._sem = sem
        self._st = st

    def wait_ge(self, sem, v):
        self._e.wait_ge(sem, v)

    def __getattr__(self, name):
        f = getattr(self._e, name)

        def g(*a, inc=None, edge=True, **kw):
            p = self._st["prev"]
            if p is not None:
                self._e.wait_ge(p[0], p[1])
            ins = f(*a, **kw)
            if inc is None:
                self._st["n"] += 1
                ins.then_inc(self._sem, 1)
                self._st["prev"] = (self._sem, self._st["n"])
            else:
                sem, k, after = inc
                ins.then_inc(sem, k)
                if edge and after is not None:
                    self._st["prev"] = (sem, after)
            return ins
        return g


def build_nc():
    nc = bass.Bass("TRN2", target_bir_lowering=False, dynamic_dma_scratch_size=16384)

    xT = nc.dram_tensor("xT", [128, ST, DC, NT], F32, kind="ExternalInput").ap()
    cs = nc.dram_tensor("cs", [128, S], F32, kind="ExternalInput").ap()
    sc = nc.dram_tensor("sc", [128, S], F32, kind="ExternalInput").ap()
    csq = nc.dram_tensor("csq", [128, Q], F32, kind="ExternalInput").ap()
    scq = nc.dram_tensor("scq", [128, Q], F32, kind="ExternalInput").ap()
    aq = nc.dram_tensor("aq", [128, DC, HD], F32, kind="ExternalInput").ap()
    ak = nc.dram_tensor("ak", [128, DC, HD], F32, kind="ExternalInput").ap()
    av = nc.dram_tensor("av", [128, DC, HD], F32, kind="ExternalInput").ap()
    ao = nc.dram_tensor("ao", [HD, D], F32, kind="ExternalInput").ap()
    fr = nc.dram_tensor("fr", [128, DC, DC, 128], F32, kind="ExternalInput").ap()
    fk = nc.dram_tensor("fk", [128, DC, DC, 128], F32, kind="ExternalInput").ap()
    fv = nc.dram_tensor("fv", [128, DC, DC, 128], F32, kind="ExternalInput").ap()
    mur = nc.dram_tensor("mur", [128, DC], F32, kind="ExternalInput").ap()
    muk = nc.dram_tensor("muk", [128, DC], F32, kind="ExternalInput").ap()
    onesd = nc.dram_tensor("onesd", [128, 128], F32, kind="ExternalInput").ap()
    identd = nc.dram_tensor("identd", [128, 128], F32, kind="ExternalInput").ap()
    halo = nc.dram_tensor("halo", [128, 1], F32, kind="ExternalInput").ap()
    y = nc.dram_tensor("y", [128, QT, DC, NT], F32, kind="ExternalOutput").ap()

    from contextlib import ExitStack
    with ExitStack() as top:
        ec = top.enter_context
        Ksb = ec(nc.sbuf_tensor("Ksb", [128, S], BF16))
        Vsb = ec(nc.sbuf_tensor("Vsb", [128, S], BF16))
        aq_s = ec(nc.sbuf_tensor("aq_s", [128, DC, 128], BF16))
        ak_s = ec(nc.sbuf_tensor("ak_s", [128, DC, 128], BF16))
        av_s = ec(nc.sbuf_tensor("av_s", [128, DC, 128], BF16))
        ao_s = ec(nc.sbuf_tensor("ao_s", [128, D], BF16))
        mur_s = ec(nc.sbuf_tensor("mur_s", [128, DC], F32))
        muk_s = ec(nc.sbuf_tensor("muk_s", [128, DC], F32))
        ones_s = ec(nc.sbuf_tensor("ones_s", [128, 128], BF16))
        id_s = ec(nc.sbuf_tensor("id_s", [128, 128], F32))
        halo_s = ec(nc.sbuf_tensor("halo_s", [128, 1], F32))
        w1k_s = ec(nc.sbuf_tensor("w1k_s", [1, 128], BF16))
        w1v_s = ec(nc.sbuf_tensor("w1v_s", [1, 128], BF16))
        eps_s = ec(nc.sbuf_tensor("eps_s", [1, 1], F32))
        m_sb = ec(nc.sbuf_tensor("m_sb", [1, NT], F32))
        e2_sb = ec(nc.sbuf_tensor("e2_sb", [1, NT], F32))
        var_sb = ec(nc.sbuf_tensor("var_sb", [1, NT], F32))
        r_sb = ec(nc.sbuf_tensor("r_sb", [1, NT], F32))
        mr_sb = ec(nc.sbuf_tensor("mr_sb", [1, NT], BF16))
        rhi_sb = ec(nc.sbuf_tensor("rhi_sb", [1, NT], BF16))
        rlo_sb = ec(nc.sbuf_tensor("rlo_sb", [1, NT], BF16))
        rb_sb = ec(nc.sbuf_tensor("rb_sb", [128, NT], F32))
        sm_sb = ec(nc.sbuf_tensor("sm_sb", [1, 8], F32))
        bbf_sb = ec(nc.sbuf_tensor("bbf_sb", [1, 1], BF16))
        bias_sb = ec(nc.sbuf_tensor("bias_sb", [128, 1], F32))
        rden_sb, dhi_sb, dlo_sb = r_sb, rhi_sb, rlo_sb
        xbf = ec(nc.sbuf_tensor("xbf", [128, 2, DC, NT], BF16))
        x2 = ec(nc.sbuf_tensor("x2", [128, DC, NT], BF16))
        tab = ec(nc.sbuf_tensor("tab", [128, 1, 2, NT], F32))
        A = ec(nc.psum_tensor("A", [128, 1024], F32))
        Bp = ec(nc.psum_tensor("Bp", [128, 1024], F32))
        C = ec(nc.psum_tensor("C", [128, 512], F32))
        Dp = ec(nc.psum_tensor("Dp", [128, 512], F32))
        Fp = ec(nc.psum_tensor("Fp", [128, 512], F32))
        G = ec(nc.psum_tensor("G", [128, 512], F32))

        sems = {}
        for name in ["s_w", "s_misc", "s_w1", "s_w1e", "s_init",
                     "s_x0", "s_x1", "s_tab", "s_sq", "s_serows", "s_st", "s_var",
                     "s_fin", "s_fin2", "s_rb", "s_rbe", "s_csr", "s_tek",
                     "s_tev", "s_kp", "s_k2s", "s_k2m", "s_kmax", "s_vs",
                     "s_vt", "s_ve2", "s_p1d",
                     "s_xf", "s_xb2", "s_tabq", "s_fwk", "s_fwr", "s_fwv",
                     "s_st2", "s_sq2",
                     "s_srow2", "s_var2", "s_fin_2", "s_rows2", "s_rbmm2",
                     "s_xn", "s_q", "s_qr", "s_q2", "s_qn", "s_qmaxd",
                     "s_qb", "s_bmm", "s_bias", "s_qk", "s_exp", "s_av",
                     "s_den", "s_dbm", "s_o", "s_mix", "s_kpre", "s_rl",
                     "s_kh", "s_rpre", "s_g", "s_facc", "s_out", "s_store",
                     "s_vser", "s_sser", "s_yini", "s_xb3"]:
            sems[name] = ec(nc.semaphore(name))
        sm = type("SM", (), sems)
        vst = {"prev": None, "n": 0}
        sst = {"prev": None, "n": 0}

        block = ec(nc.Block())

        with ExitStack() as p1:
            e1 = p1.enter_context
            csr = e1(nc.sbuf_tensor("csr", [128, NT], F32))
            scr = e1(nc.sbuf_tensor("scr", [128, NT], F32))
            p1s = e1(nc.sbuf_tensor("p1s", [128, NT], F32))
            p2s = e1(nc.sbuf_tensor("p2s", [128, NT], F32))
            vt = e1(nc.sbuf_tensor("vt", [128, NT], F32))
            k2 = e1(nc.sbuf_tensor("k2", [128, NT], BF16))

            @block.gpsimd
            def _(g):
                for wi, (dst, srcw) in enumerate([(aq_s, aq), (ak_s, ak), (av_s, av),
                                                  (ao_s, ao), (ones_s, onesd), (id_s, identd)]):
                    if wi >= 2:
                        g.wait_ge(sm.s_w, 16 * wi)
                    g.dma_start(out=dst[:], in_=srcw).then_inc(sm.s_w, 16)
                for t in range(ST):
                    if t >= 2:
                        g.wait_ge(sm.s_st, 2 * (t - 1))
                        g.wait_ge(sm.s_tek, t - 1)
                        g.wait_ge(sm.s_tev, t - 1)
                        g.wait_ge(sm.s_sq, t - 1)
                    g.dma_start(out=xbf[:, t % 2, :, :],
                                in_=xT[:, t, :, :]).then_inc(
                                    sm.s_x0 if t % 2 == 0 else sm.s_x1, 16)
                g.nop().then_inc(sm.s_p1d, 1)

            @block.sync
            def _(s):
                s.dma_start(out=mur_s[:], in_=mur).then_inc(sm.s_misc, 16)
                s.dma_start(out=muk_s[:], in_=muk).then_inc(sm.s_misc, 16)
                s.dma_start(out=halo_s[:], in_=halo).then_inc(sm.s_misc, 16)
                for t in range(ST):
                    if t >= 1:
                        s.wait_ge(sm.s_csr, t)
                    s.dma_start(out=tab[:, 0, 0, :], in_=cs[:, t * NT:(t + 1) * NT]).then_inc(sm.s_tab, 16)
                    s.dma_start(out=tab[:, 0, 1, :], in_=sc[:, t * NT:(t + 1) * NT]).then_inc(sm.s_tab, 16)
                s.nop().then_inc(sm.s_p1d, 1)

            @block.tensor
            def _(t):
                t.wait_ge(sm.s_w, 96)
                for c in range(DC):
                    ins = t.matmul(Dp[0:1, :128], ones_s[:, 0:1], ak_s[:, c, :],
                                   start=(c == 0), stop=(c == DC - 1))
                ins.then_inc(sm.s_w1, 1)
                for c in range(DC):
                    ins = t.matmul(Dp[32:33, :128], ones_s[:, 0:1], av_s[:, c, :],
                                   start=(c == 0), stop=(c == DC - 1))
                ins.then_inc(sm.s_w1, 1)

                def stats_mms(tt):
                    t.wait_ge(sm.s_x0 if tt % 2 == 0 else sm.s_x1, 16 * (tt // 2 + 1))
                    t.wait_ge(sm.s_serows, tt)
                    if tt == 0:
                        t.wait_ge(sm.s_w1e, 1)
                    for c in range(DC):
                        ins = t.matmul(Dp[0:1, :], ones_s[:, 0:1], xbf[:, tt % 2, c, :],
                                       start=(c == 0), stop=(c == DC - 1))
                    ins.then_inc(sm.s_st, 1)
                    t.wait_ge(sm.s_sq, tt + 1)
                    for c in range(DC):
                        ins = t.matmul(Dp[32:33, :], ones_s[:, 0:1], x2[:, c, :],
                                       start=(c == 0), stop=(c == DC - 1))
                    ins.then_inc(sm.s_st, 1)

                def mid_mms(tt):
                    h0 = (tt % 2) * 512
                    t.wait_ge(sm.s_fin2, tt + 1)
                    t.wait_ge(sm.s_rbe, tt)
                    t.matmul(G[:, :], ones_s[0:1, :], rhi_sb[0:1, :], start=True, stop=False)
                    t.matmul(G[:, :], ones_s[0:1, :], rlo_sb[0:1, :], start=False, stop=True).then_inc(sm.s_rb, 1)
                    t.wait_ge(sm.s_kp, max(0, tt - 1))
                    for c in range(DC):
                        t.matmul(A[:, h0:h0 + 512], ak_s[:, c, :], xbf[:, tt % 2, c, :],
                                 start=(c == 0), stop=False)
                    t.matmul(A[:, h0:h0 + 512], w1k_s[0:1, :], mr_sb[0:1, :],
                             start=False, stop=True).then_inc(sm.s_tek, 1)
                    t.wait_ge(sm.s_vs, max(0, tt - 1))
                    for c in range(DC):
                        t.matmul(Bp[:, h0:h0 + 512], av_s[:, c, :], xbf[:, tt % 2, c, :],
                                 start=(c == 0), stop=False)
                    t.matmul(Bp[:, h0:h0 + 512], w1v_s[0:1, :], mr_sb[0:1, :],
                             start=False, stop=True).then_inc(sm.s_tev, 1)

                def tail_mms(tt):
                    t.wait_ge(sm.s_vs, tt + 1)
                    t.wait_ge(sm.s_ve2, tt)
                    for j in range(4):
                        t.transpose(Fp[:, j * 128:(j + 1) * 128],
                                    vt[:, j * 128:(j + 1) * 128],
                                    id_s[:]).then_inc(sm.s_vt, 1)
                    t.wait_ge(sm.s_k2s, tt + 1)
                    t.wait_ge(sm.s_kmax, tt)
                    t.matmul(C[0:1, :], ones_s[:, 0:1], k2[:, :],
                             start=True, stop=True).then_inc(sm.s_k2m, 1)

                for stage in range(ST + 2):
                    if stage < ST:
                        stats_mms(stage)
                    if 1 <= stage <= ST:
                        mid_mms(stage - 1)
                    if stage >= 2:
                        tail_mms(stage - 2)
                t.wait_ge(sm.s_w1, 2)
                t.wait_ge(sm.s_st, 2 * ST)
                t.wait_ge(sm.s_rb, ST)
                t.wait_ge(sm.s_tek, ST)
                t.wait_ge(sm.s_tev, ST)
                t.wait_ge(sm.s_vt, 4 * ST)
                t.wait_ge(sm.s_k2m, ST)
                t.nop().then_inc(sm.s_p1d, 1)

            @block.scalar
            def _(se0):
                se = _Ser(se0, sm.s_sser, sst)
                se.wait_ge(sm.s_init, 1)
                for stage in range(ST + 2):
                    if stage < ST:
                        tt = stage
                        se.wait_ge(sm.s_x0 if tt % 2 == 0 else sm.s_x1, 16 * (tt // 2 + 1))
                        se.wait_ge(sm.s_st, 2 * tt)
                        se.activation(out=x2.ap().rearrange("p c n -> p (c n)"),
                                      in_=xbf[:, tt % 2, :, :].rearrange("p c n -> p (c n)"),
                                      func=AF.Square, inc=(sm.s_sq, 1, tt + 1))
                        se.wait_ge(sm.s_st, 2 * tt + 2)
                        se.wait_ge(sm.s_var, tt)
                        se.wait_ge(sm.s_fin2, tt)
                        se.activation(out=m_sb[0:1, :], in_=Dp[0:1, :], func=AF.Copy,
                                      scale=1.0 / D)
                        se.activation(out=e2_sb[0:1, :], in_=Dp[32:33, :], func=AF.Copy,
                                      scale=1.0 / D, inc=(sm.s_serows, 1, tt + 1))
                        se.wait_ge(sm.s_var, tt + 1)
                        se.activation(out=var_sb[0:1, :], in_=var_sb[0:1, :], func=AF.Ln,
                                      bias=eps_s[0:1, :])
                        se.activation(out=r_sb[0:1, :], in_=var_sb[0:1, :], func=AF.Exp,
                                      scale=-0.5, inc=(sm.s_fin, 1, tt + 1))
                    if stage >= 2:
                        tt = stage - 2
                        se.wait_ge(sm.s_kp, tt + 1)
                        se.wait_ge(sm.s_k2m, tt)
                        se.activation(out=k2[:, :], in_=Ksb[:, tt * NT:(tt + 1) * NT],
                                      func=AF.Square, inc=(sm.s_k2s, 1, tt + 1))
                se.nop(inc=(sm.s_p1d, 1, None))

            @block.vector
            def _(v0):
                v = _Ser(v0, sm.s_vser, vst)
                v.memset(sm_sb[0:1, :], 0.0)
                v.memset(eps_s[0:1, :], EPS, inc=(sm.s_init, 1, 1))
                v.wait_ge(sm.s_w1, 2)
                v.tensor_scalar(out=w1k_s[0:1, :], in0=Dp[0:1, :128], scalar1=-1.0,
                                scalar2=None, op0=ALU.mult)
                v.tensor_scalar(out=w1v_s[0:1, :], in0=Dp[32:33, :128], scalar1=-1.0,
                                scalar2=None, op0=ALU.mult, inc=(sm.s_w1e, 1, 1))
                for stage in range(ST + 2):
                    if stage < ST:
                        tt = stage
                        v.wait_ge(sm.s_serows, tt + 1)
                        v.wait_ge(sm.s_fin, tt)
                        v.tensor_tensor(out=var_sb[0:1, :], in0=m_sb[0:1, :],
                                        in1=m_sb[0:1, :], op=ALU.mult)
                        v.tensor_tensor(out=var_sb[0:1, :], in0=e2_sb[0:1, :],
                                        in1=var_sb[0:1, :], op=ALU.subtract,
                                        inc=(sm.s_var, 1, tt + 1))
                        v.wait_ge(sm.s_fin, tt + 1)
                        v.wait_ge(sm.s_tek, tt)
                        v.wait_ge(sm.s_tev, tt)
                        v.wait_ge(sm.s_rb, tt)
                        v.tensor_copy(mr_sb[0:1, :], m_sb[0:1, :])
                        v.tensor_copy(rhi_sb[0:1, :], r_sb[0:1, :])
                        v.tensor_tensor(out=rlo_sb[0:1, :], in0=r_sb[0:1, :],
                                        in1=rhi_sb[0:1, :], op=ALU.subtract,
                                        inc=(sm.s_fin2, 1, tt + 1))
                    if 1 <= stage <= ST:
                        tt = stage - 1
                        h0 = (tt % 2) * 512
                        v.wait_ge(sm.s_rb, tt + 1)
                        v.tensor_copy(rb_sb[:, :], G[:, :], inc=(sm.s_rbe, 1, tt + 1))
                        v.wait_ge(sm.s_tab, 32 * (tt + 1))
                        v.tensor_tensor(out=csr[:, :], in0=tab[:, 0, 0, :],
                                        in1=rb_sb[:, :], op=ALU.mult)
                        v.tensor_tensor(out=scr[:, :], in0=tab[:, 0, 1, :],
                                        in1=rb_sb[:, :], op=ALU.mult,
                                        inc=(sm.s_csr, 1, tt + 1))
                        v.wait_ge(sm.s_tek, tt + 1)
                        v.tensor_tensor(out=p1s[0:64, :], in0=csr[0:64, :], in1=A[0:64, h0:h0 + 512], op=ALU.mult)
                        v.tensor_tensor(out=p2s[0:64, :], in0=scr[0:64, :], in1=A[64:128, h0:h0 + 512], op=ALU.mult)
                        v.tensor_tensor(out=Ksb[0:64, tt * NT:(tt + 1) * NT],
                                        in0=p1s[0:64, :], in1=p2s[0:64, :], op=ALU.subtract)
                        v.tensor_tensor(out=p1s[64:128, :], in0=csr[64:128, :], in1=A[0:64, h0:h0 + 512], op=ALU.mult)
                        v.tensor_tensor(out=p2s[64:128, :], in0=scr[64:128, :], in1=A[64:128, h0:h0 + 512], op=ALU.mult)
                        v.tensor_tensor(out=Ksb[64:128, tt * NT:(tt + 1) * NT],
                                        in0=p1s[64:128, :], in1=p2s[64:128, :], op=ALU.add,
                                        inc=(sm.s_kp, 1, tt + 1))
                        v.wait_ge(sm.s_tev, tt + 1)
                        v.wait_ge(sm.s_vt, 4 * tt)
                        v.tensor_tensor(out=vt[:, :], in0=Bp[:, h0:h0 + 512], in1=rb_sb[:, :],
                                        op=ALU.mult, inc=(sm.s_vs, 1, tt + 1))
                    if stage >= 2:
                        tt = stage - 2
                        v.wait_ge(sm.s_vt, 4 * (tt + 1))
                        v.tensor_copy(Vsb[:, tt * NT:(tt + 1) * NT], Fp[:, :],
                                      inc=(sm.s_ve2, 1, tt + 1))
                        v.wait_ge(sm.s_k2m, tt + 1)
                        v.reduce_max(sm_sb[0:1, 1:2], C[0:1, :], axis=AX.X)
                        v.tensor_tensor(out=sm_sb[0:1, 0:1], in0=sm_sb[0:1, 0:1],
                                        in1=sm_sb[0:1, 1:2], op=ALU.max,
                                        inc=(sm.s_kmax, 1, tt + 1))
                v.nop(inc=(sm.s_p1d, 1, None))

        with ExitStack() as p2:
            e2 = p2.enter_context
            fbuf = [e2(nc.sbuf_tensor(f"fbuf{i}", [128, DC, DC, 128], BF16)) for i in range(2)]
            xn = e2(nc.sbuf_tensor("xn", [128, 2, DC, NT + 1], BF16))
            q_sb = e2(nc.sbuf_tensor("q_sb", [128, NT], BF16))
            q2_sb = e2(nc.sbuf_tensor("q2_sb", [128, NT], BF16))
            Pr = e2(nc.sbuf_tensor("Pr", [128, 2, 1024], BF16))
            o_sb = e2(nc.sbuf_tensor("o_sb", [128, NT], BF16))
            xr_sb = e2(nc.sbuf_tensor("xr_sb", [128, DC, NT], BF16))
            xk_sb = e2(nc.sbuf_tensor("xk_sb", [128, DC, NT], BF16))
            kh_sb = e2(nc.sbuf_tensor("kh_sb", [128, DC, NT], BF16))
            d_sb = e2(nc.sbuf_tensor("d_sb", [128, NT], BF16))
            tt_sb = e2(nc.sbuf_tensor("tt_sb", [128, 1, NT], BF16))
            t1_sb = e2(nc.sbuf_tensor("t1_sb", [128, NT], F32))
            outs = e2(nc.sbuf_tensor("outs", [128, 1, NT], F32))
            g_sb = e2(nc.sbuf_tensor("g_sb", [128, DC, NT], BF16))
            p1q, p2q = t1_sb, rb_sb

            NL = QT + 1

            @block.gpsimd
            def _(g):
                def xload(e):
                    if e >= 2:
                        g.wait_ge(sm.s_st2, 2 * (e - 1))
                        g.wait_ge(sm.s_sq2, e - 1)
                        g.wait_ge(sm.s_xn, e - 1)
                    g.dma_start(out=xbf[:, e % 2, :, :],
                                in_=xT[:, e, :, :]).then_inc(
                                    sm.s_xb2 if e % 2 == 0 else sm.s_xb3, 16)

                g.wait_ge(sm.s_p1d, 5)
                xload(0)
                xload(1)
                xload(2)
                for ell in range(1, NL):
                    if ell + 2 < NL:
                        xload(ell + 2)
                    if ell >= 1:
                        u = ell - 1
                        for i, (w, fsem) in enumerate([(fk, sm.s_fwk), (fr, sm.s_fwr),
                                                       (fv, sm.s_fwv)]):
                            li = 3 * u + i
                            buf = fbuf[li % 2]
                            if i == 0 and u >= 1:
                                g.wait_ge(sm.s_rpre, 8 * u)
                            if i == 1 and u >= 1:
                                g.wait_ge(sm.s_facc, 16 * u)
                            if i == 1:
                                g.wait_ge(sm.s_fwk, 16 * (u + 1))
                            if i == 2:
                                g.wait_ge(sm.s_kpre, 8 * u + 8)
                                g.wait_ge(sm.s_fwr, 16 * (u + 1))
                            g.dma_start(out=buf[:], in_=w).then_inc(fsem, 16)
                    if ell >= 1:
                        u = ell - 1
                        if ell == 1:
                            g.wait_ge(sm.s_yini, 16)
                        for dc in range(DC):
                            g.wait_ge(sm.s_out, 8 * u + dc + 1)
                            if 8 * u + dc >= 1:
                                g.wait_ge(sm.s_store, 16 * (8 * u + dc))
                            g.dma_start(out=y[:, u, dc, :], in_=outs[:, 0, :],
                                        accum_op=ALU.add).then_inc(sm.s_store, 16)
                g.wait_ge(sm.s_store, 16 * 8 * QT)

            @block.sync
            def _(s):
                s.wait_ge(sm.s_p1d, 5)
                s.dma_start(out=y[:, :, :, :], in_=xT[:, 1:QT + 1, :, :]).then_inc(sm.s_yini, 16)
                for ell in range(NL):
                    u = ell - 1
                    if ell >= 1:
                        s.wait_ge(sm.s_qr, u)
                        s.dma_start(out=tab[:, 0, 0, :], in_=csq[:, u * NT:(u + 1) * NT]).then_inc(sm.s_tabq, 16)
                        s.dma_start(out=tab[:, 0, 1, :], in_=scq[:, u * NT:(u + 1) * NT]).then_inc(sm.s_tabq, 16)

            @block.tensor
            def _(t):
                t.wait_ge(sm.s_p1d, 5)

                def stats_g(e):
                    t.wait_ge(sm.s_xb2 if e % 2 == 0 else sm.s_xb3, 16 * (e // 2 + 1))
                    t.wait_ge(sm.s_srow2, e)
                    for c in range(DC):
                        ins = t.matmul(Dp[32:33, :], ones_s[:, 0:1], xbf[:, e % 2, c, :],
                                       start=(c == 0), stop=(c == DC - 1))
                    ins.then_inc(sm.s_st2, 1)
                    t.wait_ge(sm.s_sq2, e + 1)
                    for c in range(DC):
                        ins = t.matmul(Dp[64:65, :], ones_s[:, 0:1], x2[:, c, :],
                                       start=(c == 0), stop=(c == DC - 1))
                    ins.then_inc(sm.s_st2, 1)

                def bcast_g(e):
                    t.wait_ge(sm.s_rows2, e + 1)
                    t.wait_ge(sm.s_xn, max(0, e - 1))
                    t.wait_ge(sm.s_o, max(0, e - 2))
                    t.wait_ge(sm.s_bias, max(0, e - 1))
                    t.wait_ge(sm.s_out, 8 * max(0, e - 2))
                    t.matmul(G[:, :], ones_s[0:1, :], rhi_sb[0:1, :], start=True, stop=False)
                    t.matmul(G[:, :], ones_s[0:1, :], rlo_sb[0:1, :],
                             start=False, stop=True).then_inc(sm.s_rbmm2, 1)
                    t.matmul(Fp[:, :], ones_s[0:1, :], mr_sb[0:1, :],
                             start=True, stop=True).then_inc(sm.s_rbmm2, 1)

                def qproj_g(uu):
                    t.wait_ge(sm.s_xn, uu + 2)
                    t.wait_ge(sm.s_o, uu)
                    for c in range(DC):
                        ins = t.matmul(C[:, :], aq_s[:, c, :], xn[:, (uu + 1) % 2, c, 1:NT + 1],
                                       start=(c == 0), stop=(c == DC - 1))
                    ins.then_inc(sm.s_q, 1)

                def qnorm_g(uu):
                    t.wait_ge(sm.s_q2, uu + 1)
                    t.wait_ge(sm.s_qmaxd, uu)
                    t.wait_ge(sm.s_den, uu)
                    t.matmul(Dp[0:1, :], ones_s[:, 0:1], q2_sb[:, :],
                             start=True, stop=True).then_inc(sm.s_qn, 1)
                    t.wait_ge(sm.s_qb, uu + 1)
                    t.wait_ge(sm.s_xn, min(uu + 2, NL))
                    t.matmul(G[:, 0:1], ones_s[0:1, :], bbf_sb[0:1, :],
                             start=True, stop=True).then_inc(sm.s_bmm, 1)

                def flash(u, ell):
                    t.wait_ge(sm.s_qr, u + 1)
                    t.wait_ge(sm.s_bias, u + 1)
                    t.wait_ge(sm.s_den, u)
                    if u >= 1:
                        t.wait_ge(sm.s_g, 8 * u)
                        t.wait_ge(sm.s_rl, 8 * u)
                        t.wait_ge(sm.s_out, 8 * u)

                    def den_av(gj):
                        t.wait_ge(sm.s_exp, 64 * u + gj + 1)
                        for h in range(2):
                            kc = 2 * gj + h
                            t.matmul(Dp[0:1, :], ones_s[:, 0:1], Pr[:, gj % 2, h * 512:(h + 1) * 512],
                                     start=(kc == 0), stop=(kc == KC - 1)).then_inc(sm.s_av, 1)
                            t.matmul(C[:, :], Vsb[:, kc * 128:(kc + 1) * 128],
                                     Pr[:, gj % 2, h * 512:(h + 1) * 512],
                                     start=(kc == 0), stop=(kc == KC - 1)).then_inc(sm.s_av, 1)

                    for gi in range(NG):
                        pg = A if gi % 2 == 0 else Bp
                        t.wait_ge(sm.s_exp, max(0, 64 * u + gi - 1))
                        for h in range(2):
                            kc = 2 * gi + h
                            t.matmul(pg[:, h * 512:(h + 1) * 512],
                                     Ksb[:, kc * 128:(kc + 1) * 128], q_sb[:, :],
                                     start=True, stop=True).then_inc(sm.s_qk, 1)
                        if gi == 2 and ell + 1 < NL:
                            stats_g(ell + 1)
                        if gi == 12 and ell + 1 < NL:
                            bcast_g(ell + 1)
                        if gi >= 1:
                            den_av(gi - 1)
                    den_av(NG - 1)

                stats_g(0)
                bcast_g(0)
                stats_g(1)
                bcast_g(1)
                qproj_g(0)
                qnorm_g(0)

                for ell in range(1, NL):
                    u = ell - 1
                    flash(u, ell)
                    t.wait_ge(sm.s_den, u + 1)
                    t.wait_ge(sm.s_xn, min(ell + 2, NL))
                    t.matmul(G[:, :], ones_s[0:1, :], dhi_sb[0:1, :], start=True, stop=False)
                    t.matmul(G[:, :], ones_s[0:1, :], dlo_sb[0:1, :],
                             start=False, stop=True).then_inc(sm.s_dbm, 1)
                    if u + 1 <= QT - 1:
                        qproj_g(u + 1)
                    t.wait_ge(sm.s_fwk, 16 * (u + 1))
                    for dc in range(DC):
                        pf = (A if dc % 2 == 0 else Bp)
                        t.wait_ge(sm.s_mix, 8 * u + dc + 1)
                        t.wait_ge(sm.s_rl, max(0, 8 * u + dc - 1))
                        for c in range(DC):
                            ins = t.matmul(pf[:, 0:512], fbuf[(3 * u) % 2][:, c, dc, :],
                                           xk_sb[:, c, :], start=(c == 0), stop=(c == DC - 1))
                        ins.then_inc(sm.s_kpre, 1)
                    t.wait_ge(sm.s_fwr, 16 * (u + 1))
                    for dc in range(DC):
                        pf = (A if dc % 2 == 0 else Bp)
                        t.wait_ge(sm.s_g, max(0, 8 * u + dc - 1))
                        if dc in (0, 1):
                            t.wait_ge(sm.s_rl, 8 * u + 7 + dc)
                        for c in range(DC):
                            ins = t.matmul(pf[:, 0:512], fbuf[(3 * u + 1) % 2][:, c, dc, :],
                                           xr_sb[:, c, :], start=(c == 0), stop=(c == DC - 1))
                        ins.then_inc(sm.s_rpre, 1)
                    if u + 1 <= QT - 1:
                        qnorm_g(u + 1)
                    t.wait_ge(sm.s_fwv, 16 * (u + 1))
                    t.wait_ge(sm.s_o, u + 1)
                    t.wait_ge(sm.s_kh, 8 * u + 8)
                    t.wait_ge(sm.s_qr, min(u + 2, QT))
                    t.wait_ge(sm.s_bias, min(u + 2, QT))
                    for dc in range(DC):
                        pacc = (A if dc % 2 == 0 else Bp)
                        pff = (C if dc % 2 == 0 else G)
                        t.wait_ge(sm.s_out, max(0, 8 * u + dc - 1))
                        if dc in (0, 1):
                            t.wait_ge(sm.s_g, 8 * u + 7 + dc)
                        t.matmul(pacc[:, 512:1024], ao_s[:, dc * 128:(dc + 1) * 128],
                                 o_sb[:, :], start=True, stop=True).then_inc(sm.s_facc, 1)
                        for c in range(DC):
                            ins = t.matmul(pff[:, 0:512], fbuf[(3 * u + 2) % 2][:, c, dc, :],
                                           kh_sb[:, c, :], start=(c == 0), stop=(c == DC - 1))
                        ins.then_inc(sm.s_facc, 1)

            @block.scalar
            def _(se0):
                se = _Ser(se0, sm.s_sser, sst)
                se.wait_ge(sm.s_p1d, 5)

                def sq_g(e):
                    se.wait_ge(sm.s_xb2 if e % 2 == 0 else sm.s_xb3, 16 * (e // 2 + 1))
                    se.wait_ge(sm.s_st2, 2 * e)
                    se.activation(out=x2.ap().rearrange("p c n -> p (c n)"),
                                  in_=xbf[:, e % 2, :, :].rearrange("p c n -> p (c n)"),
                                  func=AF.Square, inc=(sm.s_sq2, 1, e + 1))

                def rows_g_a(e):
                    se.wait_ge(sm.s_st2, 2 * (e + 1))
                    se.wait_ge(sm.s_var2, e)
                    se.wait_ge(sm.s_rows2, e)
                    se.activation(out=m_sb[0:1, :], in_=Dp[32:33, :], func=AF.Copy, scale=1.0 / D)
                    se.activation(out=e2_sb[0:1, :], in_=Dp[64:65, :], func=AF.Copy,
                                  scale=1.0 / D, inc=(sm.s_srow2, 1, e + 1))

                def rows_g_b(e):
                    se.wait_ge(sm.s_var2, e + 1)
                    se.wait_ge(sm.s_den, max(0, e - 2))
                    se.activation(out=var_sb[0:1, :], in_=var_sb[0:1, :], func=AF.Ln,
                                  bias=eps_s[0:1, :])
                    se.activation(out=r_sb[0:1, :], in_=var_sb[0:1, :], func=AF.Exp,
                                  scale=-0.5, inc=(sm.s_fin_2, 1, e + 1))

                def rows_g(e):
                    rows_g_a(e)
                    rows_g_b(e)

                def q2_g(uu):
                    se.wait_ge(sm.s_qr, uu + 1)
                    se.wait_ge(sm.s_qn, uu)
                    se.activation(out=q2_sb[:, :], in_=q_sb[:, :], func=AF.Square,
                                  inc=(sm.s_q2, 1, uu + 1))

                sq_g(0)
                sq_g(1)
                rows_g(0)
                rows_g(1)
                sq_g(2)
                q2_g(0)

                for ell in range(1, NL):
                    u = ell - 1
                    se.wait_ge(sm.s_bias, u + 1)
                    for gi in range(NG):
                        pg = A if gi % 2 == 0 else Bp
                        se.wait_ge(sm.s_qk, 128 * u + 2 * gi + 2)
                        se.wait_ge(sm.s_av, max(0, 4 * (64 * u + gi - 1)))
                        se.activation(out=Pr[:, gi % 2, :], in_=pg[:, :], func=AF.Exp,
                                      scale=RSQ, bias=bias_sb[:, :],
                                      inc=(sm.s_exp, 1, 64 * u + gi + 1))
                        if gi == 5 and ell + 1 < NL:
                            rows_g_a(ell + 1)
                        if gi == 9 and ell + 1 < NL:
                            rows_g_b(ell + 1)
                        if gi == 20 and ell + 2 < NL:
                            sq_g(ell + 2)
                    for dc in range(DC):
                        pf = (A if dc % 2 == 0 else Bp)
                        se.wait_ge(sm.s_kpre, 8 * u + dc + 1)
                        se.wait_ge(sm.s_kh, 8 * u + dc)
                        se.activation(out=tt_sb[:, 0, :], in_=pf[:, 0:512], func=AF.Relu,
                                      inc=(sm.s_rl, 1, 8 * u + dc + 1))
                    se.wait_ge(sm.s_out, 8 * u)
                    for dc in range(DC):
                        pf = (A if dc % 2 == 0 else Bp)
                        se.wait_ge(sm.s_rpre, 8 * u + dc + 1)
                        se.wait_ge(sm.s_kh, 8 * u + 8)
                        se.activation(out=tt_sb[:, 0, :], in_=pf[:, 0:512], func=AF.Tanh, scale=0.5)
                        se.activation(out=g_sb[:, dc, :], in_=tt_sb[:, 0, :], func=AF.Copy,
                                      scale=0.5, bias=0.5, inc=(sm.s_g, 1, 8 * u + dc + 1))
                    if u + 1 <= QT - 1:
                        q2_g(u + 1)

            @block.vector
            def _(v0):
                v = _Ser(v0, sm.s_vser, vst)
                v.wait_ge(sm.s_p1d, 5)

                def rows_v(e):
                    v.wait_ge(sm.s_srow2, e + 1)
                    v.wait_ge(sm.s_fin_2, e)
                    v.tensor_tensor(out=var_sb[0:1, :], in0=m_sb[0:1, :], in1=m_sb[0:1, :], op=ALU.mult)
                    v.tensor_tensor(out=var_sb[0:1, :], in0=e2_sb[0:1, :], in1=var_sb[0:1, :],
                                    op=ALU.subtract, inc=(sm.s_var2, 1, e + 1))
                    v.wait_ge(sm.s_fin_2, e + 1)
                    v.wait_ge(sm.s_rbmm2, 2 * e)
                    v.tensor_tensor(out=mr_sb[0:1, :], in0=m_sb[0:1, :], in1=r_sb[0:1, :], op=ALU.mult)
                    v.tensor_copy(rhi_sb[0:1, :], r_sb[0:1, :])
                    v.tensor_tensor(out=rlo_sb[0:1, :], in0=r_sb[0:1, :], in1=rhi_sb[0:1, :],
                                    op=ALU.subtract, inc=(sm.s_rows2, 1, e + 1))

                def xn_v(e):
                    v.wait_ge(sm.s_rbmm2, 2 * (e + 1))
                    v.wait_ge(sm.s_xb2 if e % 2 == 0 else sm.s_xb3, 16 * (e // 2 + 1))
                    if e >= 2:
                        v.wait_ge(sm.s_mix, 8 * (e - 1))
                    v.tensor_copy(rb_sb[:, :], G[:, :])
                    for dc in range(DC):
                        v.tensor_tensor(out=t1_sb[:, :], in0=xbf[:, e % 2, dc, :],
                                        in1=rb_sb[:, :], op=ALU.mult)
                        v.tensor_tensor(out=xn[:, e % 2, dc, 1:NT + 1], in0=t1_sb[:, :],
                                        in1=Fp[:, :], op=ALU.subtract)
                        if e >= 1:
                            v.tensor_copy(xn[:, e % 2, dc, 0:1], xn[:, (e - 1) % 2, dc, NT:NT + 1])
                        else:
                            v.memset(xn[:, 0, dc, 0:1], 0.0)
                    if e == 0:
                        v.wait_ge(sm.s_misc, 48)
                        for dc in range(DC):
                            v.tensor_scalar(out=xn[:, 0, dc, NT:NT + 1], in0=xn[:, 0, dc, NT:NT + 1],
                                            scalar1=halo_s[:, :], scalar2=None, op0=ALU.mult)
                    v.nop(inc=(sm.s_xn, 1, e + 1))

                def rope_v(uu):
                    v.wait_ge(sm.s_q, uu + 1)
                    v.wait_ge(sm.s_tabq, 32 * (uu + 1))
                    v.tensor_tensor(out=p1q[0:64, :], in0=tab[0:64, 0, 0, :], in1=C[0:64, :], op=ALU.mult)
                    v.tensor_tensor(out=p2q[0:64, :], in0=tab[0:64, 0, 1, :], in1=C[64:128, :], op=ALU.mult)
                    v.tensor_tensor(out=q_sb[0:64, :], in0=p1q[0:64, :], in1=p2q[0:64, :], op=ALU.subtract)
                    v.tensor_tensor(out=p1q[64:128, :], in0=tab[64:128, 0, 0, :], in1=C[0:64, :], op=ALU.mult)
                    v.tensor_tensor(out=p2q[64:128, :], in0=tab[64:128, 0, 1, :], in1=C[64:128, :], op=ALU.mult)
                    v.tensor_tensor(out=q_sb[64:128, :], in0=p1q[64:128, :], in1=p2q[64:128, :],
                                    op=ALU.add, inc=(sm.s_qr, 1, uu + 1))

                def mix_v(uu):
                    e = uu + 1
                    if uu >= 1:
                        v.wait_ge(sm.s_rpre, 8 * uu)
                        v.wait_ge(sm.s_kpre, 8 * uu)
                    for dc in range(DC):
                        v.tensor_tensor(out=d_sb[:, :], in0=xn[:, e % 2, dc, 1:NT + 1],
                                        in1=xn[:, e % 2, dc, 0:NT], op=ALU.subtract)
                        v.scalar_tensor_tensor(out=xr_sb[:, dc, :], in0=d_sb[:, :],
                                               scalar=mur_s[:, dc:dc + 1], in1=xn[:, e % 2, dc, 0:NT],
                                               op0=ALU.mult, op1=ALU.add)
                        v.scalar_tensor_tensor(out=xk_sb[:, dc, :], in0=d_sb[:, :],
                                               scalar=muk_s[:, dc:dc + 1], in1=xn[:, e % 2, dc, 0:NT],
                                               op0=ALU.mult, op1=ALU.add,
                                               inc=(sm.s_mix, 1, 8 * uu + dc + 1))

                def bias_v(uu):
                    v.wait_ge(sm.s_qn, uu + 1)
                    v.reduce_max(sm_sb[0:1, 1:2], Dp[0:1, :], axis=AX.X,
                                 inc=(sm.s_qmaxd, 1, uu + 1))
                    v.tensor_tensor(out=sm_sb[0:1, 2:3], in0=sm_sb[0:1, 1:2], in1=sm_sb[0:1, 0:1], op=ALU.add)
                    v.tensor_scalar(out=sm_sb[0:1, 3:4], in0=sm_sb[0:1, 2:3],
                                    scalar1=-0.5 * RSQ, scalar2=None, op0=ALU.mult)
                    v.tensor_copy(bbf_sb[0:1, :], sm_sb[0:1, 3:4], inc=(sm.s_qb, 1, uu + 1))
                    v.wait_ge(sm.s_bmm, uu + 1)
                    v.tensor_copy(bias_sb[:, :], G[:, 0:1], inc=(sm.s_bias, 1, uu + 1))

                rows_v(0)
                xn_v(0)
                rows_v(1)
                xn_v(1)
                rope_v(0)
                mix_v(0)
                bias_v(0)

                for ell in range(1, NL):
                    u = ell - 1
                    if ell + 1 < NL:
                        rows_v(ell + 1)
                        xn_v(ell + 1)
                    v.wait_ge(sm.s_av, 256 * (u + 1))
                    v.reciprocal(rden_sb[0:1, :], Dp[0:1, :])
                    v.tensor_copy(dhi_sb[0:1, :], rden_sb[0:1, :])
                    v.tensor_tensor(out=dlo_sb[0:1, :], in0=rden_sb[0:1, :], in1=dhi_sb[0:1, :],
                                    op=ALU.subtract, inc=(sm.s_den, 1, u + 1))
                    v.wait_ge(sm.s_dbm, u + 1)
                    v.tensor_copy(rb_sb[:, :], G[:, :])
                    v.tensor_tensor(out=o_sb[:, :], in0=C[:, :], in1=rb_sb[:, :],
                                    op=ALU.mult, inc=(sm.s_o, 1, u + 1))
                    if u + 1 <= QT - 1:
                        rope_v(u + 1)
                    if u >= 1:
                        v.wait_ge(sm.s_facc, 16 * u)
                    for dc in range(DC):
                        v.wait_ge(sm.s_rl, 8 * u + dc + 1)
                        v.tensor_tensor(out=kh_sb[:, dc, :], in0=tt_sb[:, 0, :],
                                        in1=tt_sb[:, 0, :],
                                        op=ALU.mult, inc=(sm.s_kh, 1, 8 * u + dc + 1))
                    if u + 1 <= QT - 1:
                        bias_v(u + 1)
                        mix_v(u + 1)
                    for dc in range(DC):
                        pacc = (A if dc % 2 == 0 else Bp)
                        pff = (C if dc % 2 == 0 else G)
                        v.wait_ge(sm.s_facc, 16 * u + 2 * dc + 2)
                        v.wait_ge(sm.s_g, 8 * u + dc + 1)
                        if 8 * u + dc >= 1:
                            v.wait_ge(sm.s_store, 16 * (8 * u + dc))
                        v.tensor_tensor(out=t1_sb[:, :], in0=g_sb[:, dc, :], in1=pff[:, 0:512], op=ALU.mult)
                        v.tensor_tensor(out=outs[:, 0, :], in0=t1_sb[:, :], in1=pacc[:, 512:1024],
                                        op=ALU.add, inc=(sm.s_out, 1, 8 * u + dc + 1))
    return nc


_NC_CACHE = None


def kernel(L0_attn_x, L0_attn_wq, L0_attn_wk, L0_attn_wv, L0_attn_wo,
           L0_ffn_W_r, L0_ffn_W_k, L0_ffn_W_v, L0_ffn_mu_r, L0_ffn_mu_k):
    global _NC_CACHE
    in_maps = make_in_maps(L0_attn_x, L0_attn_wq, L0_attn_wk, L0_attn_wv,
                           L0_attn_wo, L0_ffn_W_r, L0_ffn_W_k, L0_ffn_W_v,
                           L0_ffn_mu_r, L0_ffn_mu_k)
    if _NC_CACHE is None:
        _NC_CACHE = build_nc()
    res = run_bass_kernel_spmd(_NC_CACHE, in_maps, core_ids=list(range(8)))

    out = np.empty((B, S, D), dtype=np.float32)
    for core in range(8):
        b, qi = core // 4, core % 4
        yv = res.results[core]["y"]
        out[b, qi * Q:(qi + 1) * Q, :] = yv.transpose(1, 3, 2, 0).reshape(Q, D)
    return out


def make_in_maps(L0_attn_x, L0_attn_wq, L0_attn_wk, L0_attn_wv, L0_attn_wo,
                 L0_ffn_W_r, L0_ffn_W_k, L0_ffn_W_v, L0_ffn_mu_r, L0_ffn_mu_k):
    x = np.asarray(L0_attn_x, dtype=np.float32)
    wq = np.asarray(L0_attn_wq, dtype=np.float32)
    wk = np.asarray(L0_attn_wk, dtype=np.float32)
    wv = np.asarray(L0_attn_wv, dtype=np.float32)
    wo = np.asarray(L0_attn_wo, dtype=np.float32)
    Wr = np.asarray(L0_ffn_W_r, dtype=np.float32)
    Wk = np.asarray(L0_ffn_W_k, dtype=np.float32)
    Wv = np.asarray(L0_ffn_W_v, dtype=np.float32)
    mu_r = np.asarray(L0_ffn_mu_r, dtype=np.float32)
    mu_k = np.asarray(L0_ffn_mu_k, dtype=np.float32)

    try:
        import jax
        import jax.numpy as jnp
        with jax.default_device(jax.devices("cpu")[0]):
            inv_j = 1.0 / 10000.0 ** (jnp.arange(0, HD, 2, dtype=jnp.float32) / HD)
            t_j = jnp.arange(S - 1, dtype=jnp.float32)
            ang_j = t_j[:, None] * inv_j[None, :]
            cosr = np.asarray(jnp.cos(ang_j))
            sinr = np.asarray(jnp.sin(ang_j))
    except Exception:
        inv32 = (1.0 / (np.float32(10000.0) **
                        (np.arange(0, HD, 2, dtype=np.float32) / np.float32(HD)))
                 ).astype(np.float32)
        t32 = np.arange(S - 1, dtype=np.float32)
        ang32 = (t32[:, None] * inv32[None, :]).astype(np.float32)
        cosr = np.cos(ang32).astype(np.float32)
        sinr = np.sin(ang32).astype(np.float32)
    cosf = np.empty((64, S), dtype=np.float32)
    sinf = np.empty((64, S), dtype=np.float32)
    cosf[:, 1:] = cosr.T
    sinf[:, 1:] = sinr.T
    cosf[:, 0] = 1.0
    sinf[:, 0] = 0.0
    CS = np.concatenate([cosf, sinf], axis=0)
    SC = np.concatenate([sinf, cosf], axis=0)

    perm_h = np.concatenate([np.arange(0, HD, 2), np.arange(1, HD, 2)])

    def _wT_pch(w):
        return np.ascontiguousarray(w.T.reshape(DC, 128, HD).transpose(1, 0, 2))

    def _fT_pcdk(w):
        return np.ascontiguousarray(w.T.reshape(DC, 128, DC, 128).transpose(1, 0, 2, 3))

    aq_h = _wT_pch(wq[perm_h])
    ak_h = _wT_pch(wk[perm_h])
    av_h = _wT_pch(wv)
    ao_h = np.ascontiguousarray(wo.T)
    fr_h = _fT_pcdk(Wr)
    fk_h = _fT_pcdk(Wk)
    fv_h = _fT_pcdk(Wv)
    mur_h = np.ascontiguousarray(mu_r.reshape(DC, 128).T)
    muk_h = np.ascontiguousarray(mu_k.reshape(DC, 128).T)
    ones_h = np.ones((128, 128), dtype=np.float32)
    ident_h = np.eye(128, dtype=np.float32)

    in_maps = []
    for core in range(8):
        b, qi = core // 4, core % 4
        q0 = qi * Q
        if qi == 0:
            perm = np.concatenate([np.arange(S - NT, S), np.arange(0, Q), np.arange(Q, S - NT)])
            halo_h = np.zeros((128, 1), dtype=np.float32)
        else:
            perm = np.concatenate([np.arange(q0 - NT, q0 + Q), np.arange(0, q0 - NT),
                                   np.arange(q0 + Q, S)])
            halo_h = np.ones((128, 1), dtype=np.float32)
        xT_h = np.ascontiguousarray(
            x[b].T[:, perm].reshape(DC, 128, ST, NT).transpose(1, 2, 0, 3))
        in_maps.append({
            "xT": xT_h,
            "cs": np.ascontiguousarray(CS[:, perm]),
            "sc": np.ascontiguousarray(SC[:, perm]),
            "csq": np.ascontiguousarray(CS[:, q0:q0 + Q]),
            "scq": np.ascontiguousarray(SC[:, q0:q0 + Q]),
            "aq": aq_h, "ak": ak_h, "av": av_h, "ao": ao_h,
            "fr": fr_h, "fk": fk_h, "fv": fv_h,
            "mur": mur_h, "muk": muk_h,
            "onesd": ones_h, "identd": ident_h, "halo": halo_h,
        })
    return in_maps


# version 11
# speedup vs baseline: 1.0601x; 1.0600x over previous
import math
import numpy as np
import ml_dtypes
import concourse.bass as bass
import concourse.mybir as mybir
from concourse.bass_utils import run_bass_kernel_spmd

F32 = mybir.dt.float32
BF16 = mybir.dt.bfloat16
AF = mybir.ActivationFunctionType
ALU = mybir.AluOpType
AX = mybir.AxisListType

B = 2
S = 16384
D = 1024
HD = 128
NT = 512
ST = S // NT
Q = 4096
QT = Q // NT
DC = D // 128
KC = S // 128
NG = KC // 2
RSQ = 1.0 / math.sqrt(128.0)
EPS = 1e-5


class _Ser:

    def __init__(self, eng, sem, st):
        self._e = eng
        self._sem = sem
        self._st = st

    def wait_ge(self, sem, v):
        self._e.wait_ge(sem, v)

    def __getattr__(self, name):
        f = getattr(self._e, name)

        def g(*a, inc=None, edge=True, **kw):
            p = self._st["prev"]
            if p is not None:
                self._e.wait_ge(p[0], p[1])
            ins = f(*a, **kw)
            if inc is None:
                self._st["n"] += 1
                ins.then_inc(self._sem, 1)
                self._st["prev"] = (self._sem, self._st["n"])
            else:
                sem, k, after = inc
                ins.then_inc(sem, k)
                if edge and after is not None:
                    self._st["prev"] = (sem, after)
            return ins
        return g


def build_nc():
    nc = bass.Bass("TRN2", target_bir_lowering=False, dynamic_dma_scratch_size=16384)

    xT = nc.dram_tensor("xT", [128, ST, DC, NT], F32, kind="ExternalInput").ap()
    cs = nc.dram_tensor("cs", [128, S], F32, kind="ExternalInput").ap()
    sc = nc.dram_tensor("sc", [128, S], F32, kind="ExternalInput").ap()
    csq = nc.dram_tensor("csq", [128, Q], F32, kind="ExternalInput").ap()
    scq = nc.dram_tensor("scq", [128, Q], F32, kind="ExternalInput").ap()
    aq = nc.dram_tensor("aq", [128, DC, HD], F32, kind="ExternalInput").ap()
    ak = nc.dram_tensor("ak", [128, DC, HD], F32, kind="ExternalInput").ap()
    av = nc.dram_tensor("av", [128, DC, HD], F32, kind="ExternalInput").ap()
    ao = nc.dram_tensor("ao", [HD, D], F32, kind="ExternalInput").ap()
    fr = nc.dram_tensor("fr", [128, DC, DC, 128], F32, kind="ExternalInput").ap()
    fk = nc.dram_tensor("fk", [128, DC, DC, 128], F32, kind="ExternalInput").ap()
    fv = nc.dram_tensor("fv", [128, DC, DC, 128], F32, kind="ExternalInput").ap()
    mur = nc.dram_tensor("mur", [128, DC], F32, kind="ExternalInput").ap()
    muk = nc.dram_tensor("muk", [128, DC], F32, kind="ExternalInput").ap()
    onesd = nc.dram_tensor("onesd", [128, 128], F32, kind="ExternalInput").ap()
    identd = nc.dram_tensor("identd", [128, 128], F32, kind="ExternalInput").ap()
    halo = nc.dram_tensor("halo", [128, 1], F32, kind="ExternalInput").ap()
    y = nc.dram_tensor("y", [128, QT, DC, NT], F32, kind="ExternalOutput").ap()

    from contextlib import ExitStack
    with ExitStack() as top:
        ec = top.enter_context
        Ksb = ec(nc.sbuf_tensor("Ksb", [128, S], BF16))
        Vsb = ec(nc.sbuf_tensor("Vsb", [128, KC, 129], BF16))
        aq_s = ec(nc.sbuf_tensor("aq_s", [128, DC, 128], BF16))
        ak_s = ec(nc.sbuf_tensor("ak_s", [128, DC, 128], BF16))
        av_s = ec(nc.sbuf_tensor("av_s", [128, DC, 128], BF16))
        ao_s = ec(nc.sbuf_tensor("ao_s", [128, D], BF16))
        mur_s = ec(nc.sbuf_tensor("mur_s", [128, DC], F32))
        muk_s = ec(nc.sbuf_tensor("muk_s", [128, DC], F32))
        ones_s = ec(nc.sbuf_tensor("ones_s", [128, 128], BF16))
        id_s = ec(nc.sbuf_tensor("id_s", [128, 128], F32))
        halo_s = ec(nc.sbuf_tensor("halo_s", [128, 1], F32))
        w1k_s = ec(nc.sbuf_tensor("w1k_s", [1, 128], BF16))
        w1v_s = ec(nc.sbuf_tensor("w1v_s", [1, 128], BF16))
        eps_s = ec(nc.sbuf_tensor("eps_s", [1, 1], F32))
        m_sb = ec(nc.sbuf_tensor("m_sb", [1, NT], F32))
        e2_sb = ec(nc.sbuf_tensor("e2_sb", [1, NT], F32))
        var_sb = ec(nc.sbuf_tensor("var_sb", [1, NT], F32))
        r_sb = ec(nc.sbuf_tensor("r_sb", [1, NT], F32))
        mr_sb = ec(nc.sbuf_tensor("mr_sb", [1, NT], BF16))
        rhi_sb = ec(nc.sbuf_tensor("rhi_sb", [1, NT], BF16))
        rlo_sb = ec(nc.sbuf_tensor("rlo_sb", [1, NT], BF16))
        rb_sb = ec(nc.sbuf_tensor("rb_sb", [128, NT], F32))
        sm_sb = ec(nc.sbuf_tensor("sm_sb", [1, 8], F32))
        bbf_sb = ec(nc.sbuf_tensor("bbf_sb", [1, 1], BF16))
        bias_sb = ec(nc.sbuf_tensor("bias_sb", [128, 1], F32))
        rden_sb, dhi_sb, dlo_sb = r_sb, rhi_sb, rlo_sb
        xbf = ec(nc.sbuf_tensor("xbf", [128, 2, DC, NT], BF16))
        x2 = ec(nc.sbuf_tensor("x2", [128, 2, NT], BF16))
        tab = ec(nc.sbuf_tensor("tab", [128, 1, 2, NT], F32))
        A = ec(nc.psum_tensor("A", [128, 1024], F32))
        Bp = ec(nc.psum_tensor("Bp", [128, 1024], F32))
        Dp = ec(nc.psum_tensor("Dp", [128, 512], F32))
        G = ec(nc.psum_tensor("G", [128, 512], F32))
        Pq = ec(nc.psum_tensor("Pq", [128, 4, 256], F32))
        PqF = Pq.ap().rearrange("p a n -> p (a n)")

        sems = {}
        for name in ["s_w", "s_misc", "s_w1", "s_w1e", "s_init",
                     "s_x0", "s_x1", "s_tab", "s_sq", "s_serows", "s_st", "s_var",
                     "s_fin", "s_fin2", "s_rb", "s_rbe", "s_csr", "s_tek",
                     "s_tev", "s_kp", "s_k2s", "s_k2m", "s_kmax", "s_vs",
                     "s_vt", "s_ve2", "s_p1d",
                     "s_xf", "s_xb2", "s_tabq", "s_fwk", "s_fwr", "s_fwv",
                     "s_st2", "s_sq2",
                     "s_srow2", "s_var2", "s_fin_2", "s_rows2", "s_rbmm2",
                     "s_xn", "s_q", "s_qr", "s_q2", "s_qn", "s_qmaxd",
                     "s_qb", "s_bmm", "s_bias", "s_qk", "s_exp", "s_av",
                     "s_den", "s_dbm", "s_o", "s_mix", "s_kpre", "s_rl",
                     "s_kh", "s_rpre", "s_g", "s_facc", "s_out", "s_store",
                     "s_vser", "s_sser", "s_yini", "s_xb3", "s_oev", "s_sqm", "s_sqm2", "s_rbcp"]:
            sems[name] = ec(nc.semaphore(name))
        sm = type("SM", (), sems)
        vst = {"prev": None, "n": 0}
        sst = {"prev": None, "n": 0}

        block = ec(nc.Block())

        with ExitStack() as p1:
            e1 = p1.enter_context
            csr = e1(nc.sbuf_tensor("csr", [128, NT], F32))
            scr = e1(nc.sbuf_tensor("scr", [128, NT], F32))
            p1s = e1(nc.sbuf_tensor("p1s", [128, NT], F32))
            p2s = e1(nc.sbuf_tensor("p2s", [128, NT], F32))
            vt = e1(nc.sbuf_tensor("vt", [128, NT], F32))
            k2 = e1(nc.sbuf_tensor("k2", [128, NT], BF16))

            @block.gpsimd
            def _(g):
                for wi, (dst, srcw) in enumerate([(aq_s, aq), (ak_s, ak), (av_s, av),
                                                  (ao_s, ao), (ones_s, onesd), (id_s, identd)]):
                    if wi >= 2:
                        g.wait_ge(sm.s_w, 16 * wi)
                    g.dma_start(out=dst[:], in_=srcw).then_inc(sm.s_w, 16)
                for t in range(ST):
                    if t >= 2:
                        g.wait_ge(sm.s_st, t - 1)
                        g.wait_ge(sm.s_tek, t - 1)
                        g.wait_ge(sm.s_tev, t - 1)
                        g.wait_ge(sm.s_sq, 8 * (t - 1))
                    g.dma_start(out=xbf[:, t % 2, :, :],
                                in_=xT[:, t, :, :]).then_inc(
                                    sm.s_x0 if t % 2 == 0 else sm.s_x1, 16)
                g.nop().then_inc(sm.s_p1d, 1)

            @block.sync
            def _(s):
                s.dma_start(out=mur_s[:], in_=mur).then_inc(sm.s_misc, 16)
                s.dma_start(out=muk_s[:], in_=muk).then_inc(sm.s_misc, 16)
                s.dma_start(out=halo_s[:], in_=halo).then_inc(sm.s_misc, 16)
                for t in range(ST):
                    if t >= 1:
                        s.wait_ge(sm.s_csr, t)
                    s.dma_start(out=tab[:, 0, 0, :], in_=cs[:, t * NT:(t + 1) * NT]).then_inc(sm.s_tab, 16)
                    s.dma_start(out=tab[:, 0, 1, :], in_=sc[:, t * NT:(t + 1) * NT]).then_inc(sm.s_tab, 16)
                s.nop().then_inc(sm.s_p1d, 1)

            @block.tensor
            def _(t):
                t.wait_ge(sm.s_w, 96)
                for c in range(DC):
                    ins = t.matmul(Dp[0:1, :128], ones_s[:, 0:1], ak_s[:, c, :],
                                   start=(c == 0), stop=(c == DC - 1))
                ins.then_inc(sm.s_w1, 1)
                for c in range(DC):
                    ins = t.matmul(Dp[32:33, :128], ones_s[:, 0:1], av_s[:, c, :],
                                   start=(c == 0), stop=(c == DC - 1))
                ins.then_inc(sm.s_w1, 1)

                def stats_mms(tt):
                    t.wait_ge(sm.s_x0 if tt % 2 == 0 else sm.s_x1, 16 * (tt // 2 + 1))
                    t.wait_ge(sm.s_serows, tt)
                    if tt == 0:
                        t.wait_ge(sm.s_w1e, 1)
                    for c in range(DC):
                        ins = t.matmul(Dp[0:1, :], ones_s[:, 0:1], xbf[:, tt % 2, c, :],
                                       start=(c == 0), stop=(c == DC - 1))
                    ins.then_inc(sm.s_st, 1)
                    for c in range(DC):
                        t.wait_ge(sm.s_sq, 8 * tt + c + 1)
                        t.matmul(Dp[32:33, :], ones_s[:, 0:1], x2[:, c % 2, :],
                                 start=(c == 0), stop=(c == DC - 1)).then_inc(sm.s_sqm, 1)

                def mid_mms(tt):
                    h0 = (tt % 2) * 512
                    t.wait_ge(sm.s_fin2, tt + 1)
                    t.wait_ge(sm.s_rbe, tt)
                    t.matmul(G[:, :], ones_s[0:1, :], rhi_sb[0:1, :], start=True, stop=False)
                    t.matmul(G[:, :], ones_s[0:1, :], rlo_sb[0:1, :], start=False, stop=True).then_inc(sm.s_rb, 1)
                    t.wait_ge(sm.s_kp, max(0, tt - 1))
                    for c in range(DC):
                        t.matmul(A[:, h0:h0 + 512], ak_s[:, c, :], xbf[:, tt % 2, c, :],
                                 start=(c == 0), stop=False)
                    t.matmul(A[:, h0:h0 + 512], w1k_s[0:1, :], mr_sb[0:1, :],
                             start=False, stop=True).then_inc(sm.s_tek, 1)
                    t.wait_ge(sm.s_vs, max(0, tt - 1))
                    for c in range(DC):
                        t.matmul(Bp[:, h0:h0 + 512], av_s[:, c, :], xbf[:, tt % 2, c, :],
                                 start=(c == 0), stop=False)
                    t.matmul(Bp[:, h0:h0 + 512], w1v_s[0:1, :], mr_sb[0:1, :],
                             start=False, stop=True).then_inc(sm.s_tev, 1)

                def tail_mms(tt):
                    t.wait_ge(sm.s_vs, tt + 1)
                    t.wait_ge(sm.s_ve2, tt)
                    for j in range(4):
                        t.transpose(Pq[:, j, 0:128],
                                    vt[:, j * 128:(j + 1) * 128],
                                    id_s[:]).then_inc(sm.s_vt, 1)
                    t.wait_ge(sm.s_k2s, tt + 1)
                    t.wait_ge(sm.s_kmax, tt)
                    t.matmul(Dp[64:65, :], ones_s[:, 0:1], k2[:, :],
                             start=True, stop=True).then_inc(sm.s_k2m, 1)

                for stage in range(ST + 2):
                    if stage < ST:
                        stats_mms(stage)
                    if 1 <= stage <= ST:
                        mid_mms(stage - 1)
                    if stage >= 2:
                        tail_mms(stage - 2)
                t.wait_ge(sm.s_w1, 2)
                t.wait_ge(sm.s_st, ST)
                t.wait_ge(sm.s_sqm, 8 * ST)
                t.wait_ge(sm.s_rb, ST)
                t.wait_ge(sm.s_tek, ST)
                t.wait_ge(sm.s_tev, ST)
                t.wait_ge(sm.s_vt, 4 * ST)
                t.wait_ge(sm.s_k2m, ST)
                t.nop().then_inc(sm.s_p1d, 1)

            @block.scalar
            def _(se0):
                se = _Ser(se0, sm.s_sser, sst)
                se.wait_ge(sm.s_init, 1)
                for stage in range(ST + 2):
                    if stage < ST:
                        tt = stage
                        se.wait_ge(sm.s_x0 if tt % 2 == 0 else sm.s_x1, 16 * (tt // 2 + 1))
                        for c in range(DC):
                            sidx = 8 * tt + c
                            se.wait_ge(sm.s_sqm, max(0, sidx - 1))
                            se.activation(out=x2[:, c % 2, :], in_=xbf[:, tt % 2, c, :],
                                          func=AF.Square, inc=(sm.s_sq, 1, sidx + 1))
                        se.wait_ge(sm.s_st, tt + 1)
                        se.wait_ge(sm.s_sqm, 8 * (tt + 1))
                        se.wait_ge(sm.s_var, tt)
                        se.wait_ge(sm.s_fin2, tt)
                        se.activation(out=m_sb[0:1, :], in_=Dp[0:1, :], func=AF.Copy,
                                      scale=1.0 / D)
                        se.activation(out=e2_sb[0:1, :], in_=Dp[32:33, :], func=AF.Copy,
                                      scale=1.0 / D, inc=(sm.s_serows, 1, tt + 1))
                        se.wait_ge(sm.s_var, tt + 1)
                        se.activation(out=var_sb[0:1, :], in_=var_sb[0:1, :], func=AF.Ln,
                                      bias=eps_s[0:1, :])
                        se.activation(out=r_sb[0:1, :], in_=var_sb[0:1, :], func=AF.Exp,
                                      scale=-0.5, inc=(sm.s_fin, 1, tt + 1))
                    if stage >= 2:
                        tt = stage - 2
                        se.wait_ge(sm.s_kp, tt + 1)
                        se.wait_ge(sm.s_k2m, tt)
                        se.activation(out=k2[:, :], in_=Ksb[:, tt * NT:(tt + 1) * NT],
                                      func=AF.Square, inc=(sm.s_k2s, 1, tt + 1))
                se.nop(inc=(sm.s_p1d, 1, None))

            @block.vector
            def _(v0):
                v = _Ser(v0, sm.s_vser, vst)
                v.memset(sm_sb[0:1, :], 0.0)
                v.memset(Vsb[:, :, 128:129], 1.0)
                v.memset(eps_s[0:1, :], EPS, inc=(sm.s_init, 1, 1))
                v.wait_ge(sm.s_w1, 2)
                v.tensor_scalar(out=w1k_s[0:1, :], in0=Dp[0:1, :128], scalar1=-1.0,
                                scalar2=None, op0=ALU.mult)
                v.tensor_scalar(out=w1v_s[0:1, :], in0=Dp[32:33, :128], scalar1=-1.0,
                                scalar2=None, op0=ALU.mult, inc=(sm.s_w1e, 1, 1))
                for stage in range(ST + 2):
                    if stage < ST:
                        tt = stage
                        v.wait_ge(sm.s_serows, tt + 1)
                        v.wait_ge(sm.s_fin, tt)
                        v.tensor_tensor(out=var_sb[0:1, :], in0=m_sb[0:1, :],
                                        in1=m_sb[0:1, :], op=ALU.mult)
                        v.tensor_tensor(out=var_sb[0:1, :], in0=e2_sb[0:1, :],
                                        in1=var_sb[0:1, :], op=ALU.subtract,
                                        inc=(sm.s_var, 1, tt + 1))
                        v.wait_ge(sm.s_fin, tt + 1)
                        v.wait_ge(sm.s_tek, tt)
                        v.wait_ge(sm.s_tev, tt)
                        v.wait_ge(sm.s_rb, tt)
                        v.tensor_copy(mr_sb[0:1, :], m_sb[0:1, :])
                        v.tensor_copy(rhi_sb[0:1, :], r_sb[0:1, :])
                        v.tensor_tensor(out=rlo_sb[0:1, :], in0=r_sb[0:1, :],
                                        in1=rhi_sb[0:1, :], op=ALU.subtract,
                                        inc=(sm.s_fin2, 1, tt + 1))
                    if 1 <= stage <= ST:
                        tt = stage - 1
                        h0 = (tt % 2) * 512
                        v.wait_ge(sm.s_rb, tt + 1)
                        v.tensor_copy(rb_sb[:, :], G[:, :], inc=(sm.s_rbe, 1, tt + 1))
                        v.wait_ge(sm.s_tab, 32 * (tt + 1))
                        v.tensor_tensor(out=csr[:, :], in0=tab[:, 0, 0, :],
                                        in1=rb_sb[:, :], op=ALU.mult)
                        v.tensor_tensor(out=scr[:, :], in0=tab[:, 0, 1, :],
                                        in1=rb_sb[:, :], op=ALU.mult,
                                        inc=(sm.s_csr, 1, tt + 1))
                        v.wait_ge(sm.s_tek, tt + 1)
                        v.tensor_tensor(out=p1s[0:64, :], in0=csr[0:64, :], in1=A[0:64, h0:h0 + 512], op=ALU.mult)
                        v.tensor_tensor(out=p2s[0:64, :], in0=scr[0:64, :], in1=A[64:128, h0:h0 + 512], op=ALU.mult)
                        v.tensor_tensor(out=Ksb[0:64, tt * NT:(tt + 1) * NT],
                                        in0=p1s[0:64, :], in1=p2s[0:64, :], op=ALU.subtract)
                        v.tensor_tensor(out=p1s[64:128, :], in0=csr[64:128, :], in1=A[0:64, h0:h0 + 512], op=ALU.mult)
                        v.tensor_tensor(out=p2s[64:128, :], in0=scr[64:128, :], in1=A[64:128, h0:h0 + 512], op=ALU.mult)
                        v.tensor_tensor(out=Ksb[64:128, tt * NT:(tt + 1) * NT],
                                        in0=p1s[64:128, :], in1=p2s[64:128, :], op=ALU.add,
                                        inc=(sm.s_kp, 1, tt + 1))
                        v.wait_ge(sm.s_tev, tt + 1)
                        v.wait_ge(sm.s_vt, 4 * tt)
                        v.tensor_tensor(out=vt[:, :], in0=Bp[:, h0:h0 + 512], in1=rb_sb[:, :],
                                        op=ALU.mult, inc=(sm.s_vs, 1, tt + 1))
                    if stage >= 2:
                        tt = stage - 2
                        v.wait_ge(sm.s_vt, 4 * (tt + 1))
                        v.tensor_copy(Vsb[:, tt * 4:(tt + 1) * 4, 0:128], Pq[:, :, 0:128],
                                      inc=(sm.s_ve2, 1, tt + 1))
                        v.wait_ge(sm.s_k2m, tt + 1)
                        v.reduce_max(sm_sb[0:1, 1:2], Dp[64:65, :], axis=AX.X)
                        v.tensor_tensor(out=sm_sb[0:1, 0:1], in0=sm_sb[0:1, 0:1],
                                        in1=sm_sb[0:1, 1:2], op=ALU.max,
                                        inc=(sm.s_kmax, 1, tt + 1))
                v.nop(inc=(sm.s_p1d, 1, None))

        with ExitStack() as p2:
            e2 = p2.enter_context
            fbuf = [e2(nc.sbuf_tensor(f"fbuf{i}", [128, DC, DC, 128], BF16)) for i in range(2)]
            xn = e2(nc.sbuf_tensor("xn", [128, 2, DC, NT + 1], BF16))
            q_sb = e2(nc.sbuf_tensor("q_sb", [128, NT], BF16))
            q2_sb = e2(nc.sbuf_tensor("q2_sb", [128, NT], BF16))
            Pr = e2(nc.sbuf_tensor("Pr", [128, 2, 1024], BF16))
            o_sb = e2(nc.sbuf_tensor("o_sb", [128, NT], BF16))
            oq_sb = e2(nc.sbuf_tensor("oq_sb", [128, NT], F32))
            recip_sb = e2(nc.sbuf_tensor("recip_sb", [128, 4], F32))
            xr_sb = e2(nc.sbuf_tensor("xr_sb", [128, DC, NT], BF16))
            xk_sb = e2(nc.sbuf_tensor("xk_sb", [128, DC, NT], BF16))
            kh_sb = e2(nc.sbuf_tensor("kh_sb", [128, DC, NT], BF16))
            d_sb = e2(nc.sbuf_tensor("d_sb", [128, NT], BF16))
            tt_sb = e2(nc.sbuf_tensor("tt_sb", [128, 1, NT], BF16))
            t1_sb = e2(nc.sbuf_tensor("t1_sb", [128, NT], F32))
            outs = e2(nc.sbuf_tensor("outs", [128, 1, NT], F32))
            g_sb = e2(nc.sbuf_tensor("g_sb", [128, DC, NT], BF16))
            p1q, p2q = t1_sb, rb_sb

            NL = QT + 1

            @block.gpsimd
            def _(g):
                def xload(e):
                    if e >= 2:
                        g.wait_ge(sm.s_st2, e - 1)
                        g.wait_ge(sm.s_sq2, 8 * (e - 1))
                        g.wait_ge(sm.s_xn, e - 1)
                    g.dma_start(out=xbf[:, e % 2, :, :],
                                in_=xT[:, e, :, :]).then_inc(
                                    sm.s_xb2 if e % 2 == 0 else sm.s_xb3, 16)

                g.wait_ge(sm.s_p1d, 5)
                xload(0)
                xload(1)
                xload(2)
                for ell in range(1, NL):
                    if ell + 2 < NL:
                        xload(ell + 2)
                    if ell >= 1:
                        u = ell - 1
                        for i, (w, fsem) in enumerate([(fk, sm.s_fwk), (fr, sm.s_fwr),
                                                       (fv, sm.s_fwv)]):
                            li = 3 * u + i
                            buf = fbuf[li % 2]
                            if i == 0 and u >= 1:
                                g.wait_ge(sm.s_rpre, 8 * u)
                            if i == 1 and u >= 1:
                                g.wait_ge(sm.s_facc, 16 * u)
                            if i == 1:
                                g.wait_ge(sm.s_fwk, 16 * (u + 1))
                            if i == 2:
                                g.wait_ge(sm.s_kpre, 8 * u + 8)
                                g.wait_ge(sm.s_fwr, 16 * (u + 1))
                            g.dma_start(out=buf[:], in_=w).then_inc(fsem, 16)
                    if ell >= 1:
                        u = ell - 1
                        if ell == 1:
                            g.wait_ge(sm.s_yini, 16)
                        for dc in range(DC):
                            g.wait_ge(sm.s_out, 8 * u + dc + 1)
                            if 8 * u + dc >= 1:
                                g.wait_ge(sm.s_store, 16 * (8 * u + dc))
                            g.dma_start(out=y[:, u, dc, :], in_=outs[:, 0, :],
                                        accum_op=ALU.add).then_inc(sm.s_store, 16)
                g.wait_ge(sm.s_store, 16 * 8 * QT)

            @block.sync
            def _(s):
                s.wait_ge(sm.s_p1d, 5)
                s.dma_start(out=y[:, :, :, :], in_=xT[:, 1:QT + 1, :, :]).then_inc(sm.s_yini, 16)
                for ell in range(NL):
                    u = ell - 1
                    if ell >= 1:
                        s.wait_ge(sm.s_qr, u)
                        s.dma_start(out=tab[:, 0, 0, :], in_=csq[:, u * NT:(u + 1) * NT]).then_inc(sm.s_tabq, 16)
                        s.dma_start(out=tab[:, 0, 1, :], in_=scq[:, u * NT:(u + 1) * NT]).then_inc(sm.s_tabq, 16)

            @block.tensor
            def _(t):
                t.wait_ge(sm.s_p1d, 5)

                def stats_x(e):
                    t.wait_ge(sm.s_xb2 if e % 2 == 0 else sm.s_xb3, 16 * (e // 2 + 1))
                    t.wait_ge(sm.s_srow2, e)
                    for c in range(DC):
                        ins = t.matmul(Dp[32:33, :], ones_s[:, 0:1], xbf[:, e % 2, c, :],
                                       start=(c == 0), stop=(c == DC - 1))
                    ins.then_inc(sm.s_st2, 1)

                def stats_x2(e, c):
                    t.wait_ge(sm.s_sq2, 8 * e + c + 1)
                    t.matmul(Dp[64:65, :], ones_s[:, 0:1], x2[:, c % 2, :],
                             start=(c == 0), stop=(c == DC - 1)).then_inc(sm.s_sqm2, 1)

                def stats_g(e):
                    stats_x(e)
                    for c in range(DC):
                        stats_x2(e, c)

                def bcast_g(e):
                    t.wait_ge(sm.s_rows2, e + 1)
                    t.wait_ge(sm.s_xn, max(0, e))
                    t.wait_ge(sm.s_bias, max(0, e - 1))
                    t.wait_ge(sm.s_out, 8 * max(0, e - 2))
                    t.matmul(G[:, :], ones_s[0:1, :], rhi_sb[0:1, :], start=True, stop=False)
                    t.matmul(G[:, :], ones_s[0:1, :], rlo_sb[0:1, :],
                             start=False, stop=True).then_inc(sm.s_rbmm2, 1)

                def mr_g(e):
                    t.wait_ge(sm.s_rbcp, e + 1)
                    t.matmul(G[:, :], ones_s[0:1, :], mr_sb[0:1, :],
                             start=True, stop=True).then_inc(sm.s_rbmm2, 1)

                def qproj_g(uu):
                    t.wait_ge(sm.s_xn, uu + 2)
                    t.wait_ge(sm.s_o, 4 * uu)
                    t.wait_ge(sm.s_den, uu)
                    for c in range(DC):
                        ins = t.matmul(PqF[:, 0:512], aq_s[:, c, :], xn[:, (uu + 1) % 2, c, 1:NT + 1],
                                       start=(c == 0), stop=(c == DC - 1))
                    ins.then_inc(sm.s_q, 1)

                def qnorm_g(uu):
                    t.wait_ge(sm.s_q2, uu + 1)
                    t.wait_ge(sm.s_qmaxd, uu)
                    t.matmul(Dp[0:1, :], ones_s[:, 0:1], q2_sb[:, :],
                             start=True, stop=True).then_inc(sm.s_qn, 1)
                    t.wait_ge(sm.s_qb, uu + 1)
                    t.wait_ge(sm.s_xn, min(uu + 2, NL))
                    t.matmul(G[:, 0:1], ones_s[0:1, :], bbf_sb[0:1, :],
                             start=True, stop=True).then_inc(sm.s_bmm, 1)

                def flash(u, ell):
                    t.wait_ge(sm.s_qr, u + 1)
                    t.wait_ge(sm.s_bias, u + 1)
                    t.wait_ge(sm.s_den, u)
                    t.wait_ge(sm.s_o, 4 * u)
                    t.wait_ge(sm.s_oev, u)
                    if u >= 1:
                        t.wait_ge(sm.s_g, 8 * u)
                        t.wait_ge(sm.s_rl, 8 * u)
                        t.wait_ge(sm.s_out, 8 * u)

                    def av_g(gj):
                        t.wait_ge(sm.s_exp, 64 * u + gj + 1)
                        for h in range(2):
                            kc = 2 * gj + h
                            for j in range(4):
                                t.matmul(Pq[:, j, 0:129],
                                         Pr[:, gj % 2, h * 512 + j * 128:h * 512 + (j + 1) * 128],
                                         Vsb[:, kc, :],
                                         start=(kc == 0 and h == 0 and j % 2 == 0),
                                         stop=(kc == KC - 1 and h == 1 and j % 2 == 1)).then_inc(sm.s_av, 1)

                    for gi in range(NG):
                        pg = A if gi % 2 == 0 else Bp
                        t.wait_ge(sm.s_exp, max(0, 64 * u + gi - 1))
                        for h in range(2):
                            kc = 2 * gi + h
                            t.matmul(pg[:, h * 512:(h + 1) * 512],
                                     Ksb[:, kc * 128:(kc + 1) * 128], q_sb[:, :],
                                     start=True, stop=True).then_inc(sm.s_qk, 1)
                        if ell + 1 < NL:
                            if gi == 2:
                                stats_x(ell + 1)
                            if 3 <= gi <= 10:
                                stats_x2(ell + 1, gi - 3)
                            if gi == 15:
                                bcast_g(ell + 1)
                            if gi == 20:
                                mr_g(ell + 1)
                        if gi >= 1:
                            av_g(gi - 1)
                    av_g(NG - 1)

                stats_g(0)
                bcast_g(0)
                mr_g(0)
                stats_g(1)
                bcast_g(1)
                mr_g(1)
                qproj_g(0)
                qnorm_g(0)

                for ell in range(1, NL):
                    u = ell - 1
                    flash(u, ell)
                    t.wait_ge(sm.s_fwk, 16 * (u + 1))
                    for dc in range(DC):
                        pf = (A if dc % 2 == 0 else Bp)
                        t.wait_ge(sm.s_mix, 8 * u + dc + 1)
                        t.wait_ge(sm.s_rl, max(0, 8 * u + dc - 1))
                        for c in range(DC):
                            ins = t.matmul(pf[:, 0:512], fbuf[(3 * u) % 2][:, c, dc, :],
                                           xk_sb[:, c, :], start=(c == 0), stop=(c == DC - 1))
                        ins.then_inc(sm.s_kpre, 1)
                    t.wait_ge(sm.s_o, 4 * (u + 1))
                    t.wait_ge(sm.s_oev, u)
                    for j in range(4):
                        t.transpose(Pq[:, 2 + j // 2, (j % 2) * 128:(j % 2) * 128 + 128],
                                    oq_sb[:, j * 128:(j + 1) * 128],
                                    id_s[:]).then_inc(sm.s_dbm, 1)
                    if u + 1 <= QT - 1:
                        qproj_g(u + 1)
                    t.wait_ge(sm.s_fwr, 16 * (u + 1))
                    for dc in range(DC):
                        pf = (A if dc % 2 == 0 else Bp)
                        t.wait_ge(sm.s_g, max(0, 8 * u + dc - 1))
                        if dc in (0, 1):
                            t.wait_ge(sm.s_rl, 8 * u + 7 + dc)
                        for c in range(DC):
                            ins = t.matmul(pf[:, 0:512], fbuf[(3 * u + 1) % 2][:, c, dc, :],
                                           xr_sb[:, c, :], start=(c == 0), stop=(c == DC - 1))
                        ins.then_inc(sm.s_rpre, 1)
                    if u + 1 <= QT - 1:
                        qnorm_g(u + 1)
                    t.wait_ge(sm.s_fwv, 16 * (u + 1))
                    t.wait_ge(sm.s_oev, u + 1)
                    t.wait_ge(sm.s_kh, 8 * u + 8)
                    t.wait_ge(sm.s_qr, min(u + 2, QT))
                    t.wait_ge(sm.s_bias, min(u + 2, QT))
                    for dc in range(DC):
                        pacc = (A if dc % 2 == 0 else Bp)
                        pff = (PqF[:, 512:1024] if dc % 2 == 0 else G[:, 0:512])
                        t.wait_ge(sm.s_out, max(0, 8 * u + dc - 1))
                        if dc in (0, 1):
                            t.wait_ge(sm.s_g, 8 * u + 7 + dc)
                        t.matmul(pacc[:, 512:1024], ao_s[:, dc * 128:(dc + 1) * 128],
                                 o_sb[:, :], start=True, stop=True).then_inc(sm.s_facc, 1)
                        for c in range(DC):
                            ins = t.matmul(pff, fbuf[(3 * u + 2) % 2][:, c, dc, :],
                                           kh_sb[:, c, :], start=(c == 0), stop=(c == DC - 1))
                        ins.then_inc(sm.s_facc, 1)

            @block.scalar
            def _(se0):
                se = _Ser(se0, sm.s_sser, sst)
                se.wait_ge(sm.s_p1d, 5)

                def sq_c(e, c):
                    sidx = 8 * e + c
                    if c == 0:
                        se.wait_ge(sm.s_xb2 if e % 2 == 0 else sm.s_xb3, 16 * (e // 2 + 1))
                    se.wait_ge(sm.s_sqm2, max(0, sidx - 1))
                    se.activation(out=x2[:, c % 2, :], in_=xbf[:, e % 2, c, :],
                                  func=AF.Square, inc=(sm.s_sq2, 1, sidx + 1))

                def sq_g(e):
                    for c in range(DC):
                        sq_c(e, c)

                def rows_g_a(e):
                    se.wait_ge(sm.s_st2, e + 1)
                    se.wait_ge(sm.s_sqm2, 8 * (e + 1))
                    se.wait_ge(sm.s_var2, e)
                    se.wait_ge(sm.s_rows2, e)
                    se.activation(out=m_sb[0:1, :], in_=Dp[32:33, :], func=AF.Copy, scale=1.0 / D)
                    se.activation(out=e2_sb[0:1, :], in_=Dp[64:65, :], func=AF.Copy,
                                  scale=1.0 / D, inc=(sm.s_srow2, 1, e + 1))

                def rows_g_b(e):
                    se.wait_ge(sm.s_var2, e + 1)
                    se.wait_ge(sm.s_den, max(0, e - 2))
                    se.activation(out=var_sb[0:1, :], in_=var_sb[0:1, :], func=AF.Ln,
                                  bias=eps_s[0:1, :])
                    se.activation(out=r_sb[0:1, :], in_=var_sb[0:1, :], func=AF.Exp,
                                  scale=-0.5, inc=(sm.s_fin_2, 1, e + 1))

                def rows_g(e):
                    rows_g_a(e)
                    rows_g_b(e)

                def q2_g(uu):
                    se.wait_ge(sm.s_qr, uu + 1)
                    se.wait_ge(sm.s_qn, uu)
                    se.activation(out=q2_sb[:, :], in_=q_sb[:, :], func=AF.Square,
                                  inc=(sm.s_q2, 1, uu + 1))

                sq_g(0)
                rows_g(0)
                sq_g(1)
                rows_g(1)
                q2_g(0)

                for ell in range(1, NL):
                    u = ell - 1
                    se.wait_ge(sm.s_bias, u + 1)
                    for gi in range(NG):
                        pg = A if gi % 2 == 0 else Bp
                        se.wait_ge(sm.s_qk, 128 * u + 2 * gi + 2)
                        se.wait_ge(sm.s_av, max(0, 8 * (64 * u + gi - 1)))
                        se.activation(out=Pr[:, gi % 2, :], in_=pg[:, :], func=AF.Exp,
                                      scale=RSQ, bias=bias_sb[:, :],
                                      inc=(sm.s_exp, 1, 64 * u + gi + 1))
                        if 1 <= gi <= 8 and ell + 1 < NL:
                            sq_c(ell + 1, gi - 1)
                        if gi == 11 and ell + 1 < NL:
                            rows_g_a(ell + 1)
                        if gi == 13 and ell + 1 < NL:
                            rows_g_b(ell + 1)
                    se.wait_ge(sm.s_den, u + 1)
                    se.wait_ge(sm.s_dbm, 4 * u)
                    for j in range(4):
                        se.activation(out=oq_sb[:, j * 128:(j + 1) * 128], in_=Pq[:, j, 0:128],
                                      func=AF.Copy, scale=recip_sb[:, j:j + 1],
                                      inc=(sm.s_o, 1, 4 * u + j + 1))
                    for dc in range(DC):
                        pf = (A if dc % 2 == 0 else Bp)
                        se.wait_ge(sm.s_kpre, 8 * u + dc + 1)
                        se.wait_ge(sm.s_kh, 8 * u + dc)
                        se.activation(out=tt_sb[:, 0, :], in_=pf[:, 0:512], func=AF.Relu,
                                      inc=(sm.s_rl, 1, 8 * u + dc + 1))
                    se.wait_ge(sm.s_out, 8 * u)
                    for dc in range(DC):
                        pf = (A if dc % 2 == 0 else Bp)
                        se.wait_ge(sm.s_rpre, 8 * u + dc + 1)
                        se.wait_ge(sm.s_kh, 8 * u + 8)
                        se.activation(out=tt_sb[:, 0, :], in_=pf[:, 0:512], func=AF.Tanh, scale=0.5)
                        se.activation(out=g_sb[:, dc, :], in_=tt_sb[:, 0, :], func=AF.Copy,
                                      scale=0.5, bias=0.5, inc=(sm.s_g, 1, 8 * u + dc + 1))
                    if u + 1 <= QT - 1:
                        q2_g(u + 1)

            @block.vector
            def _(v0):
                v = _Ser(v0, sm.s_vser, vst)
                v.wait_ge(sm.s_p1d, 5)

                def rows_v(e):
                    v.wait_ge(sm.s_srow2, e + 1)
                    v.wait_ge(sm.s_fin_2, e)
                    v.tensor_tensor(out=var_sb[0:1, :], in0=m_sb[0:1, :], in1=m_sb[0:1, :], op=ALU.mult)
                    v.tensor_tensor(out=var_sb[0:1, :], in0=e2_sb[0:1, :], in1=var_sb[0:1, :],
                                    op=ALU.subtract, inc=(sm.s_var2, 1, e + 1))
                    v.wait_ge(sm.s_fin_2, e + 1)
                    v.wait_ge(sm.s_rbmm2, 2 * e)
                    v.tensor_tensor(out=mr_sb[0:1, :], in0=m_sb[0:1, :], in1=r_sb[0:1, :], op=ALU.mult)
                    v.tensor_copy(rhi_sb[0:1, :], r_sb[0:1, :])
                    v.tensor_tensor(out=rlo_sb[0:1, :], in0=r_sb[0:1, :], in1=rhi_sb[0:1, :],
                                    op=ALU.subtract, inc=(sm.s_rows2, 1, e + 1))

                def xn_v(e):
                    v.wait_ge(sm.s_rbmm2, 2 * e + 1)
                    v.wait_ge(sm.s_xb2 if e % 2 == 0 else sm.s_xb3, 16 * (e // 2 + 1))
                    if e >= 2:
                        v.wait_ge(sm.s_mix, 8 * (e - 1))
                    v.tensor_copy(rb_sb[:, :], G[:, :], inc=(sm.s_rbcp, 1, e + 1))
                    v.wait_ge(sm.s_rbmm2, 2 * (e + 1))
                    for dc in range(DC):
                        v.tensor_tensor(out=t1_sb[:, :], in0=xbf[:, e % 2, dc, :],
                                        in1=rb_sb[:, :], op=ALU.mult)
                        v.tensor_tensor(out=xn[:, e % 2, dc, 1:NT + 1], in0=t1_sb[:, :],
                                        in1=G[:, :], op=ALU.subtract)
                        if e >= 1:
                            v.tensor_copy(xn[:, e % 2, dc, 0:1], xn[:, (e - 1) % 2, dc, NT:NT + 1])
                        else:
                            v.memset(xn[:, 0, dc, 0:1], 0.0)
                    if e == 0:
                        v.wait_ge(sm.s_misc, 48)
                        for dc in range(DC):
                            v.tensor_scalar(out=xn[:, 0, dc, NT:NT + 1], in0=xn[:, 0, dc, NT:NT + 1],
                                            scalar1=halo_s[:, :], scalar2=None, op0=ALU.mult)
                    v.nop(inc=(sm.s_xn, 1, e + 1))

                def rope_v(uu):
                    v.wait_ge(sm.s_q, uu + 1)
                    v.wait_ge(sm.s_tabq, 32 * (uu + 1))
                    v.tensor_tensor(out=p1q[0:64, :], in0=tab[0:64, 0, 0, :], in1=PqF[0:64, 0:512], op=ALU.mult)
                    v.tensor_tensor(out=p2q[0:64, :], in0=tab[0:64, 0, 1, :], in1=PqF[64:128, 0:512], op=ALU.mult)
                    v.tensor_tensor(out=q_sb[0:64, :], in0=p1q[0:64, :], in1=p2q[0:64, :], op=ALU.subtract)
                    v.tensor_tensor(out=p1q[64:128, :], in0=tab[64:128, 0, 0, :], in1=PqF[0:64, 0:512], op=ALU.mult)
                    v.tensor_tensor(out=p2q[64:128, :], in0=tab[64:128, 0, 1, :], in1=PqF[64:128, 0:512], op=ALU.mult)
                    v.tensor_tensor(out=q_sb[64:128, :], in0=p1q[64:128, :], in1=p2q[64:128, :],
                                    op=ALU.add, inc=(sm.s_qr, 1, uu + 1))

                def mix_v(uu):
                    e = uu + 1
                    if uu >= 1:
                        v.wait_ge(sm.s_rpre, 8 * uu)
                        v.wait_ge(sm.s_kpre, 8 * uu)
                    for dc in range(DC):
                        v.tensor_tensor(out=d_sb[:, :], in0=xn[:, e % 2, dc, 1:NT + 1],
                                        in1=xn[:, e % 2, dc, 0:NT], op=ALU.subtract)
                        v.scalar_tensor_tensor(out=xr_sb[:, dc, :], in0=d_sb[:, :],
                                               scalar=mur_s[:, dc:dc + 1], in1=xn[:, e % 2, dc, 0:NT],
                                               op0=ALU.mult, op1=ALU.add)
                        v.scalar_tensor_tensor(out=xk_sb[:, dc, :], in0=d_sb[:, :],
                                               scalar=muk_s[:, dc:dc + 1], in1=xn[:, e % 2, dc, 0:NT],
                                               op0=ALU.mult, op1=ALU.add,
                                               inc=(sm.s_mix, 1, 8 * uu + dc + 1))

                def bias_v(uu):
                    v.wait_ge(sm.s_qn, uu + 1)
                    v.reduce_max(sm_sb[0:1, 1:2], Dp[0:1, :], axis=AX.X,
                                 inc=(sm.s_qmaxd, 1, uu + 1))
                    v.tensor_tensor(out=sm_sb[0:1, 2:3], in0=sm_sb[0:1, 1:2], in1=sm_sb[0:1, 0:1], op=ALU.add)
                    v.tensor_scalar(out=sm_sb[0:1, 3:4], in0=sm_sb[0:1, 2:3],
                                    scalar1=-0.5 * RSQ, scalar2=None, op0=ALU.mult)
                    v.tensor_copy(bbf_sb[0:1, :], sm_sb[0:1, 3:4], inc=(sm.s_qb, 1, uu + 1))
                    v.wait_ge(sm.s_bmm, uu + 1)
                    v.tensor_copy(bias_sb[:, :], G[:, 0:1], inc=(sm.s_bias, 1, uu + 1))

                rows_v(0)
                xn_v(0)
                rows_v(1)
                xn_v(1)
                rope_v(0)
                mix_v(0)
                bias_v(0)

                for ell in range(1, NL):
                    u = ell - 1
                    if ell + 1 < NL:
                        rows_v(ell + 1)
                        xn_v(ell + 1)
                    v.wait_ge(sm.s_av, 512 * (u + 1))
                    v.wait_ge(sm.s_o, 4 * u)
                    v.reciprocal(recip_sb[:, 0:4],
                                 Pq[:, :, 128:129].rearrange("p a n -> p (a n)"),
                                 inc=(sm.s_den, 1, u + 1))
                    if u >= 1:
                        v.wait_ge(sm.s_facc, 16 * u)
                    for dc in range(DC):
                        v.wait_ge(sm.s_rl, 8 * u + dc + 1)
                        v.tensor_tensor(out=kh_sb[:, dc, :], in0=tt_sb[:, 0, :],
                                        in1=tt_sb[:, 0, :],
                                        op=ALU.mult, inc=(sm.s_kh, 1, 8 * u + dc + 1))
                    v.wait_ge(sm.s_dbm, 4 * (u + 1))
                    v.tensor_copy(o_sb.ap().rearrange("p (a n) -> p a n", n=128),
                                  Pq[:, 2:4, 0:256].rearrange("p a (b n) -> p (a b) n", n=128),
                                  inc=(sm.s_oev, 1, u + 1))
                    if u + 1 <= QT - 1:
                        rope_v(u + 1)
                    if u + 1 <= QT - 1:
                        bias_v(u + 1)
                        mix_v(u + 1)
                    for dc in range(DC):
                        pacc = (A if dc % 2 == 0 else Bp)
                        pff = (PqF[:, 512:1024] if dc % 2 == 0 else G[:, 0:512])
                        v.wait_ge(sm.s_facc, 16 * u + 2 * dc + 2)
                        v.wait_ge(sm.s_g, 8 * u + dc + 1)
                        if 8 * u + dc >= 1:
                            v.wait_ge(sm.s_store, 16 * (8 * u + dc))
                        v.tensor_tensor(out=t1_sb[:, :], in0=g_sb[:, dc, :], in1=pff, op=ALU.mult)
                        v.tensor_tensor(out=outs[:, 0, :], in0=t1_sb[:, :], in1=pacc[:, 512:1024],
                                        op=ALU.add, inc=(sm.s_out, 1, 8 * u + dc + 1))
    return nc


_NC_CACHE = None


def kernel(L0_attn_x, L0_attn_wq, L0_attn_wk, L0_attn_wv, L0_attn_wo,
           L0_ffn_W_r, L0_ffn_W_k, L0_ffn_W_v, L0_ffn_mu_r, L0_ffn_mu_k):
    global _NC_CACHE
    in_maps = make_in_maps(L0_attn_x, L0_attn_wq, L0_attn_wk, L0_attn_wv,
                           L0_attn_wo, L0_ffn_W_r, L0_ffn_W_k, L0_ffn_W_v,
                           L0_ffn_mu_r, L0_ffn_mu_k)
    if _NC_CACHE is None:
        _NC_CACHE = build_nc()
    res = run_bass_kernel_spmd(_NC_CACHE, in_maps, core_ids=list(range(8)))

    out = np.empty((B, S, D), dtype=np.float32)
    for core in range(8):
        b, qi = core // 4, core % 4
        yv = res.results[core]["y"]
        out[b, qi * Q:(qi + 1) * Q, :] = yv.transpose(1, 3, 2, 0).reshape(Q, D)
    return out


def make_in_maps(L0_attn_x, L0_attn_wq, L0_attn_wk, L0_attn_wv, L0_attn_wo,
                 L0_ffn_W_r, L0_ffn_W_k, L0_ffn_W_v, L0_ffn_mu_r, L0_ffn_mu_k):
    x = np.asarray(L0_attn_x, dtype=np.float32)
    wq = np.asarray(L0_attn_wq, dtype=np.float32)
    wk = np.asarray(L0_attn_wk, dtype=np.float32)
    wv = np.asarray(L0_attn_wv, dtype=np.float32)
    wo = np.asarray(L0_attn_wo, dtype=np.float32)
    Wr = np.asarray(L0_ffn_W_r, dtype=np.float32)
    Wk = np.asarray(L0_ffn_W_k, dtype=np.float32)
    Wv = np.asarray(L0_ffn_W_v, dtype=np.float32)
    mu_r = np.asarray(L0_ffn_mu_r, dtype=np.float32)
    mu_k = np.asarray(L0_ffn_mu_k, dtype=np.float32)

    try:
        import jax
        import jax.numpy as jnp
        with jax.default_device(jax.devices("cpu")[0]):
            inv_j = 1.0 / 10000.0 ** (jnp.arange(0, HD, 2, dtype=jnp.float32) / HD)
            t_j = jnp.arange(S - 1, dtype=jnp.float32)
            ang_j = t_j[:, None] * inv_j[None, :]
            cosr = np.asarray(jnp.cos(ang_j))
            sinr = np.asarray(jnp.sin(ang_j))
    except Exception:
        inv32 = (1.0 / (np.float32(10000.0) **
                        (np.arange(0, HD, 2, dtype=np.float32) / np.float32(HD)))
                 ).astype(np.float32)
        t32 = np.arange(S - 1, dtype=np.float32)
        ang32 = (t32[:, None] * inv32[None, :]).astype(np.float32)
        cosr = np.cos(ang32).astype(np.float32)
        sinr = np.sin(ang32).astype(np.float32)
    cosf = np.empty((64, S), dtype=np.float32)
    sinf = np.empty((64, S), dtype=np.float32)
    cosf[:, 1:] = cosr.T
    sinf[:, 1:] = sinr.T
    cosf[:, 0] = 1.0
    sinf[:, 0] = 0.0
    CS = np.concatenate([cosf, sinf], axis=0)
    SC = np.concatenate([sinf, cosf], axis=0)

    perm_h = np.concatenate([np.arange(0, HD, 2), np.arange(1, HD, 2)])

    def _wT_pch(w):
        return np.ascontiguousarray(w.T.reshape(DC, 128, HD).transpose(1, 0, 2))

    def _fT_pcdk(w):
        return np.ascontiguousarray(w.T.reshape(DC, 128, DC, 128).transpose(1, 0, 2, 3))

    aq_h = _wT_pch(wq[perm_h])
    ak_h = _wT_pch(wk[perm_h])
    av_h = _wT_pch(wv)
    ao_h = np.ascontiguousarray(wo.T)
    fr_h = _fT_pcdk(Wr)
    fk_h = _fT_pcdk(Wk)
    fv_h = _fT_pcdk(Wv)
    mur_h = np.ascontiguousarray(mu_r.reshape(DC, 128).T)
    muk_h = np.ascontiguousarray(mu_k.reshape(DC, 128).T)
    ones_h = np.ones((128, 128), dtype=np.float32)
    ident_h = np.eye(128, dtype=np.float32)

    in_maps = []
    for core in range(8):
        b, qi = core // 4, core % 4
        q0 = qi * Q
        if qi == 0:
            perm = np.concatenate([np.arange(S - NT, S), np.arange(0, Q), np.arange(Q, S - NT)])
            halo_h = np.zeros((128, 1), dtype=np.float32)
        else:
            perm = np.concatenate([np.arange(q0 - NT, q0 + Q), np.arange(0, q0 - NT),
                                   np.arange(q0 + Q, S)])
            halo_h = np.ones((128, 1), dtype=np.float32)
        xT_h = np.ascontiguousarray(
            x[b].T[:, perm].reshape(DC, 128, ST, NT).transpose(1, 2, 0, 3))
        in_maps.append({
            "xT": xT_h,
            "cs": np.ascontiguousarray(CS[:, perm]),
            "sc": np.ascontiguousarray(SC[:, perm]),
            "csq": np.ascontiguousarray(CS[:, q0:q0 + Q]),
            "scq": np.ascontiguousarray(SC[:, q0:q0 + Q]),
            "aq": aq_h, "ak": ak_h, "av": av_h, "ao": ao_h,
            "fr": fr_h, "fk": fk_h, "fv": fv_h,
            "mur": mur_h, "muk": muk_h,
            "onesd": ones_h, "identd": ident_h, "halo": halo_h,
        })
    return in_maps


# version 12
# speedup vs baseline: 1.0793x; 1.0181x over previous
import math
import numpy as np
import ml_dtypes
import concourse.bass as bass
import concourse.mybir as mybir
from concourse.bass_utils import run_bass_kernel_spmd

F32 = mybir.dt.float32
BF16 = mybir.dt.bfloat16
AF = mybir.ActivationFunctionType
ALU = mybir.AluOpType
AX = mybir.AxisListType

B = 2
S = 16384
D = 1024
HD = 128
NT = 512
ST = S // NT
Q = 4096
QT = Q // NT
DC = D // 128
KC = S // 128
NG = KC // 2
RSQ = 1.0 / math.sqrt(128.0)
EPS = 1e-5


class _Ser:

    def __init__(self, eng, sem, st):
        self._e = eng
        self._sem = sem
        self._st = st

    def wait_ge(self, sem, v):
        self._e.wait_ge(sem, v)

    def __getattr__(self, name):
        f = getattr(self._e, name)

        def g(*a, inc=None, edge=True, **kw):
            p = self._st["prev"]
            if p is not None:
                self._e.wait_ge(p[0], p[1])
            ins = f(*a, **kw)
            if inc is None:
                self._st["n"] += 1
                ins.then_inc(self._sem, 1)
                self._st["prev"] = (self._sem, self._st["n"])
            else:
                sem, k, after = inc
                ins.then_inc(sem, k)
                if edge and after is not None:
                    self._st["prev"] = (sem, after)
            return ins
        return g


def build_nc():
    nc = bass.Bass("TRN2", target_bir_lowering=False, dynamic_dma_scratch_size=16384)

    xT = nc.dram_tensor("xT", [128, ST, DC, NT], F32, kind="ExternalInput").ap()
    cs = nc.dram_tensor("cs", [128, S], F32, kind="ExternalInput").ap()
    sc = nc.dram_tensor("sc", [128, S], F32, kind="ExternalInput").ap()
    csq = nc.dram_tensor("csq", [128, Q], F32, kind="ExternalInput").ap()
    scq = nc.dram_tensor("scq", [128, Q], F32, kind="ExternalInput").ap()
    aq = nc.dram_tensor("aq", [128, DC, HD], F32, kind="ExternalInput").ap()
    ak = nc.dram_tensor("ak", [128, DC, HD], F32, kind="ExternalInput").ap()
    av = nc.dram_tensor("av", [128, DC, HD], F32, kind="ExternalInput").ap()
    ao = nc.dram_tensor("ao", [HD, D], F32, kind="ExternalInput").ap()
    fr = nc.dram_tensor("fr", [128, DC, DC, 128], F32, kind="ExternalInput").ap()
    fk = nc.dram_tensor("fk", [128, DC, DC, 128], F32, kind="ExternalInput").ap()
    fv = nc.dram_tensor("fv", [128, DC, DC, 128], F32, kind="ExternalInput").ap()
    mur = nc.dram_tensor("mur", [128, DC], F32, kind="ExternalInput").ap()
    muk = nc.dram_tensor("muk", [128, DC], F32, kind="ExternalInput").ap()
    onesd = nc.dram_tensor("onesd", [128, 128], F32, kind="ExternalInput").ap()
    identd = nc.dram_tensor("identd", [128, 128], F32, kind="ExternalInput").ap()
    halo = nc.dram_tensor("halo", [128, 1], F32, kind="ExternalInput").ap()
    y = nc.dram_tensor("y", [128, QT, DC, NT], F32, kind="ExternalOutput").ap()

    from contextlib import ExitStack
    with ExitStack() as top:
        ec = top.enter_context
        Ksb = ec(nc.sbuf_tensor("Ksb", [128, S], BF16))
        Vsb = ec(nc.sbuf_tensor("Vsb", [128, KC, 129], BF16))
        aq_s = ec(nc.sbuf_tensor("aq_s", [128, DC, 128], BF16))
        ak_s = ec(nc.sbuf_tensor("ak_s", [128, DC, 128], BF16))
        av_s = ec(nc.sbuf_tensor("av_s", [128, DC, 128], BF16))
        ao_s = ec(nc.sbuf_tensor("ao_s", [128, D], BF16))
        mur_s = ec(nc.sbuf_tensor("mur_s", [128, DC], F32))
        muk_s = ec(nc.sbuf_tensor("muk_s", [128, DC], F32))
        ones_s = ec(nc.sbuf_tensor("ones_s", [128, 128], BF16))
        id_s = ec(nc.sbuf_tensor("id_s", [128, 128], F32))
        halo_s = ec(nc.sbuf_tensor("halo_s", [128, 1], F32))
        w1k_s = ec(nc.sbuf_tensor("w1k_s", [1, 128], BF16))
        w1v_s = ec(nc.sbuf_tensor("w1v_s", [1, 128], BF16))
        eps_s = ec(nc.sbuf_tensor("eps_s", [1, 1], F32))
        m_sb = ec(nc.sbuf_tensor("m_sb", [1, NT], F32))
        e2_sb = ec(nc.sbuf_tensor("e2_sb", [1, NT], F32))
        var_sb = ec(nc.sbuf_tensor("var_sb", [1, NT], F32))
        r_sb = ec(nc.sbuf_tensor("r_sb", [1, NT], F32))
        mr_sb = ec(nc.sbuf_tensor("mr_sb", [1, NT], BF16))
        rhi_sb = ec(nc.sbuf_tensor("rhi_sb", [1, NT], BF16))
        rlo_sb = ec(nc.sbuf_tensor("rlo_sb", [1, NT], BF16))
        rb_sb = ec(nc.sbuf_tensor("rb_sb", [128, NT], F32))
        sm_sb = ec(nc.sbuf_tensor("sm_sb", [1, 8], F32))
        bbf_sb = ec(nc.sbuf_tensor("bbf_sb", [1, 1], BF16))
        bias_sb = ec(nc.sbuf_tensor("bias_sb", [128, 1], F32))
        rden_sb, dhi_sb, dlo_sb = r_sb, rhi_sb, rlo_sb
        xbf = ec(nc.sbuf_tensor("xbf", [128, 2, DC, NT], BF16))
        x2 = ec(nc.sbuf_tensor("x2", [128, 4, NT], BF16))
        tab = ec(nc.sbuf_tensor("tab", [128, 1, 2, NT], F32))
        A = ec(nc.psum_tensor("A", [128, 1024], F32))
        Bp = ec(nc.psum_tensor("Bp", [128, 1024], F32))
        Dp = ec(nc.psum_tensor("Dp", [128, 512], F32))
        G = ec(nc.psum_tensor("G", [128, 512], F32))
        Pq = ec(nc.psum_tensor("Pq", [128, 4, 256], F32))
        PqF = Pq.ap().rearrange("p a n -> p (a n)")

        sems = {}
        for name in ["s_w", "s_misc", "s_w1", "s_w1e", "s_init",
                     "s_x0", "s_x1", "s_tab", "s_sq", "s_serows", "s_st", "s_var",
                     "s_fin", "s_fin2", "s_rb", "s_rbe", "s_csr", "s_tek",
                     "s_tev", "s_kp", "s_k2s", "s_k2m", "s_kmax", "s_vs",
                     "s_vt", "s_ve2", "s_p1d",
                     "s_xf", "s_xb2", "s_tabq", "s_fwk", "s_fwr", "s_fwv",
                     "s_st2", "s_sq2",
                     "s_srow2", "s_var2", "s_fin_2", "s_rows2", "s_rbmm2",
                     "s_xn", "s_q", "s_qr", "s_q2", "s_qn", "s_qmaxd",
                     "s_qb", "s_bmm", "s_bias", "s_qk", "s_exp", "s_av",
                     "s_den", "s_dbm", "s_o", "s_mix", "s_kpre", "s_rl",
                     "s_kh", "s_rpre", "s_g", "s_facc", "s_out", "s_store",
                     "s_vser", "s_sser", "s_yini", "s_xb3", "s_oev", "s_sqm", "s_sqm2", "s_rbcp"]:
            sems[name] = ec(nc.semaphore(name))
        sm = type("SM", (), sems)
        vst = {"prev": None, "n": 0}
        sst = {"prev": None, "n": 0}

        block = ec(nc.Block())

        with ExitStack() as p1:
            e1 = p1.enter_context
            csr = e1(nc.sbuf_tensor("csr", [128, NT], F32))
            scr = e1(nc.sbuf_tensor("scr", [128, NT], F32))
            p1s = e1(nc.sbuf_tensor("p1s", [128, NT], F32))
            p2s = e1(nc.sbuf_tensor("p2s", [128, NT], F32))
            vt = e1(nc.sbuf_tensor("vt", [128, NT], F32))
            k2 = e1(nc.sbuf_tensor("k2", [128, NT], BF16))

            @block.gpsimd
            def _(g):
                for wi, (dst, srcw) in enumerate([(aq_s, aq), (ak_s, ak), (av_s, av),
                                                  (ao_s, ao), (ones_s, onesd), (id_s, identd)]):
                    if wi >= 2:
                        g.wait_ge(sm.s_w, 16 * wi)
                    g.dma_start(out=dst[:], in_=srcw).then_inc(sm.s_w, 16)
                for t in range(ST):
                    if t >= 2:
                        g.wait_ge(sm.s_st, t - 1)
                        g.wait_ge(sm.s_tek, t - 1)
                        g.wait_ge(sm.s_tev, t - 1)
                        g.wait_ge(sm.s_sq, 8 * (t - 1))
                    g.dma_start(out=xbf[:, t % 2, :, :],
                                in_=xT[:, t, :, :]).then_inc(
                                    sm.s_x0 if t % 2 == 0 else sm.s_x1, 16)
                g.nop().then_inc(sm.s_p1d, 1)

            @block.sync
            def _(s):
                s.dma_start(out=mur_s[:], in_=mur).then_inc(sm.s_misc, 16)
                s.dma_start(out=muk_s[:], in_=muk).then_inc(sm.s_misc, 16)
                s.dma_start(out=halo_s[:], in_=halo).then_inc(sm.s_misc, 16)
                for t in range(ST):
                    if t >= 1:
                        s.wait_ge(sm.s_csr, t)
                    s.dma_start(out=tab[:, 0, 0, :], in_=cs[:, t * NT:(t + 1) * NT]).then_inc(sm.s_tab, 16)
                    s.dma_start(out=tab[:, 0, 1, :], in_=sc[:, t * NT:(t + 1) * NT]).then_inc(sm.s_tab, 16)
                s.nop().then_inc(sm.s_p1d, 1)

            @block.tensor
            def _(t):
                t.wait_ge(sm.s_w, 96)
                for c in range(DC):
                    ins = t.matmul(Dp[0:1, :128], ones_s[:, 0:1], ak_s[:, c, :],
                                   start=(c == 0), stop=(c == DC - 1))
                ins.then_inc(sm.s_w1, 1)
                for c in range(DC):
                    ins = t.matmul(Dp[32:33, :128], ones_s[:, 0:1], av_s[:, c, :],
                                   start=(c == 0), stop=(c == DC - 1))
                ins.then_inc(sm.s_w1, 1)

                def stats_mms(tt):
                    t.wait_ge(sm.s_x0 if tt % 2 == 0 else sm.s_x1, 16 * (tt // 2 + 1))
                    t.wait_ge(sm.s_serows, tt)
                    if tt == 0:
                        t.wait_ge(sm.s_w1e, 1)
                    for c in range(DC):
                        ins = t.matmul(Dp[0:1, :], ones_s[:, 0:1], xbf[:, tt % 2, c, :],
                                       start=(c == 0), stop=(c == DC - 1))
                    ins.then_inc(sm.s_st, 1)
                    for c in range(DC):
                        t.wait_ge(sm.s_sq, 8 * tt + c + 1)
                        t.matmul(Dp[32:33, :], ones_s[:, 0:1], x2[:, c % 4, :],
                                 start=(c == 0), stop=(c == DC - 1)).then_inc(sm.s_sqm, 1)

                def mid_mms(tt):
                    h0 = (tt % 2) * 512
                    t.wait_ge(sm.s_fin2, tt + 1)
                    t.wait_ge(sm.s_rbe, tt)
                    t.matmul(G[:, :], ones_s[0:1, :], rhi_sb[0:1, :], start=True, stop=False)
                    t.matmul(G[:, :], ones_s[0:1, :], rlo_sb[0:1, :], start=False, stop=True).then_inc(sm.s_rb, 1)
                    t.wait_ge(sm.s_kp, max(0, tt - 1))
                    for c in range(DC):
                        t.matmul(A[:, h0:h0 + 512], ak_s[:, c, :], xbf[:, tt % 2, c, :],
                                 start=(c == 0), stop=False)
                    t.matmul(A[:, h0:h0 + 512], w1k_s[0:1, :], mr_sb[0:1, :],
                             start=False, stop=True).then_inc(sm.s_tek, 1)
                    t.wait_ge(sm.s_vs, max(0, tt - 1))
                    for c in range(DC):
                        t.matmul(Bp[:, h0:h0 + 512], av_s[:, c, :], xbf[:, tt % 2, c, :],
                                 start=(c == 0), stop=False)
                    t.matmul(Bp[:, h0:h0 + 512], w1v_s[0:1, :], mr_sb[0:1, :],
                             start=False, stop=True).then_inc(sm.s_tev, 1)

                def tail_mms(tt):
                    t.wait_ge(sm.s_vs, tt + 1)
                    t.wait_ge(sm.s_ve2, tt)
                    for j in range(4):
                        t.transpose(Pq[:, j, 0:128],
                                    vt[:, j * 128:(j + 1) * 128],
                                    id_s[:]).then_inc(sm.s_vt, 1)
                    t.wait_ge(sm.s_k2s, tt + 1)
                    t.wait_ge(sm.s_kmax, tt)
                    t.matmul(Dp[64:65, :], ones_s[:, 0:1], k2[:, :],
                             start=True, stop=True).then_inc(sm.s_k2m, 1)

                for stage in range(ST + 2):
                    if stage < ST:
                        stats_mms(stage)
                    if 1 <= stage <= ST:
                        mid_mms(stage - 1)
                    if stage >= 2:
                        tail_mms(stage - 2)
                t.wait_ge(sm.s_w1, 2)
                t.wait_ge(sm.s_st, ST)
                t.wait_ge(sm.s_sqm, 8 * ST)
                t.wait_ge(sm.s_rb, ST)
                t.wait_ge(sm.s_tek, ST)
                t.wait_ge(sm.s_tev, ST)
                t.wait_ge(sm.s_vt, 4 * ST)
                t.wait_ge(sm.s_k2m, ST)
                t.nop().then_inc(sm.s_p1d, 1)

            @block.scalar
            def _(se0):
                se = _Ser(se0, sm.s_sser, sst)
                se.wait_ge(sm.s_init, 1)
                for stage in range(ST + 2):
                    if stage < ST:
                        tt = stage
                        se.wait_ge(sm.s_x0 if tt % 2 == 0 else sm.s_x1, 16 * (tt // 2 + 1))
                        for c in range(DC):
                            sidx = 8 * tt + c
                            se.wait_ge(sm.s_sqm, max(0, sidx - 3))
                            se.activation(out=x2[:, c % 4, :], in_=xbf[:, tt % 2, c, :],
                                          func=AF.Square, inc=(sm.s_sq, 1, sidx + 1))
                        se.wait_ge(sm.s_st, tt + 1)
                        se.wait_ge(sm.s_sqm, 8 * (tt + 1))
                        se.wait_ge(sm.s_var, tt)
                        se.wait_ge(sm.s_fin2, tt)
                        se.activation(out=m_sb[0:1, :], in_=Dp[0:1, :], func=AF.Copy,
                                      scale=1.0 / D)
                        se.activation(out=e2_sb[0:1, :], in_=Dp[32:33, :], func=AF.Copy,
                                      scale=1.0 / D, inc=(sm.s_serows, 1, tt + 1))
                        se.wait_ge(sm.s_var, tt + 1)
                        se.activation(out=var_sb[0:1, :], in_=var_sb[0:1, :], func=AF.Ln,
                                      bias=eps_s[0:1, :])
                        se.activation(out=r_sb[0:1, :], in_=var_sb[0:1, :], func=AF.Exp,
                                      scale=-0.5, inc=(sm.s_fin, 1, tt + 1))
                    if stage >= 2:
                        tt = stage - 2
                        se.wait_ge(sm.s_kp, tt + 1)
                        se.wait_ge(sm.s_k2m, tt)
                        se.activation(out=k2[:, :], in_=Ksb[:, tt * NT:(tt + 1) * NT],
                                      func=AF.Square, inc=(sm.s_k2s, 1, tt + 1))
                se.nop(inc=(sm.s_p1d, 1, None))

            @block.vector
            def _(v0):
                v = _Ser(v0, sm.s_vser, vst)
                v.memset(sm_sb[0:1, :], 0.0)
                v.memset(Vsb[:, :, 128:129], 1.0)
                v.memset(eps_s[0:1, :], EPS, inc=(sm.s_init, 1, 1))
                v.wait_ge(sm.s_w1, 2)
                v.tensor_scalar(out=w1k_s[0:1, :], in0=Dp[0:1, :128], scalar1=-1.0,
                                scalar2=None, op0=ALU.mult)
                v.tensor_scalar(out=w1v_s[0:1, :], in0=Dp[32:33, :128], scalar1=-1.0,
                                scalar2=None, op0=ALU.mult, inc=(sm.s_w1e, 1, 1))
                for stage in range(ST + 2):
                    if stage < ST:
                        tt = stage
                        v.wait_ge(sm.s_serows, tt + 1)
                        v.wait_ge(sm.s_fin, tt)
                        v.tensor_tensor(out=var_sb[0:1, :], in0=m_sb[0:1, :],
                                        in1=m_sb[0:1, :], op=ALU.mult)
                        v.tensor_tensor(out=var_sb[0:1, :], in0=e2_sb[0:1, :],
                                        in1=var_sb[0:1, :], op=ALU.subtract,
                                        inc=(sm.s_var, 1, tt + 1))
                        v.wait_ge(sm.s_fin, tt + 1)
                        v.wait_ge(sm.s_tek, tt)
                        v.wait_ge(sm.s_tev, tt)
                        v.wait_ge(sm.s_rb, tt)
                        v.tensor_copy(mr_sb[0:1, :], m_sb[0:1, :])
                        v.tensor_copy(rhi_sb[0:1, :], r_sb[0:1, :])
                        v.tensor_tensor(out=rlo_sb[0:1, :], in0=r_sb[0:1, :],
                                        in1=rhi_sb[0:1, :], op=ALU.subtract,
                                        inc=(sm.s_fin2, 1, tt + 1))
                    if 1 <= stage <= ST:
                        tt = stage - 1
                        h0 = (tt % 2) * 512
                        v.wait_ge(sm.s_rb, tt + 1)
                        v.tensor_copy(rb_sb[:, :], G[:, :], inc=(sm.s_rbe, 1, tt + 1))
                        v.wait_ge(sm.s_tab, 32 * (tt + 1))
                        v.tensor_tensor(out=csr[:, :], in0=tab[:, 0, 0, :],
                                        in1=rb_sb[:, :], op=ALU.mult)
                        v.tensor_tensor(out=scr[:, :], in0=tab[:, 0, 1, :],
                                        in1=rb_sb[:, :], op=ALU.mult,
                                        inc=(sm.s_csr, 1, tt + 1))
                        v.wait_ge(sm.s_tek, tt + 1)
                        v.tensor_tensor(out=p1s[0:64, :], in0=csr[0:64, :], in1=A[0:64, h0:h0 + 512], op=ALU.mult)
                        v.tensor_tensor(out=p2s[0:64, :], in0=scr[0:64, :], in1=A[64:128, h0:h0 + 512], op=ALU.mult)
                        v.tensor_tensor(out=Ksb[0:64, tt * NT:(tt + 1) * NT],
                                        in0=p1s[0:64, :], in1=p2s[0:64, :], op=ALU.subtract)
                        v.tensor_tensor(out=p1s[64:128, :], in0=csr[64:128, :], in1=A[0:64, h0:h0 + 512], op=ALU.mult)
                        v.tensor_tensor(out=p2s[64:128, :], in0=scr[64:128, :], in1=A[64:128, h0:h0 + 512], op=ALU.mult)
                        v.tensor_tensor(out=Ksb[64:128, tt * NT:(tt + 1) * NT],
                                        in0=p1s[64:128, :], in1=p2s[64:128, :], op=ALU.add,
                                        inc=(sm.s_kp, 1, tt + 1))
                        v.wait_ge(sm.s_tev, tt + 1)
                        v.wait_ge(sm.s_vt, 4 * tt)
                        v.tensor_tensor(out=vt[:, :], in0=Bp[:, h0:h0 + 512], in1=rb_sb[:, :],
                                        op=ALU.mult, inc=(sm.s_vs, 1, tt + 1))
                    if stage >= 2:
                        tt = stage - 2
                        v.wait_ge(sm.s_vt, 4 * (tt + 1))
                        v.tensor_copy(Vsb[:, tt * 4:(tt + 1) * 4, 0:128], Pq[:, :, 0:128],
                                      inc=(sm.s_ve2, 1, tt + 1))
                        v.wait_ge(sm.s_k2m, tt + 1)
                        v.reduce_max(sm_sb[0:1, 1:2], Dp[64:65, :], axis=AX.X)
                        v.tensor_tensor(out=sm_sb[0:1, 0:1], in0=sm_sb[0:1, 0:1],
                                        in1=sm_sb[0:1, 1:2], op=ALU.max,
                                        inc=(sm.s_kmax, 1, tt + 1))
                v.nop(inc=(sm.s_p1d, 1, None))

        with ExitStack() as p2:
            e2 = p2.enter_context
            fbuf = [e2(nc.sbuf_tensor(f"fbuf{i}", [128, DC, DC, 128], BF16)) for i in range(2)]
            xn = e2(nc.sbuf_tensor("xn", [128, 2, DC, NT + 1], BF16))
            q_sb = e2(nc.sbuf_tensor("q_sb", [128, NT], BF16))
            q2_sb = e2(nc.sbuf_tensor("q2_sb", [128, NT], BF16))
            Pr = e2(nc.sbuf_tensor("Pr", [128, 2, 1024], BF16))
            o_sb = e2(nc.sbuf_tensor("o_sb", [128, NT], BF16))
            oq_sb = e2(nc.sbuf_tensor("oq_sb", [128, NT], F32))
            recip_sb = e2(nc.sbuf_tensor("recip_sb", [128, 4], F32))
            xr_sb = e2(nc.sbuf_tensor("xr_sb", [128, DC, NT], BF16))
            xk_sb = e2(nc.sbuf_tensor("xk_sb", [128, DC, NT], BF16))
            kh_sb = e2(nc.sbuf_tensor("kh_sb", [128, DC, NT], BF16))
            d_sb = e2(nc.sbuf_tensor("d_sb", [128, NT], BF16))
            tt_sb = e2(nc.sbuf_tensor("tt_sb", [128, 1, NT], BF16))
            t1_sb = e2(nc.sbuf_tensor("t1_sb", [128, NT], F32))
            outs = e2(nc.sbuf_tensor("outs", [128, 1, NT], F32))
            g_sb = e2(nc.sbuf_tensor("g_sb", [128, DC, NT], BF16))
            p1q, p2q = t1_sb, rb_sb

            NL = QT + 1

            @block.gpsimd
            def _(g):
                def xload(e):
                    if e >= 2:
                        g.wait_ge(sm.s_st2, e - 1)
                        g.wait_ge(sm.s_sq2, 8 * (e - 1))
                        g.wait_ge(sm.s_xn, e - 1)
                    g.dma_start(out=xbf[:, e % 2, :, :],
                                in_=xT[:, e, :, :]).then_inc(
                                    sm.s_xb2 if e % 2 == 0 else sm.s_xb3, 16)

                g.wait_ge(sm.s_p1d, 5)
                xload(0)
                xload(1)
                xload(2)
                for ell in range(1, NL):
                    if ell + 2 < NL:
                        xload(ell + 2)
                    if ell >= 1:
                        u = ell - 1
                        for i, (w, fsem) in enumerate([(fk, sm.s_fwk), (fr, sm.s_fwr),
                                                       (fv, sm.s_fwv)]):
                            li = 3 * u + i
                            buf = fbuf[li % 2]
                            if i == 0 and u >= 1:
                                g.wait_ge(sm.s_rpre, 8 * u)
                            if i == 1 and u >= 1:
                                g.wait_ge(sm.s_facc, 16 * u)
                            if i == 1:
                                g.wait_ge(sm.s_fwk, 16 * (u + 1))
                            if i == 2:
                                g.wait_ge(sm.s_kpre, 8 * u + 8)
                                g.wait_ge(sm.s_fwr, 16 * (u + 1))
                            g.dma_start(out=buf[:], in_=w).then_inc(fsem, 16)
                    if ell >= 1:
                        u = ell - 1
                        if ell == 1:
                            g.wait_ge(sm.s_yini, 16)
                        for dc in range(DC):
                            g.wait_ge(sm.s_out, 8 * u + dc + 1)
                            if 8 * u + dc >= 1:
                                g.wait_ge(sm.s_store, 16 * (8 * u + dc))
                            g.dma_start(out=y[:, u, dc, :], in_=outs[:, 0, :],
                                        accum_op=ALU.add).then_inc(sm.s_store, 16)
                g.wait_ge(sm.s_store, 16 * 8 * QT)

            @block.sync
            def _(s):
                s.wait_ge(sm.s_p1d, 5)
                s.dma_start(out=y[:, :, :, :], in_=xT[:, 1:QT + 1, :, :]).then_inc(sm.s_yini, 16)
                for ell in range(NL):
                    u = ell - 1
                    if ell >= 1:
                        s.wait_ge(sm.s_qr, u)
                        s.dma_start(out=tab[:, 0, 0, :], in_=csq[:, u * NT:(u + 1) * NT]).then_inc(sm.s_tabq, 16)
                        s.dma_start(out=tab[:, 0, 1, :], in_=scq[:, u * NT:(u + 1) * NT]).then_inc(sm.s_tabq, 16)

            @block.tensor
            def _(t):
                t.wait_ge(sm.s_p1d, 5)

                def stats_x(e):
                    t.wait_ge(sm.s_xb2 if e % 2 == 0 else sm.s_xb3, 16 * (e // 2 + 1))
                    t.wait_ge(sm.s_srow2, e)
                    for c in range(DC):
                        ins = t.matmul(Dp[32:33, :], ones_s[:, 0:1], xbf[:, e % 2, c, :],
                                       start=(c == 0), stop=(c == DC - 1))
                    ins.then_inc(sm.s_st2, 1)

                def stats_x2(e, c):
                    t.wait_ge(sm.s_sq2, 8 * e + c + 1)
                    t.matmul(Dp[64:65, :], ones_s[:, 0:1], x2[:, c % 4, :],
                             start=(c == 0), stop=(c == DC - 1)).then_inc(sm.s_sqm2, 1)

                def stats_g(e):
                    stats_x(e)
                    for c in range(DC):
                        stats_x2(e, c)

                def bcast_g(e):
                    t.wait_ge(sm.s_rows2, e + 1)
                    t.wait_ge(sm.s_xn, max(0, e))
                    t.wait_ge(sm.s_bias, max(0, e - 1))
                    t.wait_ge(sm.s_out, 8 * max(0, e - 2))
                    t.matmul(G[:, :], ones_s[0:1, :], rhi_sb[0:1, :], start=True, stop=False)
                    t.matmul(G[:, :], ones_s[0:1, :], rlo_sb[0:1, :],
                             start=False, stop=True).then_inc(sm.s_rbmm2, 1)

                def mr_g(e):
                    t.wait_ge(sm.s_rbcp, e + 1)
                    t.matmul(G[:, :], ones_s[0:1, :], mr_sb[0:1, :],
                             start=True, stop=True).then_inc(sm.s_rbmm2, 1)

                def qproj_g(uu):
                    t.wait_ge(sm.s_xn, uu + 2)
                    t.wait_ge(sm.s_o, 4 * uu)
                    t.wait_ge(sm.s_den, uu)
                    for c in range(DC):
                        ins = t.matmul(PqF[:, 0:512], aq_s[:, c, :], xn[:, (uu + 1) % 2, c, 1:NT + 1],
                                       start=(c == 0), stop=(c == DC - 1))
                    ins.then_inc(sm.s_q, 1)

                def qnorm_g(uu):
                    t.wait_ge(sm.s_q2, uu + 1)
                    t.wait_ge(sm.s_qmaxd, uu)
                    t.matmul(Dp[0:1, :], ones_s[:, 0:1], q2_sb[:, :],
                             start=True, stop=True).then_inc(sm.s_qn, 1)
                    t.wait_ge(sm.s_qb, uu + 1)
                    t.wait_ge(sm.s_xn, min(uu + 2, NL))
                    t.matmul(G[:, 0:1], ones_s[0:1, :], bbf_sb[0:1, :],
                             start=True, stop=True).then_inc(sm.s_bmm, 1)

                def flash(u, ell):
                    t.wait_ge(sm.s_qr, u + 1)
                    t.wait_ge(sm.s_bias, u + 1)
                    t.wait_ge(sm.s_den, u)
                    t.wait_ge(sm.s_o, 4 * u)
                    t.wait_ge(sm.s_oev, u)
                    if u >= 1:
                        t.wait_ge(sm.s_g, 8 * u)
                        t.wait_ge(sm.s_rl, 8 * u)
                        t.wait_ge(sm.s_out, 8 * u)

                    def av_g(gj):
                        t.wait_ge(sm.s_exp, 64 * u + gj + 1)
                        for h in range(2):
                            kc = 2 * gj + h
                            for j in range(4):
                                t.matmul(Pq[:, j, 0:129],
                                         Pr[:, gj % 2, h * 512 + j * 128:h * 512 + (j + 1) * 128],
                                         Vsb[:, kc, :],
                                         start=(kc == 0 and h == 0 and j % 2 == 0),
                                         stop=(kc == KC - 1 and h == 1 and j % 2 == 1)).then_inc(sm.s_av, 1)

                    for gi in range(NG):
                        pg = A if gi % 2 == 0 else Bp
                        t.wait_ge(sm.s_exp, max(0, 64 * u + gi - 1))
                        for h in range(2):
                            kc = 2 * gi + h
                            t.matmul(pg[:, h * 512:(h + 1) * 512],
                                     Ksb[:, kc * 128:(kc + 1) * 128], q_sb[:, :],
                                     start=True, stop=True).then_inc(sm.s_qk, 1)
                        if ell + 1 < NL:
                            if gi == 2:
                                stats_x(ell + 1)
                            if gi == 7:
                                for c2 in range(4):
                                    stats_x2(ell + 1, c2)
                            if gi == 11:
                                for c2 in range(4, DC):
                                    stats_x2(ell + 1, c2)
                            if gi == 18:
                                bcast_g(ell + 1)
                            if gi == 22:
                                mr_g(ell + 1)
                        if gi >= 1:
                            av_g(gi - 1)
                    av_g(NG - 1)

                stats_g(0)
                bcast_g(0)
                mr_g(0)
                stats_g(1)
                bcast_g(1)
                mr_g(1)
                qproj_g(0)
                qnorm_g(0)

                for ell in range(1, NL):
                    u = ell - 1
                    flash(u, ell)
                    t.wait_ge(sm.s_fwk, 16 * (u + 1))
                    for dc in range(DC):
                        pf = (A if dc % 2 == 0 else Bp)
                        t.wait_ge(sm.s_mix, 8 * u + dc + 1)
                        t.wait_ge(sm.s_rl, max(0, 8 * u + dc - 1))
                        for c in range(DC):
                            ins = t.matmul(pf[:, 0:512], fbuf[(3 * u) % 2][:, c, dc, :],
                                           xk_sb[:, c, :], start=(c == 0), stop=(c == DC - 1))
                        ins.then_inc(sm.s_kpre, 1)
                    t.wait_ge(sm.s_o, 4 * (u + 1))
                    t.wait_ge(sm.s_oev, u)
                    for j in range(4):
                        t.transpose(Pq[:, 2 + j // 2, (j % 2) * 128:(j % 2) * 128 + 128],
                                    oq_sb[:, j * 128:(j + 1) * 128],
                                    id_s[:]).then_inc(sm.s_dbm, 1)
                    if u + 1 <= QT - 1:
                        qproj_g(u + 1)
                    t.wait_ge(sm.s_fwr, 16 * (u + 1))
                    for dc in range(DC):
                        pf = (A if dc % 2 == 0 else Bp)
                        t.wait_ge(sm.s_g, max(0, 8 * u + dc - 1))
                        if dc in (0, 1):
                            t.wait_ge(sm.s_rl, 8 * u + 7 + dc)
                        for c in range(DC):
                            ins = t.matmul(pf[:, 0:512], fbuf[(3 * u + 1) % 2][:, c, dc, :],
                                           xr_sb[:, c, :], start=(c == 0), stop=(c == DC - 1))
                        ins.then_inc(sm.s_rpre, 1)
                    if u + 1 <= QT - 1:
                        qnorm_g(u + 1)
                    t.wait_ge(sm.s_fwv, 16 * (u + 1))
                    t.wait_ge(sm.s_oev, u + 1)
                    t.wait_ge(sm.s_kh, 8 * u + 8)
                    t.wait_ge(sm.s_qr, min(u + 2, QT))
                    t.wait_ge(sm.s_bias, min(u + 2, QT))
                    for dc in range(DC):
                        pacc = (A if dc % 2 == 0 else Bp)
                        pff = (PqF[:, 512:1024] if dc % 2 == 0 else G[:, 0:512])
                        t.wait_ge(sm.s_out, max(0, 8 * u + dc - 1))
                        if dc in (0, 1):
                            t.wait_ge(sm.s_g, 8 * u + 7 + dc)
                        t.matmul(pacc[:, 512:1024], ao_s[:, dc * 128:(dc + 1) * 128],
                                 o_sb[:, :], start=True, stop=True).then_inc(sm.s_facc, 1)
                        for c in range(DC):
                            ins = t.matmul(pff, fbuf[(3 * u + 2) % 2][:, c, dc, :],
                                           kh_sb[:, c, :], start=(c == 0), stop=(c == DC - 1))
                        ins.then_inc(sm.s_facc, 1)

            @block.scalar
            def _(se0):
                se = _Ser(se0, sm.s_sser, sst)
                se.wait_ge(sm.s_p1d, 5)

                def sq_c(e, c):
                    sidx = 8 * e + c
                    if c == 0:
                        se.wait_ge(sm.s_xb2 if e % 2 == 0 else sm.s_xb3, 16 * (e // 2 + 1))
                    se.wait_ge(sm.s_sqm2, max(0, sidx - 3))
                    se.activation(out=x2[:, c % 4, :], in_=xbf[:, e % 2, c, :],
                                  func=AF.Square, inc=(sm.s_sq2, 1, sidx + 1))

                def sq_g(e):
                    for c in range(DC):
                        sq_c(e, c)

                def rows_g_a(e):
                    se.wait_ge(sm.s_st2, e + 1)
                    se.wait_ge(sm.s_sqm2, 8 * (e + 1))
                    se.wait_ge(sm.s_var2, e)
                    se.wait_ge(sm.s_rows2, e)
                    se.activation(out=m_sb[0:1, :], in_=Dp[32:33, :], func=AF.Copy, scale=1.0 / D)
                    se.activation(out=e2_sb[0:1, :], in_=Dp[64:65, :], func=AF.Copy,
                                  scale=1.0 / D, inc=(sm.s_srow2, 1, e + 1))

                def rows_g_b(e):
                    se.wait_ge(sm.s_var2, e + 1)
                    se.wait_ge(sm.s_den, max(0, e - 2))
                    se.activation(out=var_sb[0:1, :], in_=var_sb[0:1, :], func=AF.Ln,
                                  bias=eps_s[0:1, :])
                    se.activation(out=r_sb[0:1, :], in_=var_sb[0:1, :], func=AF.Exp,
                                  scale=-0.5, inc=(sm.s_fin_2, 1, e + 1))

                def rows_g(e):
                    rows_g_a(e)
                    rows_g_b(e)

                def q2_g(uu):
                    se.wait_ge(sm.s_qr, uu + 1)
                    se.wait_ge(sm.s_qn, uu)
                    se.activation(out=q2_sb[:, :], in_=q_sb[:, :], func=AF.Square,
                                  inc=(sm.s_q2, 1, uu + 1))

                sq_g(0)
                rows_g(0)
                sq_g(1)
                rows_g(1)
                q2_g(0)

                for ell in range(1, NL):
                    u = ell - 1
                    se.wait_ge(sm.s_bias, u + 1)
                    for gi in range(NG):
                        pg = A if gi % 2 == 0 else Bp
                        se.wait_ge(sm.s_qk, 128 * u + 2 * gi + 2)
                        se.wait_ge(sm.s_av, max(0, 8 * (64 * u + gi - 1)))
                        se.activation(out=Pr[:, gi % 2, :], in_=pg[:, :], func=AF.Exp,
                                      scale=RSQ, bias=bias_sb[:, :],
                                      inc=(sm.s_exp, 1, 64 * u + gi + 1))
                        if gi == 5 and ell + 1 < NL:
                            for c in range(4):
                                sq_c(ell + 1, c)
                        if gi == 9 and ell + 1 < NL:
                            for c in range(4, DC):
                                sq_c(ell + 1, c)
                        if gi == 13 and ell + 1 < NL:
                            rows_g_a(ell + 1)
                        if gi == 15 and ell + 1 < NL:
                            rows_g_b(ell + 1)
                    se.wait_ge(sm.s_den, u + 1)
                    se.wait_ge(sm.s_dbm, 4 * u)
                    for j in range(4):
                        se.activation(out=oq_sb[:, j * 128:(j + 1) * 128], in_=Pq[:, j, 0:128],
                                      func=AF.Copy, scale=recip_sb[:, j:j + 1],
                                      inc=(sm.s_o, 1, 4 * u + j + 1))
                    for dc in range(DC):
                        pf = (A if dc % 2 == 0 else Bp)
                        se.wait_ge(sm.s_kpre, 8 * u + dc + 1)
                        se.wait_ge(sm.s_kh, 8 * u + dc)
                        se.activation(out=tt_sb[:, 0, :], in_=pf[:, 0:512], func=AF.Relu,
                                      inc=(sm.s_rl, 1, 8 * u + dc + 1))
                    se.wait_ge(sm.s_out, 8 * u)
                    for dc in range(DC):
                        pf = (A if dc % 2 == 0 else Bp)
                        se.wait_ge(sm.s_rpre, 8 * u + dc + 1)
                        se.wait_ge(sm.s_kh, 8 * u + 8)
                        se.activation(out=tt_sb[:, 0, :], in_=pf[:, 0:512], func=AF.Tanh, scale=0.5)
                        se.activation(out=g_sb[:, dc, :], in_=tt_sb[:, 0, :], func=AF.Copy,
                                      scale=0.5, bias=0.5, inc=(sm.s_g, 1, 8 * u + dc + 1))
                    if u + 1 <= QT - 1:
                        q2_g(u + 1)

            @block.vector
            def _(v0):
                v = _Ser(v0, sm.s_vser, vst)
                v.wait_ge(sm.s_p1d, 5)

                def rows_v(e):
                    v.wait_ge(sm.s_srow2, e + 1)
                    v.wait_ge(sm.s_fin_2, e)
                    v.tensor_tensor(out=var_sb[0:1, :], in0=m_sb[0:1, :], in1=m_sb[0:1, :], op=ALU.mult)
                    v.tensor_tensor(out=var_sb[0:1, :], in0=e2_sb[0:1, :], in1=var_sb[0:1, :],
                                    op=ALU.subtract, inc=(sm.s_var2, 1, e + 1))
                    v.wait_ge(sm.s_fin_2, e + 1)
                    v.wait_ge(sm.s_rbmm2, 2 * e)
                    v.tensor_tensor(out=mr_sb[0:1, :], in0=m_sb[0:1, :], in1=r_sb[0:1, :], op=ALU.mult)
                    v.tensor_copy(rhi_sb[0:1, :], r_sb[0:1, :])
                    v.tensor_tensor(out=rlo_sb[0:1, :], in0=r_sb[0:1, :], in1=rhi_sb[0:1, :],
                                    op=ALU.subtract, inc=(sm.s_rows2, 1, e + 1))

                def xn_v(e):
                    v.wait_ge(sm.s_rbmm2, 2 * e + 1)
                    v.wait_ge(sm.s_xb2 if e % 2 == 0 else sm.s_xb3, 16 * (e // 2 + 1))
                    if e >= 2:
                        v.wait_ge(sm.s_mix, 8 * (e - 1))
                    v.tensor_copy(rb_sb[:, :], G[:, :], inc=(sm.s_rbcp, 1, e + 1))
                    v.wait_ge(sm.s_rbmm2, 2 * (e + 1))
                    for dc in range(DC):
                        v.tensor_tensor(out=t1_sb[:, :], in0=xbf[:, e % 2, dc, :],
                                        in1=rb_sb[:, :], op=ALU.mult)
                        v.tensor_tensor(out=xn[:, e % 2, dc, 1:NT + 1], in0=t1_sb[:, :],
                                        in1=G[:, :], op=ALU.subtract)
                        if e >= 1:
                            v.tensor_copy(xn[:, e % 2, dc, 0:1], xn[:, (e - 1) % 2, dc, NT:NT + 1])
                        else:
                            v.memset(xn[:, 0, dc, 0:1], 0.0)
                    if e == 0:
                        v.wait_ge(sm.s_misc, 48)
                        for dc in range(DC):
                            v.tensor_scalar(out=xn[:, 0, dc, NT:NT + 1], in0=xn[:, 0, dc, NT:NT + 1],
                                            scalar1=halo_s[:, :], scalar2=None, op0=ALU.mult)
                    v.nop(inc=(sm.s_xn, 1, e + 1))

                def rope_v(uu):
                    v.wait_ge(sm.s_q, uu + 1)
                    v.wait_ge(sm.s_tabq, 32 * (uu + 1))
                    v.tensor_tensor(out=p1q[0:64, :], in0=tab[0:64, 0, 0, :], in1=PqF[0:64, 0:512], op=ALU.mult)
                    v.tensor_tensor(out=p2q[0:64, :], in0=tab[0:64, 0, 1, :], in1=PqF[64:128, 0:512], op=ALU.mult)
                    v.tensor_tensor(out=q_sb[0:64, :], in0=p1q[0:64, :], in1=p2q[0:64, :], op=ALU.subtract)
                    v.tensor_tensor(out=p1q[64:128, :], in0=tab[64:128, 0, 0, :], in1=PqF[0:64, 0:512], op=ALU.mult)
                    v.tensor_tensor(out=p2q[64:128, :], in0=tab[64:128, 0, 1, :], in1=PqF[64:128, 0:512], op=ALU.mult)
                    v.tensor_tensor(out=q_sb[64:128, :], in0=p1q[64:128, :], in1=p2q[64:128, :],
                                    op=ALU.add, inc=(sm.s_qr, 1, uu + 1))

                def mix_v(uu):
                    e = uu + 1
                    if uu >= 1:
                        v.wait_ge(sm.s_rpre, 8 * uu)
                        v.wait_ge(sm.s_kpre, 8 * uu)
                    for dc in range(DC):
                        v.tensor_tensor(out=d_sb[:, :], in0=xn[:, e % 2, dc, 1:NT + 1],
                                        in1=xn[:, e % 2, dc, 0:NT], op=ALU.subtract)
                        v.scalar_tensor_tensor(out=xr_sb[:, dc, :], in0=d_sb[:, :],
                                               scalar=mur_s[:, dc:dc + 1], in1=xn[:, e % 2, dc, 0:NT],
                                               op0=ALU.mult, op1=ALU.add)
                        v.scalar_tensor_tensor(out=xk_sb[:, dc, :], in0=d_sb[:, :],
                                               scalar=muk_s[:, dc:dc + 1], in1=xn[:, e % 2, dc, 0:NT],
                                               op0=ALU.mult, op1=ALU.add,
                                               inc=(sm.s_mix, 1, 8 * uu + dc + 1))

                def bias_v(uu):
                    v.wait_ge(sm.s_qn, uu + 1)
                    v.reduce_max(sm_sb[0:1, 1:2], Dp[0:1, :], axis=AX.X,
                                 inc=(sm.s_qmaxd, 1, uu + 1))
                    v.tensor_tensor(out=sm_sb[0:1, 2:3], in0=sm_sb[0:1, 1:2], in1=sm_sb[0:1, 0:1], op=ALU.add)
                    v.tensor_scalar(out=sm_sb[0:1, 3:4], in0=sm_sb[0:1, 2:3],
                                    scalar1=-0.5 * RSQ, scalar2=None, op0=ALU.mult)
                    v.tensor_copy(bbf_sb[0:1, :], sm_sb[0:1, 3:4], inc=(sm.s_qb, 1, uu + 1))
                    v.wait_ge(sm.s_bmm, uu + 1)
                    v.tensor_copy(bias_sb[:, :], G[:, 0:1], inc=(sm.s_bias, 1, uu + 1))

                rows_v(0)
                xn_v(0)
                rows_v(1)
                xn_v(1)
                rope_v(0)
                mix_v(0)
                bias_v(0)

                for ell in range(1, NL):
                    u = ell - 1
                    if ell + 1 < NL:
                        rows_v(ell + 1)
                        xn_v(ell + 1)
                    v.wait_ge(sm.s_av, 512 * (u + 1))
                    v.wait_ge(sm.s_o, 4 * u)
                    v.reciprocal(recip_sb[:, 0:4],
                                 Pq[:, :, 128:129].rearrange("p a n -> p (a n)"),
                                 inc=(sm.s_den, 1, u + 1))
                    if u >= 1:
                        v.wait_ge(sm.s_facc, 16 * u)
                    for dc in range(DC):
                        v.wait_ge(sm.s_rl, 8 * u + dc + 1)
                        v.tensor_tensor(out=kh_sb[:, dc, :], in0=tt_sb[:, 0, :],
                                        in1=tt_sb[:, 0, :],
                                        op=ALU.mult, inc=(sm.s_kh, 1, 8 * u + dc + 1))
                    v.wait_ge(sm.s_dbm, 4 * (u + 1))
                    v.tensor_copy(o_sb.ap().rearrange("p (a n) -> p a n", n=128),
                                  Pq[:, 2:4, 0:256].rearrange("p a (b n) -> p (a b) n", n=128),
                                  inc=(sm.s_oev, 1, u + 1))
                    if u + 1 <= QT - 1:
                        rope_v(u + 1)
                    if u + 1 <= QT - 1:
                        bias_v(u + 1)
                        mix_v(u + 1)
                    for dc in range(DC):
                        pacc = (A if dc % 2 == 0 else Bp)
                        pff = (PqF[:, 512:1024] if dc % 2 == 0 else G[:, 0:512])
                        v.wait_ge(sm.s_facc, 16 * u + 2 * dc + 2)
                        v.wait_ge(sm.s_g, 8 * u + dc + 1)
                        if 8 * u + dc >= 1:
                            v.wait_ge(sm.s_store, 16 * (8 * u + dc))
                        v.tensor_tensor(out=t1_sb[:, :], in0=g_sb[:, dc, :], in1=pff, op=ALU.mult)
                        v.tensor_tensor(out=outs[:, 0, :], in0=t1_sb[:, :], in1=pacc[:, 512:1024],
                                        op=ALU.add, inc=(sm.s_out, 1, 8 * u + dc + 1))
    return nc


_NC_CACHE = None


def kernel(L0_attn_x, L0_attn_wq, L0_attn_wk, L0_attn_wv, L0_attn_wo,
           L0_ffn_W_r, L0_ffn_W_k, L0_ffn_W_v, L0_ffn_mu_r, L0_ffn_mu_k):
    global _NC_CACHE
    in_maps = make_in_maps(L0_attn_x, L0_attn_wq, L0_attn_wk, L0_attn_wv,
                           L0_attn_wo, L0_ffn_W_r, L0_ffn_W_k, L0_ffn_W_v,
                           L0_ffn_mu_r, L0_ffn_mu_k)
    if _NC_CACHE is None:
        _NC_CACHE = build_nc()
    res = run_bass_kernel_spmd(_NC_CACHE, in_maps, core_ids=list(range(8)))

    out = np.empty((B, S, D), dtype=np.float32)
    for core in range(8):
        b, qi = core // 4, core % 4
        yv = res.results[core]["y"]
        out[b, qi * Q:(qi + 1) * Q, :] = yv.transpose(1, 3, 2, 0).reshape(Q, D)
    return out


def make_in_maps(L0_attn_x, L0_attn_wq, L0_attn_wk, L0_attn_wv, L0_attn_wo,
                 L0_ffn_W_r, L0_ffn_W_k, L0_ffn_W_v, L0_ffn_mu_r, L0_ffn_mu_k):
    x = np.asarray(L0_attn_x, dtype=np.float32)
    wq = np.asarray(L0_attn_wq, dtype=np.float32)
    wk = np.asarray(L0_attn_wk, dtype=np.float32)
    wv = np.asarray(L0_attn_wv, dtype=np.float32)
    wo = np.asarray(L0_attn_wo, dtype=np.float32)
    Wr = np.asarray(L0_ffn_W_r, dtype=np.float32)
    Wk = np.asarray(L0_ffn_W_k, dtype=np.float32)
    Wv = np.asarray(L0_ffn_W_v, dtype=np.float32)
    mu_r = np.asarray(L0_ffn_mu_r, dtype=np.float32)
    mu_k = np.asarray(L0_ffn_mu_k, dtype=np.float32)

    try:
        import jax
        import jax.numpy as jnp
        with jax.default_device(jax.devices("cpu")[0]):
            inv_j = 1.0 / 10000.0 ** (jnp.arange(0, HD, 2, dtype=jnp.float32) / HD)
            t_j = jnp.arange(S - 1, dtype=jnp.float32)
            ang_j = t_j[:, None] * inv_j[None, :]
            cosr = np.asarray(jnp.cos(ang_j))
            sinr = np.asarray(jnp.sin(ang_j))
    except Exception:
        inv32 = (1.0 / (np.float32(10000.0) **
                        (np.arange(0, HD, 2, dtype=np.float32) / np.float32(HD)))
                 ).astype(np.float32)
        t32 = np.arange(S - 1, dtype=np.float32)
        ang32 = (t32[:, None] * inv32[None, :]).astype(np.float32)
        cosr = np.cos(ang32).astype(np.float32)
        sinr = np.sin(ang32).astype(np.float32)
    cosf = np.empty((64, S), dtype=np.float32)
    sinf = np.empty((64, S), dtype=np.float32)
    cosf[:, 1:] = cosr.T
    sinf[:, 1:] = sinr.T
    cosf[:, 0] = 1.0
    sinf[:, 0] = 0.0
    CS = np.concatenate([cosf, sinf], axis=0)
    SC = np.concatenate([sinf, cosf], axis=0)

    perm_h = np.concatenate([np.arange(0, HD, 2), np.arange(1, HD, 2)])

    def _wT_pch(w):
        return np.ascontiguousarray(w.T.reshape(DC, 128, HD).transpose(1, 0, 2))

    def _fT_pcdk(w):
        return np.ascontiguousarray(w.T.reshape(DC, 128, DC, 128).transpose(1, 0, 2, 3))

    aq_h = _wT_pch(wq[perm_h])
    ak_h = _wT_pch(wk[perm_h])
    av_h = _wT_pch(wv)
    ao_h = np.ascontiguousarray(wo.T)
    fr_h = _fT_pcdk(Wr)
    fk_h = _fT_pcdk(Wk)
    fv_h = _fT_pcdk(Wv)
    mur_h = np.ascontiguousarray(mu_r.reshape(DC, 128).T)
    muk_h = np.ascontiguousarray(mu_k.reshape(DC, 128).T)
    ones_h = np.ones((128, 128), dtype=np.float32)
    ident_h = np.eye(128, dtype=np.float32)

    in_maps = []
    for core in range(8):
        b, qi = core // 4, core % 4
        q0 = qi * Q
        if qi == 0:
            perm = np.concatenate([np.arange(S - NT, S), np.arange(0, Q), np.arange(Q, S - NT)])
            halo_h = np.zeros((128, 1), dtype=np.float32)
        else:
            perm = np.concatenate([np.arange(q0 - NT, q0 + Q), np.arange(0, q0 - NT),
                                   np.arange(q0 + Q, S)])
            halo_h = np.ones((128, 1), dtype=np.float32)
        xT_h = np.ascontiguousarray(
            x[b].T[:, perm].reshape(DC, 128, ST, NT).transpose(1, 2, 0, 3))
        in_maps.append({
            "xT": xT_h,
            "cs": np.ascontiguousarray(CS[:, perm]),
            "sc": np.ascontiguousarray(SC[:, perm]),
            "csq": np.ascontiguousarray(CS[:, q0:q0 + Q]),
            "scq": np.ascontiguousarray(SC[:, q0:q0 + Q]),
            "aq": aq_h, "ak": ak_h, "av": av_h, "ao": ao_h,
            "fr": fr_h, "fk": fk_h, "fv": fv_h,
            "mur": mur_h, "muk": muk_h,
            "onesd": ones_h, "identd": ident_h, "halo": halo_h,
        })
    return in_maps


# version 13
# speedup vs baseline: 1.1066x; 1.0253x over previous
import math
import numpy as np
import ml_dtypes
import concourse.bass as bass
import concourse.mybir as mybir
from concourse.bass_utils import run_bass_kernel_spmd

F32 = mybir.dt.float32
BF16 = mybir.dt.bfloat16
AF = mybir.ActivationFunctionType
ALU = mybir.AluOpType
AX = mybir.AxisListType

B = 2
S = 16384
D = 1024
HD = 128
NT = 512
ST = S // NT
Q = 4096
QT = Q // NT
DC = D // 128
KC = S // 128
NG = KC // 2
RSQ = 1.0 / math.sqrt(128.0)
EPS = 1e-5


class _Ser:

    def __init__(self, eng, sem, st):
        self._e = eng
        self._sem = sem
        self._st = st

    def wait_ge(self, sem, v):
        self._e.wait_ge(sem, v)

    def __getattr__(self, name):
        f = getattr(self._e, name)

        def g(*a, inc=None, edge=True, chain=True, **kw):
            p = self._st["prev"]
            if p is not None and chain:
                self._e.wait_ge(p[0], p[1])
            ins = f(*a, **kw)
            if inc is None:
                self._st["n"] += 1
                ins.then_inc(self._sem, 1)
                self._st["prev"] = (self._sem, self._st["n"])
            else:
                sem, k, after = inc
                ins.then_inc(sem, k)
                if edge and after is not None:
                    self._st["prev"] = (sem, after)
            return ins
        return g


def build_nc():
    nc = bass.Bass("TRN2", target_bir_lowering=False, dynamic_dma_scratch_size=16384)

    xT = nc.dram_tensor("xT", [128, ST, DC, NT], F32, kind="ExternalInput").ap()
    cs = nc.dram_tensor("cs", [128, S], F32, kind="ExternalInput").ap()
    sc = nc.dram_tensor("sc", [128, S], F32, kind="ExternalInput").ap()
    csq = nc.dram_tensor("csq", [128, Q], F32, kind="ExternalInput").ap()
    scq = nc.dram_tensor("scq", [128, Q], F32, kind="ExternalInput").ap()
    aq = nc.dram_tensor("aq", [128, DC, HD], F32, kind="ExternalInput").ap()
    ak = nc.dram_tensor("ak", [128, DC, HD], F32, kind="ExternalInput").ap()
    av = nc.dram_tensor("av", [128, DC, HD], F32, kind="ExternalInput").ap()
    ao = nc.dram_tensor("ao", [HD, D], F32, kind="ExternalInput").ap()
    fr = nc.dram_tensor("fr", [128, DC, DC, 128], F32, kind="ExternalInput").ap()
    fk = nc.dram_tensor("fk", [128, DC, DC, 128], F32, kind="ExternalInput").ap()
    fv = nc.dram_tensor("fv", [128, DC, DC, 128], F32, kind="ExternalInput").ap()
    mur = nc.dram_tensor("mur", [128, DC], F32, kind="ExternalInput").ap()
    muk = nc.dram_tensor("muk", [128, DC], F32, kind="ExternalInput").ap()
    onesd = nc.dram_tensor("onesd", [128, 128], F32, kind="ExternalInput").ap()
    identd = nc.dram_tensor("identd", [128, 128], F32, kind="ExternalInput").ap()
    halo = nc.dram_tensor("halo", [128, 1], F32, kind="ExternalInput").ap()
    y = nc.dram_tensor("y", [128, QT, DC, NT], F32, kind="ExternalOutput").ap()

    from contextlib import ExitStack
    with ExitStack() as top:
        ec = top.enter_context
        Ksb = ec(nc.sbuf_tensor("Ksb", [128, S], BF16))
        Vsb = ec(nc.sbuf_tensor("Vsb", [128, KC, 129], BF16))
        aq_s = ec(nc.sbuf_tensor("aq_s", [128, DC, 128], BF16))
        ak_s = ec(nc.sbuf_tensor("ak_s", [128, DC, 128], BF16))
        av_s = ec(nc.sbuf_tensor("av_s", [128, DC, 128], BF16))
        ao_s = ec(nc.sbuf_tensor("ao_s", [128, D], BF16))
        mur_s = ec(nc.sbuf_tensor("mur_s", [128, DC], F32))
        muk_s = ec(nc.sbuf_tensor("muk_s", [128, DC], F32))
        ones_s = ec(nc.sbuf_tensor("ones_s", [128, 128], BF16))
        id_s = ec(nc.sbuf_tensor("id_s", [128, 128], F32))
        halo_s = ec(nc.sbuf_tensor("halo_s", [128, 1], F32))
        w1k_s = ec(nc.sbuf_tensor("w1k_s", [1, 128], BF16))
        w1v_s = ec(nc.sbuf_tensor("w1v_s", [1, 128], BF16))
        eps_s = ec(nc.sbuf_tensor("eps_s", [1, 1], F32))
        m_sb = ec(nc.sbuf_tensor("m_sb", [1, NT], F32))
        e2_sb = ec(nc.sbuf_tensor("e2_sb", [1, NT], F32))
        var_sb = ec(nc.sbuf_tensor("var_sb", [1, NT], F32))
        r_sb = ec(nc.sbuf_tensor("r_sb", [1, NT], F32))
        mr_sb = ec(nc.sbuf_tensor("mr_sb", [1, NT], BF16))
        rhi_sb = ec(nc.sbuf_tensor("rhi_sb", [1, NT], BF16))
        rlo_sb = ec(nc.sbuf_tensor("rlo_sb", [1, NT], BF16))
        rb_sb = ec(nc.sbuf_tensor("rb_sb", [128, NT], F32))
        sm_sb = ec(nc.sbuf_tensor("sm_sb", [1, 8], F32))
        bbf_sb = ec(nc.sbuf_tensor("bbf_sb", [1, 1], BF16))
        bias_sb = ec(nc.sbuf_tensor("bias_sb", [128, 1], F32))
        rden_sb, dhi_sb, dlo_sb = r_sb, rhi_sb, rlo_sb
        xbf = ec(nc.sbuf_tensor("xbf", [128, 2, DC, NT], BF16))
        x2 = ec(nc.sbuf_tensor("x2", [128, 4, NT], BF16))
        tab = ec(nc.sbuf_tensor("tab", [128, 1, 2, NT], F32))
        A = ec(nc.psum_tensor("A", [128, 1024], F32))
        Bp = ec(nc.psum_tensor("Bp", [128, 1024], F32))
        Dp = ec(nc.psum_tensor("Dp", [128, 512], F32))
        G = ec(nc.psum_tensor("G", [128, 512], F32))
        Pq = ec(nc.psum_tensor("Pq", [128, 4, 256], F32))
        PqF = Pq.ap().rearrange("p a n -> p (a n)")

        sems = {}
        for name in ["s_w", "s_misc", "s_w1", "s_w1e", "s_init",
                     "s_x0", "s_x1", "s_tab", "s_sq", "s_serows", "s_st", "s_var",
                     "s_fin", "s_fin2", "s_rb", "s_rbe", "s_csr", "s_tek",
                     "s_tev", "s_kp", "s_k2s", "s_k2m", "s_kmax", "s_vs",
                     "s_vt", "s_ve2", "s_p1d",
                     "s_xf", "s_xb2", "s_tabq", "s_fwk", "s_fwr", "s_fwv",
                     "s_st2", "s_sq2",
                     "s_srow2", "s_var2", "s_fin_2", "s_rows2", "s_rbmm2",
                     "s_xn", "s_q", "s_qr", "s_q2", "s_qn", "s_qmaxd",
                     "s_qb", "s_bmm", "s_bias", "s_qk", "s_exp", "s_av",
                     "s_den", "s_dbm", "s_o", "s_mix", "s_kpre", "s_rl",
                     "s_kh", "s_rpre", "s_g", "s_facc", "s_out", "s_store",
                     "s_vser", "s_sser", "s_yini", "s_xb3", "s_oev", "s_sqm", "s_sqm2", "s_rbcp"]:
            sems[name] = ec(nc.semaphore(name))
        sm = type("SM", (), sems)
        vst = {"prev": None, "n": 0}
        sst = {"prev": None, "n": 0}

        block = ec(nc.Block())

        with ExitStack() as p1:
            e1 = p1.enter_context
            csr = e1(nc.sbuf_tensor("csr", [128, NT], F32))
            scr = e1(nc.sbuf_tensor("scr", [128, NT], F32))
            p1s = e1(nc.sbuf_tensor("p1s", [128, NT], F32))
            p2s = e1(nc.sbuf_tensor("p2s", [128, NT], F32))
            vt = e1(nc.sbuf_tensor("vt", [128, NT], F32))
            k2 = e1(nc.sbuf_tensor("k2", [128, NT], BF16))

            @block.gpsimd
            def _(g):
                for wi, (dst, srcw) in enumerate([(aq_s, aq), (ak_s, ak), (av_s, av),
                                                  (ao_s, ao), (ones_s, onesd), (id_s, identd)]):
                    if wi >= 2:
                        g.wait_ge(sm.s_w, 16 * wi)
                    g.dma_start(out=dst[:], in_=srcw).then_inc(sm.s_w, 16)
                for t in range(ST):
                    if t >= 2:
                        g.wait_ge(sm.s_st, t - 1)
                        g.wait_ge(sm.s_tek, t - 1)
                        g.wait_ge(sm.s_tev, t - 1)
                        g.wait_ge(sm.s_sq, 8 * (t - 1))
                    g.dma_start(out=xbf[:, t % 2, :, :],
                                in_=xT[:, t, :, :]).then_inc(
                                    sm.s_x0 if t % 2 == 0 else sm.s_x1, 16)
                g.nop().then_inc(sm.s_p1d, 1)

            @block.sync
            def _(s):
                s.dma_start(out=mur_s[:], in_=mur).then_inc(sm.s_misc, 16)
                s.dma_start(out=muk_s[:], in_=muk).then_inc(sm.s_misc, 16)
                s.dma_start(out=halo_s[:], in_=halo).then_inc(sm.s_misc, 16)
                for t in range(ST):
                    if t >= 1:
                        s.wait_ge(sm.s_csr, t)
                    s.dma_start(out=tab[:, 0, 0, :], in_=cs[:, t * NT:(t + 1) * NT]).then_inc(sm.s_tab, 16)
                    s.dma_start(out=tab[:, 0, 1, :], in_=sc[:, t * NT:(t + 1) * NT]).then_inc(sm.s_tab, 16)
                s.nop().then_inc(sm.s_p1d, 1)

            @block.tensor
            def _(t):
                t.wait_ge(sm.s_w, 96)
                for c in range(DC):
                    ins = t.matmul(Dp[0:1, :128], ones_s[:, 0:1], ak_s[:, c, :],
                                   start=(c == 0), stop=(c == DC - 1))
                ins.then_inc(sm.s_w1, 1)
                for c in range(DC):
                    ins = t.matmul(Dp[32:33, :128], ones_s[:, 0:1], av_s[:, c, :],
                                   start=(c == 0), stop=(c == DC - 1))
                ins.then_inc(sm.s_w1, 1)

                def stats_mms(tt):
                    t.wait_ge(sm.s_x0 if tt % 2 == 0 else sm.s_x1, 16 * (tt // 2 + 1))
                    t.wait_ge(sm.s_serows, tt)
                    if tt == 0:
                        t.wait_ge(sm.s_w1e, 1)
                    for c in range(DC):
                        ins = t.matmul(Dp[0:1, :], ones_s[:, 0:1], xbf[:, tt % 2, c, :],
                                       start=(c == 0), stop=(c == DC - 1))
                    ins.then_inc(sm.s_st, 1)
                    for c in range(DC):
                        t.wait_ge(sm.s_sq, 8 * tt + c + 1)
                        t.matmul(Dp[32:33, :], ones_s[:, 0:1], x2[:, c % 4, :],
                                 start=(c == 0), stop=(c == DC - 1)).then_inc(sm.s_sqm, 1)

                def mid_mms(tt):
                    h0 = (tt % 2) * 512
                    t.wait_ge(sm.s_fin2, tt + 1)
                    t.wait_ge(sm.s_rbe, tt)
                    t.matmul(G[:, :], ones_s[0:1, :], rhi_sb[0:1, :], start=True, stop=False)
                    t.matmul(G[:, :], ones_s[0:1, :], rlo_sb[0:1, :], start=False, stop=True).then_inc(sm.s_rb, 1)
                    t.wait_ge(sm.s_kp, max(0, tt - 1))
                    for c in range(DC):
                        t.matmul(A[:, h0:h0 + 512], ak_s[:, c, :], xbf[:, tt % 2, c, :],
                                 start=(c == 0), stop=False)
                    t.matmul(A[:, h0:h0 + 512], w1k_s[0:1, :], mr_sb[0:1, :],
                             start=False, stop=True).then_inc(sm.s_tek, 1)
                    t.wait_ge(sm.s_vs, max(0, tt - 1))
                    for c in range(DC):
                        t.matmul(Bp[:, h0:h0 + 512], av_s[:, c, :], xbf[:, tt % 2, c, :],
                                 start=(c == 0), stop=False)
                    t.matmul(Bp[:, h0:h0 + 512], w1v_s[0:1, :], mr_sb[0:1, :],
                             start=False, stop=True).then_inc(sm.s_tev, 1)

                def tail_mms(tt):
                    t.wait_ge(sm.s_vs, tt + 1)
                    t.wait_ge(sm.s_ve2, tt)
                    for j in range(4):
                        t.transpose(Pq[:, j, 0:128],
                                    vt[:, j * 128:(j + 1) * 128],
                                    id_s[:]).then_inc(sm.s_vt, 1)
                    t.wait_ge(sm.s_k2s, tt + 1)
                    t.wait_ge(sm.s_kmax, tt)
                    t.matmul(Dp[64:65, :], ones_s[:, 0:1], k2[:, :],
                             start=True, stop=True).then_inc(sm.s_k2m, 1)

                for stage in range(ST + 2):
                    if stage < ST:
                        stats_mms(stage)
                    if 1 <= stage <= ST:
                        mid_mms(stage - 1)
                    if stage >= 2:
                        tail_mms(stage - 2)
                t.wait_ge(sm.s_w1, 2)
                t.wait_ge(sm.s_st, ST)
                t.wait_ge(sm.s_sqm, 8 * ST)
                t.wait_ge(sm.s_rb, ST)
                t.wait_ge(sm.s_tek, ST)
                t.wait_ge(sm.s_tev, ST)
                t.wait_ge(sm.s_vt, 4 * ST)
                t.wait_ge(sm.s_k2m, ST)
                t.nop().then_inc(sm.s_p1d, 1)

            @block.scalar
            def _(se0):
                se = _Ser(se0, sm.s_sser, sst)
                se.wait_ge(sm.s_init, 1)
                for stage in range(ST + 2):
                    if stage < ST:
                        tt = stage
                        se.wait_ge(sm.s_x0 if tt % 2 == 0 else sm.s_x1, 16 * (tt // 2 + 1))
                        for c in range(DC):
                            sidx = 8 * tt + c
                            se.wait_ge(sm.s_sqm, max(0, sidx - 3))
                            se.activation(out=x2[:, c % 4, :], in_=xbf[:, tt % 2, c, :],
                                          func=AF.Square, inc=(sm.s_sq, 1, sidx + 1), chain=False)
                        se.wait_ge(sm.s_st, tt + 1)
                        se.wait_ge(sm.s_sqm, 8 * (tt + 1))
                        se.wait_ge(sm.s_var, tt)
                        se.wait_ge(sm.s_fin2, tt)
                        se.activation(out=m_sb[0:1, :], in_=Dp[0:1, :], func=AF.Copy,
                                      scale=1.0 / D)
                        se.activation(out=e2_sb[0:1, :], in_=Dp[32:33, :], func=AF.Copy,
                                      scale=1.0 / D, inc=(sm.s_serows, 1, tt + 1))
                        se.wait_ge(sm.s_var, tt + 1)
                        se.activation(out=var_sb[0:1, :], in_=var_sb[0:1, :], func=AF.Ln,
                                      bias=eps_s[0:1, :])
                        se.activation(out=r_sb[0:1, :], in_=var_sb[0:1, :], func=AF.Exp,
                                      scale=-0.5, inc=(sm.s_fin, 1, tt + 1))
                    if stage >= 2:
                        tt = stage - 2
                        se.wait_ge(sm.s_kp, tt + 1)
                        se.wait_ge(sm.s_k2m, tt)
                        se.activation(out=k2[:, :], in_=Ksb[:, tt * NT:(tt + 1) * NT],
                                      func=AF.Square, inc=(sm.s_k2s, 1, tt + 1))
                se.nop(inc=(sm.s_p1d, 1, None))

            @block.vector
            def _(v0):
                v = _Ser(v0, sm.s_vser, vst)
                v.memset(sm_sb[0:1, :], 0.0)
                v.memset(Vsb[:, :, 128:129], 1.0)
                v.memset(eps_s[0:1, :], EPS, inc=(sm.s_init, 1, 1))
                v.wait_ge(sm.s_w1, 2)
                v.tensor_scalar(out=w1k_s[0:1, :], in0=Dp[0:1, :128], scalar1=-1.0,
                                scalar2=None, op0=ALU.mult)
                v.tensor_scalar(out=w1v_s[0:1, :], in0=Dp[32:33, :128], scalar1=-1.0,
                                scalar2=None, op0=ALU.mult, inc=(sm.s_w1e, 1, 1))
                for stage in range(ST + 2):
                    if stage < ST:
                        tt = stage
                        v.wait_ge(sm.s_serows, tt + 1)
                        v.wait_ge(sm.s_fin, tt)
                        v.tensor_tensor(out=var_sb[0:1, :], in0=m_sb[0:1, :],
                                        in1=m_sb[0:1, :], op=ALU.mult)
                        v.tensor_tensor(out=var_sb[0:1, :], in0=e2_sb[0:1, :],
                                        in1=var_sb[0:1, :], op=ALU.subtract,
                                        inc=(sm.s_var, 1, tt + 1))
                        v.wait_ge(sm.s_fin, tt + 1)
                        v.wait_ge(sm.s_tek, tt)
                        v.wait_ge(sm.s_tev, tt)
                        v.wait_ge(sm.s_rb, tt)
                        v.tensor_copy(mr_sb[0:1, :], m_sb[0:1, :])
                        v.tensor_copy(rhi_sb[0:1, :], r_sb[0:1, :])
                        v.tensor_tensor(out=rlo_sb[0:1, :], in0=r_sb[0:1, :],
                                        in1=rhi_sb[0:1, :], op=ALU.subtract,
                                        inc=(sm.s_fin2, 1, tt + 1))
                    if 1 <= stage <= ST:
                        tt = stage - 1
                        h0 = (tt % 2) * 512
                        v.wait_ge(sm.s_rb, tt + 1)
                        v.tensor_copy(rb_sb[:, :], G[:, :], inc=(sm.s_rbe, 1, tt + 1))
                        v.wait_ge(sm.s_tab, 32 * (tt + 1))
                        v.tensor_tensor(out=csr[:, :], in0=tab[:, 0, 0, :],
                                        in1=rb_sb[:, :], op=ALU.mult)
                        v.tensor_tensor(out=scr[:, :], in0=tab[:, 0, 1, :],
                                        in1=rb_sb[:, :], op=ALU.mult,
                                        inc=(sm.s_csr, 1, tt + 1))
                        v.wait_ge(sm.s_tek, tt + 1)
                        v.tensor_tensor(out=p1s[0:64, :], in0=csr[0:64, :], in1=A[0:64, h0:h0 + 512], op=ALU.mult)
                        v.tensor_tensor(out=p2s[0:64, :], in0=scr[0:64, :], in1=A[64:128, h0:h0 + 512], op=ALU.mult)
                        v.tensor_tensor(out=Ksb[0:64, tt * NT:(tt + 1) * NT],
                                        in0=p1s[0:64, :], in1=p2s[0:64, :], op=ALU.subtract)
                        v.tensor_tensor(out=p1s[64:128, :], in0=csr[64:128, :], in1=A[0:64, h0:h0 + 512], op=ALU.mult)
                        v.tensor_tensor(out=p2s[64:128, :], in0=scr[64:128, :], in1=A[64:128, h0:h0 + 512], op=ALU.mult)
                        v.tensor_tensor(out=Ksb[64:128, tt * NT:(tt + 1) * NT],
                                        in0=p1s[64:128, :], in1=p2s[64:128, :], op=ALU.add,
                                        inc=(sm.s_kp, 1, tt + 1))
                        v.wait_ge(sm.s_tev, tt + 1)
                        v.wait_ge(sm.s_vt, 4 * tt)
                        v.tensor_tensor(out=vt[:, :], in0=Bp[:, h0:h0 + 512], in1=rb_sb[:, :],
                                        op=ALU.mult, inc=(sm.s_vs, 1, tt + 1))
                    if stage >= 2:
                        tt = stage - 2
                        v.wait_ge(sm.s_vt, 4 * (tt + 1))
                        v.tensor_copy(Vsb[:, tt * 4:(tt + 1) * 4, 0:128], Pq[:, :, 0:128],
                                      inc=(sm.s_ve2, 1, tt + 1))
                        v.wait_ge(sm.s_k2m, tt + 1)
                        v.reduce_max(sm_sb[0:1, 1:2], Dp[64:65, :], axis=AX.X)
                        v.tensor_tensor(out=sm_sb[0:1, 0:1], in0=sm_sb[0:1, 0:1],
                                        in1=sm_sb[0:1, 1:2], op=ALU.max,
                                        inc=(sm.s_kmax, 1, tt + 1))
                v.nop(inc=(sm.s_p1d, 1, None))

        with ExitStack() as p2:
            e2 = p2.enter_context
            fbuf = [e2(nc.sbuf_tensor(f"fbuf{i}", [128, DC, DC, 128], BF16)) for i in range(2)]
            xn = e2(nc.sbuf_tensor("xn", [128, 2, DC, NT + 1], BF16))
            q_sb = e2(nc.sbuf_tensor("q_sb", [128, NT], BF16))
            q2_sb = e2(nc.sbuf_tensor("q2_sb", [128, NT], BF16))
            Pr = e2(nc.sbuf_tensor("Pr", [128, 2, 1024], BF16))
            o_sb = e2(nc.sbuf_tensor("o_sb", [128, NT], BF16))
            oq_sb = e2(nc.sbuf_tensor("oq_sb", [128, NT], F32))
            recip_sb = e2(nc.sbuf_tensor("recip_sb", [128, 4], F32))
            xr_sb = e2(nc.sbuf_tensor("xr_sb", [128, DC, NT], BF16))
            xk_sb = e2(nc.sbuf_tensor("xk_sb", [128, DC, NT], BF16))
            kh_sb = e2(nc.sbuf_tensor("kh_sb", [128, DC, NT], BF16))
            d_sb = e2(nc.sbuf_tensor("d_sb", [128, NT], BF16))
            tt_sb = e2(nc.sbuf_tensor("tt_sb", [128, 1, NT], BF16))
            t1_sb = e2(nc.sbuf_tensor("t1_sb", [128, NT], F32))
            outs = e2(nc.sbuf_tensor("outs", [128, 1, NT], F32))
            g_sb = e2(nc.sbuf_tensor("g_sb", [128, DC, NT], BF16))
            p1q, p2q = t1_sb, rb_sb

            NL = QT + 1

            @block.gpsimd
            def _(g):
                def xload(e):
                    if e >= 2:
                        g.wait_ge(sm.s_st2, e - 1)
                        g.wait_ge(sm.s_sq2, 8 * (e - 1))
                        g.wait_ge(sm.s_xn, e - 1)
                    g.dma_start(out=xbf[:, e % 2, :, :],
                                in_=xT[:, e, :, :]).then_inc(
                                    sm.s_xb2 if e % 2 == 0 else sm.s_xb3, 16)

                g.wait_ge(sm.s_p1d, 5)
                xload(0)
                xload(1)
                xload(2)
                for ell in range(1, NL):
                    if ell + 2 < NL:
                        xload(ell + 2)
                    if ell >= 1:
                        u = ell - 1
                        for i, (w, fsem) in enumerate([(fk, sm.s_fwk), (fr, sm.s_fwr),
                                                       (fv, sm.s_fwv)]):
                            li = 3 * u + i
                            buf = fbuf[li % 2]
                            if i == 0 and u >= 1:
                                g.wait_ge(sm.s_rpre, 8 * u)
                            if i == 1 and u >= 1:
                                g.wait_ge(sm.s_facc, 16 * u)
                            if i == 1:
                                g.wait_ge(sm.s_fwk, 16 * (u + 1))
                            if i == 2:
                                g.wait_ge(sm.s_kpre, 8 * u + 8)
                                g.wait_ge(sm.s_fwr, 16 * (u + 1))
                            g.dma_start(out=buf[:], in_=w).then_inc(fsem, 16)
                    if ell >= 1:
                        u = ell - 1
                        if ell == 1:
                            g.wait_ge(sm.s_yini, 16)
                        for dc in range(DC):
                            g.wait_ge(sm.s_out, 8 * u + dc + 1)
                            if 8 * u + dc >= 1:
                                g.wait_ge(sm.s_store, 16 * (8 * u + dc))
                            g.dma_start(out=y[:, u, dc, :], in_=outs[:, 0, :],
                                        accum_op=ALU.add).then_inc(sm.s_store, 16)
                g.wait_ge(sm.s_store, 16 * 8 * QT)

            @block.sync
            def _(s):
                s.wait_ge(sm.s_p1d, 5)
                s.dma_start(out=y[:, :, :, :], in_=xT[:, 1:QT + 1, :, :]).then_inc(sm.s_yini, 16)
                for ell in range(NL):
                    u = ell - 1
                    if ell >= 1:
                        s.wait_ge(sm.s_qr, u)
                        s.dma_start(out=tab[:, 0, 0, :], in_=csq[:, u * NT:(u + 1) * NT]).then_inc(sm.s_tabq, 16)
                        s.dma_start(out=tab[:, 0, 1, :], in_=scq[:, u * NT:(u + 1) * NT]).then_inc(sm.s_tabq, 16)

            @block.tensor
            def _(t):
                t.wait_ge(sm.s_p1d, 5)

                def stats_x(e):
                    t.wait_ge(sm.s_xb2 if e % 2 == 0 else sm.s_xb3, 16 * (e // 2 + 1))
                    t.wait_ge(sm.s_srow2, e)
                    for c in range(DC):
                        ins = t.matmul(Dp[32:33, :], ones_s[:, 0:1], xbf[:, e % 2, c, :],
                                       start=(c == 0), stop=(c == DC - 1))
                    ins.then_inc(sm.s_st2, 1)

                def stats_x2(e, c):
                    t.wait_ge(sm.s_sq2, 8 * e + c + 1)
                    t.matmul(Dp[64:65, :], ones_s[:, 0:1], x2[:, c % 4, :],
                             start=(c == 0), stop=(c == DC - 1)).then_inc(sm.s_sqm2, 1)

                def stats_g(e):
                    stats_x(e)
                    for c in range(DC):
                        stats_x2(e, c)

                def bcast_g(e):
                    t.wait_ge(sm.s_rows2, e + 1)
                    t.wait_ge(sm.s_xn, max(0, e))
                    t.wait_ge(sm.s_bias, max(0, e - 1))
                    t.wait_ge(sm.s_out, 8 * max(0, e - 2))
                    t.matmul(G[:, :], ones_s[0:1, :], rhi_sb[0:1, :], start=True, stop=False)
                    t.matmul(G[:, :], ones_s[0:1, :], rlo_sb[0:1, :],
                             start=False, stop=True).then_inc(sm.s_rbmm2, 1)

                def mr_g(e):
                    t.wait_ge(sm.s_rbcp, e + 1)
                    t.matmul(G[:, :], ones_s[0:1, :], mr_sb[0:1, :],
                             start=True, stop=True).then_inc(sm.s_rbmm2, 1)

                def qproj_g(uu):
                    t.wait_ge(sm.s_xn, uu + 2)
                    t.wait_ge(sm.s_o, 4 * uu)
                    t.wait_ge(sm.s_den, uu)
                    for c in range(DC):
                        ins = t.matmul(PqF[:, 0:512], aq_s[:, c, :], xn[:, (uu + 1) % 2, c, 1:NT + 1],
                                       start=(c == 0), stop=(c == DC - 1))
                    ins.then_inc(sm.s_q, 1)

                def qnorm_g(uu):
                    t.wait_ge(sm.s_q2, uu + 1)
                    t.wait_ge(sm.s_qmaxd, uu)
                    t.matmul(Dp[0:1, :], ones_s[:, 0:1], q2_sb[:, :],
                             start=True, stop=True).then_inc(sm.s_qn, 1)
                    t.wait_ge(sm.s_qb, uu + 1)
                    t.wait_ge(sm.s_xn, min(uu + 2, NL))
                    t.matmul(G[:, 0:1], ones_s[0:1, :], bbf_sb[0:1, :],
                             start=True, stop=True).then_inc(sm.s_bmm, 1)

                def flash(u, ell):
                    t.wait_ge(sm.s_qr, u + 1)
                    t.wait_ge(sm.s_bias, u + 1)
                    t.wait_ge(sm.s_den, u)
                    t.wait_ge(sm.s_o, 4 * u)
                    t.wait_ge(sm.s_oev, u)
                    if u >= 1:
                        t.wait_ge(sm.s_g, 8 * u)
                        t.wait_ge(sm.s_rl, 8 * u)
                        t.wait_ge(sm.s_out, 8 * u)

                    def av_g(gj):
                        t.wait_ge(sm.s_exp, 64 * u + gj + 1)
                        for h in range(2):
                            kc = 2 * gj + h
                            for j in range(4):
                                t.matmul(Pq[:, j, 0:129],
                                         Pr[:, gj % 2, h * 512 + j * 128:h * 512 + (j + 1) * 128],
                                         Vsb[:, kc, :],
                                         start=(kc == 0 and h == 0 and j % 2 == 0),
                                         stop=(kc == KC - 1 and h == 1 and j % 2 == 1)).then_inc(sm.s_av, 1)

                    for gi in range(NG):
                        pg = A if gi % 2 == 0 else Bp
                        t.wait_ge(sm.s_exp, max(0, 64 * u + gi - 1))
                        for h in range(2):
                            kc = 2 * gi + h
                            t.matmul(pg[:, h * 512:(h + 1) * 512],
                                     Ksb[:, kc * 128:(kc + 1) * 128], q_sb[:, :],
                                     start=True, stop=True).then_inc(sm.s_qk, 1)
                        if ell + 1 < NL:
                            if gi == 2:
                                stats_x(ell + 1)
                            if gi == 7:
                                for c2 in range(4):
                                    stats_x2(ell + 1, c2)
                            if gi == 11:
                                for c2 in range(4, DC):
                                    stats_x2(ell + 1, c2)
                            if gi == 18:
                                bcast_g(ell + 1)
                            if gi == 22:
                                mr_g(ell + 1)
                        if gi >= 1:
                            av_g(gi - 1)
                    av_g(NG - 1)

                stats_g(0)
                bcast_g(0)
                mr_g(0)
                stats_g(1)
                bcast_g(1)
                mr_g(1)
                qproj_g(0)
                qnorm_g(0)

                for ell in range(1, NL):
                    u = ell - 1
                    flash(u, ell)
                    t.wait_ge(sm.s_fwk, 16 * (u + 1))
                    for dc in range(DC):
                        pf = (A if dc % 2 == 0 else Bp)
                        t.wait_ge(sm.s_mix, 8 * u + dc + 1)
                        t.wait_ge(sm.s_rl, max(0, 8 * u + dc - 1))
                        for c in range(DC):
                            ins = t.matmul(pf[:, 0:512], fbuf[(3 * u) % 2][:, c, dc, :],
                                           xk_sb[:, c, :], start=(c == 0), stop=(c == DC - 1))
                        ins.then_inc(sm.s_kpre, 1)
                    t.wait_ge(sm.s_o, 4 * (u + 1))
                    t.wait_ge(sm.s_oev, u)
                    for j in range(4):
                        t.transpose(Pq[:, 2 + j // 2, (j % 2) * 128:(j % 2) * 128 + 128],
                                    oq_sb[:, j * 128:(j + 1) * 128],
                                    id_s[:]).then_inc(sm.s_dbm, 1)
                    if u + 1 <= QT - 1:
                        qproj_g(u + 1)
                    t.wait_ge(sm.s_fwr, 16 * (u + 1))
                    for dc in range(DC):
                        pf = (A if dc % 2 == 0 else Bp)
                        t.wait_ge(sm.s_g, max(0, 8 * u + dc - 1))
                        if dc in (0, 1):
                            t.wait_ge(sm.s_rl, 8 * u + 7 + dc)
                        for c in range(DC):
                            ins = t.matmul(pf[:, 0:512], fbuf[(3 * u + 1) % 2][:, c, dc, :],
                                           xr_sb[:, c, :], start=(c == 0), stop=(c == DC - 1))
                        ins.then_inc(sm.s_rpre, 1)
                    if u + 1 <= QT - 1:
                        qnorm_g(u + 1)
                    t.wait_ge(sm.s_fwv, 16 * (u + 1))
                    t.wait_ge(sm.s_oev, u + 1)
                    t.wait_ge(sm.s_kh, 8 * u + 8)
                    t.wait_ge(sm.s_qr, min(u + 2, QT))
                    t.wait_ge(sm.s_bias, min(u + 2, QT))
                    for dc in range(DC):
                        pacc = (A if dc % 2 == 0 else Bp)
                        pff = (PqF[:, 512:1024] if dc % 2 == 0 else G[:, 0:512])
                        t.wait_ge(sm.s_out, max(0, 8 * u + dc - 1))
                        if dc in (0, 1):
                            t.wait_ge(sm.s_g, 8 * u + 7 + dc)
                        t.matmul(pacc[:, 512:1024], ao_s[:, dc * 128:(dc + 1) * 128],
                                 o_sb[:, :], start=True, stop=True).then_inc(sm.s_facc, 1)
                        for c in range(DC):
                            ins = t.matmul(pff, fbuf[(3 * u + 2) % 2][:, c, dc, :],
                                           kh_sb[:, c, :], start=(c == 0), stop=(c == DC - 1))
                        ins.then_inc(sm.s_facc, 1)

            @block.scalar
            def _(se0):
                se = _Ser(se0, sm.s_sser, sst)
                se.wait_ge(sm.s_p1d, 5)

                def sq_c(e, c):
                    sidx = 8 * e + c
                    if c == 0:
                        se.wait_ge(sm.s_xb2 if e % 2 == 0 else sm.s_xb3, 16 * (e // 2 + 1))
                    se.wait_ge(sm.s_sqm2, max(0, sidx - 3))
                    se.activation(out=x2[:, c % 4, :], in_=xbf[:, e % 2, c, :],
                                  func=AF.Square, inc=(sm.s_sq2, 1, sidx + 1), chain=False)

                def sq_g(e):
                    for c in range(DC):
                        sq_c(e, c)

                def rows_g_a(e):
                    se.wait_ge(sm.s_st2, e + 1)
                    se.wait_ge(sm.s_sqm2, 8 * (e + 1))
                    se.wait_ge(sm.s_var2, e)
                    se.wait_ge(sm.s_rows2, e)
                    se.activation(out=m_sb[0:1, :], in_=Dp[32:33, :], func=AF.Copy, scale=1.0 / D)
                    se.activation(out=e2_sb[0:1, :], in_=Dp[64:65, :], func=AF.Copy,
                                  scale=1.0 / D, inc=(sm.s_srow2, 1, e + 1))

                def rows_g_b(e):
                    se.wait_ge(sm.s_var2, e + 1)
                    se.wait_ge(sm.s_den, max(0, e - 2))
                    se.activation(out=var_sb[0:1, :], in_=var_sb[0:1, :], func=AF.Ln,
                                  bias=eps_s[0:1, :])
                    se.activation(out=r_sb[0:1, :], in_=var_sb[0:1, :], func=AF.Exp,
                                  scale=-0.5, inc=(sm.s_fin_2, 1, e + 1))

                def rows_g(e):
                    rows_g_a(e)
                    rows_g_b(e)

                def q2_g(uu):
                    se.wait_ge(sm.s_qr, uu + 1)
                    se.wait_ge(sm.s_qn, uu)
                    se.activation(out=q2_sb[:, :], in_=q_sb[:, :], func=AF.Square,
                                  inc=(sm.s_q2, 1, uu + 1))

                sq_g(0)
                rows_g(0)
                sq_g(1)
                rows_g(1)
                q2_g(0)

                for ell in range(1, NL):
                    u = ell - 1
                    se.wait_ge(sm.s_bias, u + 1)
                    for gi in range(NG):
                        pg = A if gi % 2 == 0 else Bp
                        se.wait_ge(sm.s_qk, 128 * u + 2 * gi + 2)
                        se.wait_ge(sm.s_av, max(0, 8 * (64 * u + gi - 1)))
                        se.activation(out=Pr[:, gi % 2, :], in_=pg[:, :], func=AF.Exp,
                                      scale=RSQ, bias=bias_sb[:, :],
                                      inc=(sm.s_exp, 1, 64 * u + gi + 1), chain=False)
                        if gi == 5 and ell + 1 < NL:
                            for c in range(4):
                                sq_c(ell + 1, c)
                        if gi == 9 and ell + 1 < NL:
                            for c in range(4, DC):
                                sq_c(ell + 1, c)
                        if gi == 13 and ell + 1 < NL:
                            rows_g_a(ell + 1)
                        if gi == 15 and ell + 1 < NL:
                            rows_g_b(ell + 1)
                    se.wait_ge(sm.s_den, u + 1)
                    se.wait_ge(sm.s_dbm, 4 * u)
                    for j in range(4):
                        se.activation(out=oq_sb[:, j * 128:(j + 1) * 128], in_=Pq[:, j, 0:128],
                                      func=AF.Copy, scale=recip_sb[:, j:j + 1],
                                      inc=(sm.s_o, 1, 4 * u + j + 1))
                    for dc in range(DC):
                        pf = (A if dc % 2 == 0 else Bp)
                        se.wait_ge(sm.s_kpre, 8 * u + dc + 1)
                        se.wait_ge(sm.s_kh, 8 * u + dc)
                        se.activation(out=tt_sb[:, 0, :], in_=pf[:, 0:512], func=AF.Relu,
                                      inc=(sm.s_rl, 1, 8 * u + dc + 1))
                    se.wait_ge(sm.s_out, 8 * u)
                    for dc in range(DC):
                        pf = (A if dc % 2 == 0 else Bp)
                        se.wait_ge(sm.s_rpre, 8 * u + dc + 1)
                        se.wait_ge(sm.s_kh, 8 * u + 8)
                        se.activation(out=tt_sb[:, 0, :], in_=pf[:, 0:512], func=AF.Tanh, scale=0.5)
                        se.activation(out=g_sb[:, dc, :], in_=tt_sb[:, 0, :], func=AF.Copy,
                                      scale=0.5, bias=0.5, inc=(sm.s_g, 1, 8 * u + dc + 1))
                    if u + 1 <= QT - 1:
                        q2_g(u + 1)

            @block.vector
            def _(v0):
                v = _Ser(v0, sm.s_vser, vst)
                v.wait_ge(sm.s_p1d, 5)

                def rows_v(e):
                    v.wait_ge(sm.s_srow2, e + 1)
                    v.wait_ge(sm.s_fin_2, e)
                    v.tensor_tensor(out=var_sb[0:1, :], in0=m_sb[0:1, :], in1=m_sb[0:1, :], op=ALU.mult)
                    v.tensor_tensor(out=var_sb[0:1, :], in0=e2_sb[0:1, :], in1=var_sb[0:1, :],
                                    op=ALU.subtract, inc=(sm.s_var2, 1, e + 1))
                    v.wait_ge(sm.s_fin_2, e + 1)
                    v.wait_ge(sm.s_rbmm2, 2 * e)
                    v.tensor_tensor(out=mr_sb[0:1, :], in0=m_sb[0:1, :], in1=r_sb[0:1, :], op=ALU.mult)
                    v.tensor_copy(rhi_sb[0:1, :], r_sb[0:1, :])
                    v.tensor_tensor(out=rlo_sb[0:1, :], in0=r_sb[0:1, :], in1=rhi_sb[0:1, :],
                                    op=ALU.subtract, inc=(sm.s_rows2, 1, e + 1))

                def xn_v(e):
                    v.wait_ge(sm.s_rbmm2, 2 * e + 1)
                    v.wait_ge(sm.s_xb2 if e % 2 == 0 else sm.s_xb3, 16 * (e // 2 + 1))
                    if e >= 2:
                        v.wait_ge(sm.s_mix, 8 * (e - 1))
                    v.tensor_copy(rb_sb[:, :], G[:, :], inc=(sm.s_rbcp, 1, e + 1))
                    v.wait_ge(sm.s_rbmm2, 2 * (e + 1))
                    for dc in range(DC):
                        v.tensor_tensor(out=t1_sb[:, :], in0=xbf[:, e % 2, dc, :],
                                        in1=rb_sb[:, :], op=ALU.mult)
                        v.tensor_tensor(out=xn[:, e % 2, dc, 1:NT + 1], in0=t1_sb[:, :],
                                        in1=G[:, :], op=ALU.subtract)
                        if e >= 1:
                            v.tensor_copy(xn[:, e % 2, dc, 0:1], xn[:, (e - 1) % 2, dc, NT:NT + 1])
                        else:
                            v.memset(xn[:, 0, dc, 0:1], 0.0)
                    if e == 0:
                        v.wait_ge(sm.s_misc, 48)
                        for dc in range(DC):
                            v.tensor_scalar(out=xn[:, 0, dc, NT:NT + 1], in0=xn[:, 0, dc, NT:NT + 1],
                                            scalar1=halo_s[:, :], scalar2=None, op0=ALU.mult)
                    v.nop(inc=(sm.s_xn, 1, e + 1))

                def rope_v(uu):
                    v.wait_ge(sm.s_q, uu + 1)
                    v.wait_ge(sm.s_tabq, 32 * (uu + 1))
                    v.tensor_tensor(out=p1q[0:64, :], in0=tab[0:64, 0, 0, :], in1=PqF[0:64, 0:512], op=ALU.mult)
                    v.tensor_tensor(out=p2q[0:64, :], in0=tab[0:64, 0, 1, :], in1=PqF[64:128, 0:512], op=ALU.mult)
                    v.tensor_tensor(out=q_sb[0:64, :], in0=p1q[0:64, :], in1=p2q[0:64, :], op=ALU.subtract)
                    v.tensor_tensor(out=p1q[64:128, :], in0=tab[64:128, 0, 0, :], in1=PqF[0:64, 0:512], op=ALU.mult)
                    v.tensor_tensor(out=p2q[64:128, :], in0=tab[64:128, 0, 1, :], in1=PqF[64:128, 0:512], op=ALU.mult)
                    v.tensor_tensor(out=q_sb[64:128, :], in0=p1q[64:128, :], in1=p2q[64:128, :],
                                    op=ALU.add, inc=(sm.s_qr, 1, uu + 1))

                def mix_v(uu):
                    e = uu + 1
                    if uu >= 1:
                        v.wait_ge(sm.s_rpre, 8 * uu)
                        v.wait_ge(sm.s_kpre, 8 * uu)
                    for dc in range(DC):
                        v.tensor_tensor(out=d_sb[:, :], in0=xn[:, e % 2, dc, 1:NT + 1],
                                        in1=xn[:, e % 2, dc, 0:NT], op=ALU.subtract)
                        v.scalar_tensor_tensor(out=xr_sb[:, dc, :], in0=d_sb[:, :],
                                               scalar=mur_s[:, dc:dc + 1], in1=xn[:, e % 2, dc, 0:NT],
                                               op0=ALU.mult, op1=ALU.add)
                        v.scalar_tensor_tensor(out=xk_sb[:, dc, :], in0=d_sb[:, :],
                                               scalar=muk_s[:, dc:dc + 1], in1=xn[:, e % 2, dc, 0:NT],
                                               op0=ALU.mult, op1=ALU.add,
                                               inc=(sm.s_mix, 1, 8 * uu + dc + 1))

                def bias_v(uu):
                    v.wait_ge(sm.s_qn, uu + 1)
                    v.reduce_max(sm_sb[0:1, 1:2], Dp[0:1, :], axis=AX.X,
                                 inc=(sm.s_qmaxd, 1, uu + 1))
                    v.tensor_tensor(out=sm_sb[0:1, 2:3], in0=sm_sb[0:1, 1:2], in1=sm_sb[0:1, 0:1], op=ALU.add)
                    v.tensor_scalar(out=sm_sb[0:1, 3:4], in0=sm_sb[0:1, 2:3],
                                    scalar1=-0.5 * RSQ, scalar2=None, op0=ALU.mult)
                    v.tensor_copy(bbf_sb[0:1, :], sm_sb[0:1, 3:4], inc=(sm.s_qb, 1, uu + 1))
                    v.wait_ge(sm.s_bmm, uu + 1)
                    v.tensor_copy(bias_sb[:, :], G[:, 0:1], inc=(sm.s_bias, 1, uu + 1))

                rows_v(0)
                xn_v(0)
                rows_v(1)
                xn_v(1)
                rope_v(0)
                mix_v(0)
                bias_v(0)

                for ell in range(1, NL):
                    u = ell - 1
                    if ell + 1 < NL:
                        rows_v(ell + 1)
                        xn_v(ell + 1)
                    v.wait_ge(sm.s_av, 512 * (u + 1))
                    v.wait_ge(sm.s_o, 4 * u)
                    v.reciprocal(recip_sb[:, 0:4],
                                 Pq[:, :, 128:129].rearrange("p a n -> p (a n)"),
                                 inc=(sm.s_den, 1, u + 1))
                    if u >= 1:
                        v.wait_ge(sm.s_facc, 16 * u)
                    for dc in range(DC):
                        v.wait_ge(sm.s_rl, 8 * u + dc + 1)
                        v.tensor_tensor(out=kh_sb[:, dc, :], in0=tt_sb[:, 0, :],
                                        in1=tt_sb[:, 0, :],
                                        op=ALU.mult, inc=(sm.s_kh, 1, 8 * u + dc + 1),
                                        chain=False)
                    v.wait_ge(sm.s_dbm, 4 * (u + 1))
                    v.tensor_copy(o_sb.ap().rearrange("p (a n) -> p a n", n=128),
                                  Pq[:, 2:4, 0:256].rearrange("p a (b n) -> p (a b) n", n=128),
                                  inc=(sm.s_oev, 1, u + 1))
                    if u + 1 <= QT - 1:
                        rope_v(u + 1)
                    if u + 1 <= QT - 1:
                        bias_v(u + 1)
                        mix_v(u + 1)
                    for dc in range(DC):
                        pacc = (A if dc % 2 == 0 else Bp)
                        pff = (PqF[:, 512:1024] if dc % 2 == 0 else G[:, 0:512])
                        v.wait_ge(sm.s_facc, 16 * u + 2 * dc + 2)
                        v.wait_ge(sm.s_g, 8 * u + dc + 1)
                        if 8 * u + dc >= 1:
                            v.wait_ge(sm.s_store, 16 * (8 * u + dc))
                        v.tensor_tensor(out=t1_sb[:, :], in0=g_sb[:, dc, :], in1=pff, op=ALU.mult)
                        v.tensor_tensor(out=outs[:, 0, :], in0=t1_sb[:, :], in1=pacc[:, 512:1024],
                                        op=ALU.add, inc=(sm.s_out, 1, 8 * u + dc + 1))
    return nc


_NC_CACHE = None


def kernel(L0_attn_x, L0_attn_wq, L0_attn_wk, L0_attn_wv, L0_attn_wo,
           L0_ffn_W_r, L0_ffn_W_k, L0_ffn_W_v, L0_ffn_mu_r, L0_ffn_mu_k):
    global _NC_CACHE
    in_maps = make_in_maps(L0_attn_x, L0_attn_wq, L0_attn_wk, L0_attn_wv,
                           L0_attn_wo, L0_ffn_W_r, L0_ffn_W_k, L0_ffn_W_v,
                           L0_ffn_mu_r, L0_ffn_mu_k)
    if _NC_CACHE is None:
        _NC_CACHE = build_nc()
    res = run_bass_kernel_spmd(_NC_CACHE, in_maps, core_ids=list(range(8)))

    out = np.empty((B, S, D), dtype=np.float32)
    for core in range(8):
        b, qi = core // 4, core % 4
        yv = res.results[core]["y"]
        out[b, qi * Q:(qi + 1) * Q, :] = yv.transpose(1, 3, 2, 0).reshape(Q, D)
    return out


def make_in_maps(L0_attn_x, L0_attn_wq, L0_attn_wk, L0_attn_wv, L0_attn_wo,
                 L0_ffn_W_r, L0_ffn_W_k, L0_ffn_W_v, L0_ffn_mu_r, L0_ffn_mu_k):
    x = np.asarray(L0_attn_x, dtype=np.float32)
    wq = np.asarray(L0_attn_wq, dtype=np.float32)
    wk = np.asarray(L0_attn_wk, dtype=np.float32)
    wv = np.asarray(L0_attn_wv, dtype=np.float32)
    wo = np.asarray(L0_attn_wo, dtype=np.float32)
    Wr = np.asarray(L0_ffn_W_r, dtype=np.float32)
    Wk = np.asarray(L0_ffn_W_k, dtype=np.float32)
    Wv = np.asarray(L0_ffn_W_v, dtype=np.float32)
    mu_r = np.asarray(L0_ffn_mu_r, dtype=np.float32)
    mu_k = np.asarray(L0_ffn_mu_k, dtype=np.float32)

    try:
        import jax
        import jax.numpy as jnp
        with jax.default_device(jax.devices("cpu")[0]):
            inv_j = 1.0 / 10000.0 ** (jnp.arange(0, HD, 2, dtype=jnp.float32) / HD)
            t_j = jnp.arange(S - 1, dtype=jnp.float32)
            ang_j = t_j[:, None] * inv_j[None, :]
            cosr = np.asarray(jnp.cos(ang_j))
            sinr = np.asarray(jnp.sin(ang_j))
    except Exception:
        inv32 = (1.0 / (np.float32(10000.0) **
                        (np.arange(0, HD, 2, dtype=np.float32) / np.float32(HD)))
                 ).astype(np.float32)
        t32 = np.arange(S - 1, dtype=np.float32)
        ang32 = (t32[:, None] * inv32[None, :]).astype(np.float32)
        cosr = np.cos(ang32).astype(np.float32)
        sinr = np.sin(ang32).astype(np.float32)
    cosf = np.empty((64, S), dtype=np.float32)
    sinf = np.empty((64, S), dtype=np.float32)
    cosf[:, 1:] = cosr.T
    sinf[:, 1:] = sinr.T
    cosf[:, 0] = 1.0
    sinf[:, 0] = 0.0
    CS = np.concatenate([cosf, sinf], axis=0)
    SC = np.concatenate([sinf, cosf], axis=0)

    perm_h = np.concatenate([np.arange(0, HD, 2), np.arange(1, HD, 2)])

    def _wT_pch(w):
        return np.ascontiguousarray(w.T.reshape(DC, 128, HD).transpose(1, 0, 2))

    def _fT_pcdk(w):
        return np.ascontiguousarray(w.T.reshape(DC, 128, DC, 128).transpose(1, 0, 2, 3))

    aq_h = _wT_pch(wq[perm_h])
    ak_h = _wT_pch(wk[perm_h])
    av_h = _wT_pch(wv)
    ao_h = np.ascontiguousarray(wo.T)
    fr_h = _fT_pcdk(Wr)
    fk_h = _fT_pcdk(Wk)
    fv_h = _fT_pcdk(Wv)
    mur_h = np.ascontiguousarray(mu_r.reshape(DC, 128).T)
    muk_h = np.ascontiguousarray(mu_k.reshape(DC, 128).T)
    ones_h = np.ones((128, 128), dtype=np.float32)
    ident_h = np.eye(128, dtype=np.float32)

    in_maps = []
    for core in range(8):
        b, qi = core // 4, core % 4
        q0 = qi * Q
        if qi == 0:
            perm = np.concatenate([np.arange(S - NT, S), np.arange(0, Q), np.arange(Q, S - NT)])
            halo_h = np.zeros((128, 1), dtype=np.float32)
        else:
            perm = np.concatenate([np.arange(q0 - NT, q0 + Q), np.arange(0, q0 - NT),
                                   np.arange(q0 + Q, S)])
            halo_h = np.ones((128, 1), dtype=np.float32)
        xT_h = np.ascontiguousarray(
            x[b].T[:, perm].reshape(DC, 128, ST, NT).transpose(1, 2, 0, 3))
        in_maps.append({
            "xT": xT_h,
            "cs": np.ascontiguousarray(CS[:, perm]),
            "sc": np.ascontiguousarray(SC[:, perm]),
            "csq": np.ascontiguousarray(CS[:, q0:q0 + Q]),
            "scq": np.ascontiguousarray(SC[:, q0:q0 + Q]),
            "aq": aq_h, "ak": ak_h, "av": av_h, "ao": ao_h,
            "fr": fr_h, "fk": fk_h, "fv": fv_h,
            "mur": mur_h, "muk": muk_h,
            "onesd": ones_h, "identd": ident_h, "halo": halo_h,
        })
    return in_maps


# version 14
# speedup vs baseline: 1.1133x; 1.0061x over previous
import math
import numpy as np
import ml_dtypes
import concourse.bass as bass
import concourse.mybir as mybir
from concourse.bass_utils import run_bass_kernel_spmd

F32 = mybir.dt.float32
BF16 = mybir.dt.bfloat16
AF = mybir.ActivationFunctionType
ALU = mybir.AluOpType
AX = mybir.AxisListType

B = 2
S = 16384
D = 1024
HD = 128
NT = 512
ST = S // NT
Q = 4096
QT = Q // NT
DC = D // 128
KC = S // 128
NG = KC // 2
RSQ = 1.0 / math.sqrt(128.0)
EPS = 1e-5


class _Ser:

    def __init__(self, eng, sem, st):
        self._e = eng
        self._sem = sem
        self._st = st

    def wait_ge(self, sem, v):
        self._e.wait_ge(sem, v)

    def __getattr__(self, name):
        f = getattr(self._e, name)

        def g(*a, inc=None, edge=True, chain=True, **kw):
            p = self._st["prev"]
            if p is not None and chain:
                self._e.wait_ge(p[0], p[1])
            ins = f(*a, **kw)
            if inc is None:
                self._st["n"] += 1
                ins.then_inc(self._sem, 1)
                self._st["prev"] = (self._sem, self._st["n"])
            else:
                sem, k, after = inc
                ins.then_inc(sem, k)
                if edge and after is not None:
                    self._st["prev"] = (sem, after)
            return ins
        return g


def build_nc():
    nc = bass.Bass("TRN2", target_bir_lowering=False, dynamic_dma_scratch_size=16384)

    xT = nc.dram_tensor("xT", [128, ST, DC, NT], F32, kind="ExternalInput").ap()
    cs = nc.dram_tensor("cs", [128, S], F32, kind="ExternalInput").ap()
    sc = nc.dram_tensor("sc", [128, S], F32, kind="ExternalInput").ap()
    csq = nc.dram_tensor("csq", [128, Q], F32, kind="ExternalInput").ap()
    scq = nc.dram_tensor("scq", [128, Q], F32, kind="ExternalInput").ap()
    aq = nc.dram_tensor("aq", [128, DC, HD], F32, kind="ExternalInput").ap()
    ak = nc.dram_tensor("ak", [128, DC, HD], F32, kind="ExternalInput").ap()
    av = nc.dram_tensor("av", [128, DC, HD], F32, kind="ExternalInput").ap()
    ao = nc.dram_tensor("ao", [HD, D], F32, kind="ExternalInput").ap()
    fr = nc.dram_tensor("fr", [128, DC, DC, 128], F32, kind="ExternalInput").ap()
    fk = nc.dram_tensor("fk", [128, DC, DC, 128], F32, kind="ExternalInput").ap()
    fv = nc.dram_tensor("fv", [128, DC, DC, 128], F32, kind="ExternalInput").ap()
    mur = nc.dram_tensor("mur", [128, DC], F32, kind="ExternalInput").ap()
    muk = nc.dram_tensor("muk", [128, DC], F32, kind="ExternalInput").ap()
    onesd = nc.dram_tensor("onesd", [128, 128], F32, kind="ExternalInput").ap()
    identd = nc.dram_tensor("identd", [128, 128], F32, kind="ExternalInput").ap()
    halo = nc.dram_tensor("halo", [128, 1], F32, kind="ExternalInput").ap()
    y = nc.dram_tensor("y", [128, QT, DC, NT], F32, kind="ExternalOutput").ap()

    from contextlib import ExitStack
    with ExitStack() as top:
        ec = top.enter_context
        Ksb = ec(nc.sbuf_tensor("Ksb", [128, S], BF16))
        Vsb = ec(nc.sbuf_tensor("Vsb", [128, KC, 129], BF16))
        aq_s = ec(nc.sbuf_tensor("aq_s", [128, DC, 128], BF16))
        ak_s = ec(nc.sbuf_tensor("ak_s", [128, DC, 128], BF16))
        av_s = ec(nc.sbuf_tensor("av_s", [128, DC, 128], BF16))
        ao_s = ec(nc.sbuf_tensor("ao_s", [128, D], BF16))
        mur_s = ec(nc.sbuf_tensor("mur_s", [128, DC], F32))
        muk_s = ec(nc.sbuf_tensor("muk_s", [128, DC], F32))
        ones_s = ec(nc.sbuf_tensor("ones_s", [128, 128], BF16))
        id_s = ec(nc.sbuf_tensor("id_s", [128, 128], F32))
        halo_s = ec(nc.sbuf_tensor("halo_s", [128, 1], F32))
        w1k_s = ec(nc.sbuf_tensor("w1k_s", [1, 128], BF16))
        w1v_s = ec(nc.sbuf_tensor("w1v_s", [1, 128], BF16))
        eps_s = ec(nc.sbuf_tensor("eps_s", [1, 1], F32))
        m_sb = ec(nc.sbuf_tensor("m_sb", [1, NT], F32))
        e2_sb = ec(nc.sbuf_tensor("e2_sb", [1, NT], F32))
        var_sb = ec(nc.sbuf_tensor("var_sb", [1, NT], F32))
        r_sb = ec(nc.sbuf_tensor("r_sb", [1, NT], F32))
        mr_sb = ec(nc.sbuf_tensor("mr_sb", [1, NT], BF16))
        rhi_sb = ec(nc.sbuf_tensor("rhi_sb", [1, NT], BF16))
        rlo_sb = ec(nc.sbuf_tensor("rlo_sb", [1, NT], BF16))
        rb_sb = ec(nc.sbuf_tensor("rb_sb", [128, NT], F32))
        sm_sb = ec(nc.sbuf_tensor("sm_sb", [1, 8], F32))
        bbf_sb = ec(nc.sbuf_tensor("bbf_sb", [1, 1], BF16))
        bias_sb = ec(nc.sbuf_tensor("bias_sb", [128, 1], F32))
        rden_sb, dhi_sb, dlo_sb = r_sb, rhi_sb, rlo_sb
        xbf = ec(nc.sbuf_tensor("xbf", [128, 2, DC, NT], BF16))
        x2 = ec(nc.sbuf_tensor("x2", [128, 4, NT], BF16))
        tab = ec(nc.sbuf_tensor("tab", [128, 1, 2, NT], F32))
        A = ec(nc.psum_tensor("A", [128, 1024], F32))
        Bp = ec(nc.psum_tensor("Bp", [128, 1024], F32))
        Dp = ec(nc.psum_tensor("Dp", [128, 512], F32))
        G = ec(nc.psum_tensor("G", [128, 512], F32))
        Pq = ec(nc.psum_tensor("Pq", [128, 4, 256], F32))
        PqF = Pq.ap().rearrange("p a n -> p (a n)")

        sems = {}
        for name in ["s_w", "s_misc", "s_w1", "s_w1e", "s_init",
                     "s_x0", "s_x1", "s_tab", "s_sq", "s_serows", "s_st", "s_var",
                     "s_fin", "s_fin2", "s_rb", "s_rbe", "s_csr", "s_tek",
                     "s_tev", "s_kp", "s_k2s", "s_k2m", "s_kmax", "s_vs",
                     "s_vt", "s_ve2", "s_p1d",
                     "s_xf", "s_xb2", "s_tabq", "s_fwk", "s_fwr", "s_fwv",
                     "s_st2", "s_sq2",
                     "s_srow2", "s_var2", "s_fin_2", "s_rows2", "s_rbmm2",
                     "s_xn", "s_q", "s_qr", "s_q2", "s_qn", "s_qmaxd",
                     "s_qb", "s_bmm", "s_bias", "s_qk", "s_exp", "s_av",
                     "s_den", "s_dbm", "s_o", "s_mix", "s_kpre", "s_rl",
                     "s_kh", "s_rpre", "s_g", "s_facc", "s_out", "s_store",
                     "s_vser", "s_sser", "s_yini", "s_xb3", "s_oev", "s_sqm", "s_sqm2", "s_rbcp"]:
            sems[name] = ec(nc.semaphore(name))
        sm = type("SM", (), sems)
        vst = {"prev": None, "n": 0}
        sst = {"prev": None, "n": 0}

        block = ec(nc.Block())

        with ExitStack() as p1:
            e1 = p1.enter_context
            csr = e1(nc.sbuf_tensor("csr", [128, NT], F32))
            scr = e1(nc.sbuf_tensor("scr", [128, NT], F32))
            p1s = e1(nc.sbuf_tensor("p1s", [128, NT], F32))
            p2s = e1(nc.sbuf_tensor("p2s", [128, NT], F32))
            vt = e1(nc.sbuf_tensor("vt", [128, NT], F32))
            k2 = e1(nc.sbuf_tensor("k2", [128, NT], BF16))

            @block.gpsimd
            def _(g):
                for wi, (dst, srcw) in enumerate([(aq_s, aq), (ak_s, ak), (av_s, av),
                                                  (ao_s, ao), (ones_s, onesd), (id_s, identd)]):
                    if wi >= 2:
                        g.wait_ge(sm.s_w, 16 * wi)
                    g.dma_start(out=dst[:], in_=srcw).then_inc(sm.s_w, 16)
                for t in range(ST):
                    if t >= 2:
                        g.wait_ge(sm.s_st, t - 1)
                        g.wait_ge(sm.s_tek, t - 1)
                        g.wait_ge(sm.s_tev, t - 1)
                        g.wait_ge(sm.s_sq, 8 * (t - 1))
                    g.dma_start(out=xbf[:, t % 2, :, :],
                                in_=xT[:, t, :, :]).then_inc(
                                    sm.s_x0 if t % 2 == 0 else sm.s_x1, 16)
                g.nop().then_inc(sm.s_p1d, 1)

            @block.sync
            def _(s):
                s.dma_start(out=mur_s[:], in_=mur).then_inc(sm.s_misc, 16)
                s.dma_start(out=muk_s[:], in_=muk).then_inc(sm.s_misc, 16)
                s.dma_start(out=halo_s[:], in_=halo).then_inc(sm.s_misc, 16)
                for t in range(ST):
                    if t >= 1:
                        s.wait_ge(sm.s_csr, t)
                    s.dma_start(out=tab[:, 0, 0, :], in_=cs[:, t * NT:(t + 1) * NT]).then_inc(sm.s_tab, 16)
                    s.dma_start(out=tab[:, 0, 1, :], in_=sc[:, t * NT:(t + 1) * NT]).then_inc(sm.s_tab, 16)
                s.nop().then_inc(sm.s_p1d, 1)

            @block.tensor
            def _(t):
                t.wait_ge(sm.s_w, 96)
                for c in range(DC):
                    ins = t.matmul(Dp[0:1, :128], ones_s[:, 0:1], ak_s[:, c, :],
                                   start=(c == 0), stop=(c == DC - 1))
                ins.then_inc(sm.s_w1, 1)
                for c in range(DC):
                    ins = t.matmul(Dp[32:33, :128], ones_s[:, 0:1], av_s[:, c, :],
                                   start=(c == 0), stop=(c == DC - 1))
                ins.then_inc(sm.s_w1, 1)

                def stats_mms(tt):
                    t.wait_ge(sm.s_x0 if tt % 2 == 0 else sm.s_x1, 16 * (tt // 2 + 1))
                    t.wait_ge(sm.s_serows, tt)
                    if tt == 0:
                        t.wait_ge(sm.s_w1e, 1)
                    for c in range(DC):
                        ins = t.matmul(Dp[0:1, :], ones_s[:, 0:1], xbf[:, tt % 2, c, :],
                                       start=(c == 0), stop=(c == DC - 1))
                    ins.then_inc(sm.s_st, 1)
                    for c in range(DC):
                        t.wait_ge(sm.s_sq, 8 * tt + c + 1)
                        t.matmul(Dp[32:33, :], ones_s[:, 0:1], x2[:, c % 4, :],
                                 start=(c == 0), stop=(c == DC - 1)).then_inc(sm.s_sqm, 1)

                def mid_mms(tt):
                    h0 = (tt % 2) * 512
                    t.wait_ge(sm.s_fin2, tt + 1)
                    t.wait_ge(sm.s_rbe, tt)
                    t.matmul(G[:, :], ones_s[0:1, :], rhi_sb[0:1, :], start=True, stop=False)
                    t.matmul(G[:, :], ones_s[0:1, :], rlo_sb[0:1, :], start=False, stop=True).then_inc(sm.s_rb, 1)
                    t.wait_ge(sm.s_kp, max(0, tt - 1))
                    for c in range(DC):
                        t.matmul(A[:, h0:h0 + 512], ak_s[:, c, :], xbf[:, tt % 2, c, :],
                                 start=(c == 0), stop=False)
                    t.matmul(A[:, h0:h0 + 512], w1k_s[0:1, :], mr_sb[0:1, :],
                             start=False, stop=True).then_inc(sm.s_tek, 1)
                    t.wait_ge(sm.s_vs, max(0, tt - 1))
                    for c in range(DC):
                        t.matmul(Bp[:, h0:h0 + 512], av_s[:, c, :], xbf[:, tt % 2, c, :],
                                 start=(c == 0), stop=False)
                    t.matmul(Bp[:, h0:h0 + 512], w1v_s[0:1, :], mr_sb[0:1, :],
                             start=False, stop=True).then_inc(sm.s_tev, 1)

                def tail_mms(tt):
                    t.wait_ge(sm.s_vs, tt + 1)
                    t.wait_ge(sm.s_ve2, tt)
                    for j in range(4):
                        t.transpose(Pq[:, j, 0:128],
                                    vt[:, j * 128:(j + 1) * 128],
                                    id_s[:]).then_inc(sm.s_vt, 1)
                    t.wait_ge(sm.s_k2s, tt + 1)
                    t.wait_ge(sm.s_kmax, tt)
                    t.matmul(Dp[64:65, :], ones_s[:, 0:1], k2[:, :],
                             start=True, stop=True).then_inc(sm.s_k2m, 1)

                for stage in range(ST + 2):
                    if stage < ST:
                        stats_mms(stage)
                    if 1 <= stage <= ST:
                        mid_mms(stage - 1)
                    if stage >= 2:
                        tail_mms(stage - 2)
                t.wait_ge(sm.s_w1, 2)
                t.wait_ge(sm.s_st, ST)
                t.wait_ge(sm.s_sqm, 8 * ST)
                t.wait_ge(sm.s_rb, ST)
                t.wait_ge(sm.s_tek, ST)
                t.wait_ge(sm.s_tev, ST)
                t.wait_ge(sm.s_vt, 4 * ST)
                t.wait_ge(sm.s_k2m, ST)
                t.nop().then_inc(sm.s_p1d, 1)

            @block.scalar
            def _(se0):
                se = _Ser(se0, sm.s_sser, sst)
                se.wait_ge(sm.s_init, 1)
                for stage in range(ST + 2):
                    if stage < ST:
                        tt = stage
                        se.wait_ge(sm.s_x0 if tt % 2 == 0 else sm.s_x1, 16 * (tt // 2 + 1))
                        for c in range(DC):
                            sidx = 8 * tt + c
                            se.wait_ge(sm.s_sqm, max(0, sidx - 3))
                            se.activation(out=x2[:, c % 4, :], in_=xbf[:, tt % 2, c, :],
                                          func=AF.Square, inc=(sm.s_sq, 1, sidx + 1), chain=False)
                        se.wait_ge(sm.s_st, tt + 1)
                        se.wait_ge(sm.s_sqm, 8 * (tt + 1))
                        se.wait_ge(sm.s_var, tt)
                        se.wait_ge(sm.s_fin2, tt)
                        se.activation(out=m_sb[0:1, :], in_=Dp[0:1, :], func=AF.Copy,
                                      scale=1.0 / D)
                        se.activation(out=e2_sb[0:1, :], in_=Dp[32:33, :], func=AF.Copy,
                                      scale=1.0 / D, inc=(sm.s_serows, 1, tt + 1))
                        se.wait_ge(sm.s_var, tt + 1)
                        se.activation(out=var_sb[0:1, :], in_=var_sb[0:1, :], func=AF.Ln,
                                      bias=eps_s[0:1, :])
                        se.activation(out=r_sb[0:1, :], in_=var_sb[0:1, :], func=AF.Exp,
                                      scale=-0.5, inc=(sm.s_fin, 1, tt + 1))
                    if stage >= 2:
                        tt = stage - 2
                        se.wait_ge(sm.s_kp, tt + 1)
                        se.wait_ge(sm.s_k2m, tt)
                        se.activation(out=k2[:, :], in_=Ksb[:, tt * NT:(tt + 1) * NT],
                                      func=AF.Square, inc=(sm.s_k2s, 1, tt + 1))
                se.nop(inc=(sm.s_p1d, 1, None))

            @block.vector
            def _(v0):
                v = _Ser(v0, sm.s_vser, vst)
                v.memset(sm_sb[0:1, :], 0.0)
                v.memset(Vsb[:, :, 128:129], 1.0)
                v.memset(eps_s[0:1, :], EPS, inc=(sm.s_init, 1, 1))
                v.wait_ge(sm.s_w1, 2)
                v.tensor_scalar(out=w1k_s[0:1, :], in0=Dp[0:1, :128], scalar1=-1.0,
                                scalar2=None, op0=ALU.mult)
                v.tensor_scalar(out=w1v_s[0:1, :], in0=Dp[32:33, :128], scalar1=-1.0,
                                scalar2=None, op0=ALU.mult, inc=(sm.s_w1e, 1, 1))
                for stage in range(ST + 2):
                    if stage < ST:
                        tt = stage
                        v.wait_ge(sm.s_serows, tt + 1)
                        v.wait_ge(sm.s_fin, tt)
                        v.tensor_tensor(out=var_sb[0:1, :], in0=m_sb[0:1, :],
                                        in1=m_sb[0:1, :], op=ALU.mult)
                        v.tensor_tensor(out=var_sb[0:1, :], in0=e2_sb[0:1, :],
                                        in1=var_sb[0:1, :], op=ALU.subtract,
                                        inc=(sm.s_var, 1, tt + 1))
                        v.wait_ge(sm.s_fin, tt + 1)
                        v.wait_ge(sm.s_tek, tt)
                        v.wait_ge(sm.s_tev, tt)
                        v.wait_ge(sm.s_rb, tt)
                        v.tensor_copy(mr_sb[0:1, :], m_sb[0:1, :])
                        v.tensor_copy(rhi_sb[0:1, :], r_sb[0:1, :])
                        v.tensor_tensor(out=rlo_sb[0:1, :], in0=r_sb[0:1, :],
                                        in1=rhi_sb[0:1, :], op=ALU.subtract,
                                        inc=(sm.s_fin2, 1, tt + 1))
                    if 1 <= stage <= ST:
                        tt = stage - 1
                        h0 = (tt % 2) * 512
                        v.wait_ge(sm.s_rb, tt + 1)
                        v.tensor_copy(rb_sb[:, :], G[:, :], inc=(sm.s_rbe, 1, tt + 1))
                        v.wait_ge(sm.s_tab, 32 * (tt + 1))
                        v.tensor_tensor(out=csr[:, :], in0=tab[:, 0, 0, :],
                                        in1=rb_sb[:, :], op=ALU.mult)
                        v.tensor_tensor(out=scr[:, :], in0=tab[:, 0, 1, :],
                                        in1=rb_sb[:, :], op=ALU.mult,
                                        inc=(sm.s_csr, 1, tt + 1))
                        v.wait_ge(sm.s_tek, tt + 1)
                        v.tensor_tensor(out=p1s[0:64, :], in0=csr[0:64, :], in1=A[0:64, h0:h0 + 512], op=ALU.mult)
                        v.tensor_tensor(out=p2s[0:64, :], in0=scr[0:64, :], in1=A[64:128, h0:h0 + 512], op=ALU.mult)
                        v.tensor_tensor(out=Ksb[0:64, tt * NT:(tt + 1) * NT],
                                        in0=p1s[0:64, :], in1=p2s[0:64, :], op=ALU.subtract)
                        v.tensor_tensor(out=p1s[64:128, :], in0=csr[64:128, :], in1=A[0:64, h0:h0 + 512], op=ALU.mult)
                        v.tensor_tensor(out=p2s[64:128, :], in0=scr[64:128, :], in1=A[64:128, h0:h0 + 512], op=ALU.mult)
                        v.tensor_tensor(out=Ksb[64:128, tt * NT:(tt + 1) * NT],
                                        in0=p1s[64:128, :], in1=p2s[64:128, :], op=ALU.add,
                                        inc=(sm.s_kp, 1, tt + 1))
                        v.wait_ge(sm.s_tev, tt + 1)
                        v.wait_ge(sm.s_vt, 4 * tt)
                        v.tensor_tensor(out=vt[:, :], in0=Bp[:, h0:h0 + 512], in1=rb_sb[:, :],
                                        op=ALU.mult, inc=(sm.s_vs, 1, tt + 1))
                    if stage >= 2:
                        tt = stage - 2
                        v.wait_ge(sm.s_vt, 4 * (tt + 1))
                        v.tensor_copy(Vsb[:, tt * 4:(tt + 1) * 4, 0:128], Pq[:, :, 0:128],
                                      inc=(sm.s_ve2, 1, tt + 1))
                        v.wait_ge(sm.s_k2m, tt + 1)
                        v.reduce_max(sm_sb[0:1, 1:2], Dp[64:65, :], axis=AX.X)
                        v.tensor_tensor(out=sm_sb[0:1, 0:1], in0=sm_sb[0:1, 0:1],
                                        in1=sm_sb[0:1, 1:2], op=ALU.max,
                                        inc=(sm.s_kmax, 1, tt + 1))
                v.nop(inc=(sm.s_p1d, 1, None))

        with ExitStack() as p2:
            e2 = p2.enter_context
            fbuf = [e2(nc.sbuf_tensor(f"fbuf{i}", [128, DC, DC, 128], BF16)) for i in range(2)]
            xn = e2(nc.sbuf_tensor("xn", [128, 2, DC, NT + 1], BF16))
            q_sb = e2(nc.sbuf_tensor("q_sb", [128, NT], BF16))
            q2_sb = e2(nc.sbuf_tensor("q2_sb", [128, NT], BF16))
            Pr = e2(nc.sbuf_tensor("Pr", [128, 2, 1024], BF16))
            o_sb = e2(nc.sbuf_tensor("o_sb", [128, NT], BF16))
            oq_sb = e2(nc.sbuf_tensor("oq_sb", [128, NT], F32))
            recip_sb = e2(nc.sbuf_tensor("recip_sb", [128, 4], F32))
            xr_sb = e2(nc.sbuf_tensor("xr_sb", [128, DC, NT], BF16))
            xk_sb = e2(nc.sbuf_tensor("xk_sb", [128, DC, NT], BF16))
            kh_sb = e2(nc.sbuf_tensor("kh_sb", [128, DC, NT], BF16))
            d_sb = e2(nc.sbuf_tensor("d_sb", [128, NT], BF16))
            tt_sb = e2(nc.sbuf_tensor("tt_sb", [128, 2, NT], BF16))
            t1_sb = e2(nc.sbuf_tensor("t1_sb", [128, NT], F32))
            outs = e2(nc.sbuf_tensor("outs", [128, 1, NT], F32))
            g_sb = e2(nc.sbuf_tensor("g_sb", [128, DC, NT], BF16))
            p1q, p2q = t1_sb, rb_sb

            NL = QT + 1

            @block.gpsimd
            def _(g):
                def xload(e):
                    if e >= 2:
                        g.wait_ge(sm.s_st2, e - 1)
                        g.wait_ge(sm.s_sq2, 8 * (e - 1))
                        g.wait_ge(sm.s_xn, e - 1)
                    g.dma_start(out=xbf[:, e % 2, :, :],
                                in_=xT[:, e, :, :]).then_inc(
                                    sm.s_xb2 if e % 2 == 0 else sm.s_xb3, 16)

                g.wait_ge(sm.s_p1d, 5)
                xload(0)
                xload(1)
                xload(2)
                for ell in range(1, NL):
                    if ell + 2 < NL:
                        xload(ell + 2)
                    if ell >= 1:
                        u = ell - 1
                        for i, (w, fsem) in enumerate([(fk, sm.s_fwk), (fr, sm.s_fwr),
                                                       (fv, sm.s_fwv)]):
                            li = 3 * u + i
                            buf = fbuf[li % 2]
                            if i == 0 and u >= 1:
                                g.wait_ge(sm.s_rpre, 8 * u)
                            if i == 1 and u >= 1:
                                g.wait_ge(sm.s_facc, 16 * u)
                            if i == 1:
                                g.wait_ge(sm.s_fwk, 16 * (u + 1))
                            if i == 2:
                                g.wait_ge(sm.s_kpre, 8 * u + 8)
                                g.wait_ge(sm.s_fwr, 16 * (u + 1))
                            g.dma_start(out=buf[:], in_=w).then_inc(fsem, 16)
                    if ell >= 1:
                        u = ell - 1
                        if ell == 1:
                            g.wait_ge(sm.s_yini, 16)
                        for dc in range(DC):
                            g.wait_ge(sm.s_out, 8 * u + dc + 1)
                            if 8 * u + dc >= 1:
                                g.wait_ge(sm.s_store, 16 * (8 * u + dc))
                            g.dma_start(out=y[:, u, dc, :], in_=outs[:, 0, :],
                                        accum_op=ALU.add).then_inc(sm.s_store, 16)
                g.wait_ge(sm.s_store, 16 * 8 * QT)

            @block.sync
            def _(s):
                s.wait_ge(sm.s_p1d, 5)
                s.dma_start(out=y[:, :, :, :], in_=xT[:, 1:QT + 1, :, :]).then_inc(sm.s_yini, 16)
                for ell in range(NL):
                    u = ell - 1
                    if ell >= 1:
                        s.wait_ge(sm.s_qr, u)
                        s.dma_start(out=tab[:, 0, 0, :], in_=csq[:, u * NT:(u + 1) * NT]).then_inc(sm.s_tabq, 16)
                        s.dma_start(out=tab[:, 0, 1, :], in_=scq[:, u * NT:(u + 1) * NT]).then_inc(sm.s_tabq, 16)

            @block.tensor
            def _(t):
                t.wait_ge(sm.s_p1d, 5)

                def stats_x(e):
                    t.wait_ge(sm.s_xb2 if e % 2 == 0 else sm.s_xb3, 16 * (e // 2 + 1))
                    t.wait_ge(sm.s_srow2, e)
                    for c in range(DC):
                        ins = t.matmul(Dp[32:33, :], ones_s[:, 0:1], xbf[:, e % 2, c, :],
                                       start=(c == 0), stop=(c == DC - 1))
                    ins.then_inc(sm.s_st2, 1)

                def stats_x2(e, c):
                    t.wait_ge(sm.s_sq2, 8 * e + c + 1)
                    t.matmul(Dp[64:65, :], ones_s[:, 0:1], x2[:, c % 4, :],
                             start=(c == 0), stop=(c == DC - 1)).then_inc(sm.s_sqm2, 1)

                def stats_g(e):
                    stats_x(e)
                    for c in range(DC):
                        stats_x2(e, c)

                def bcast_g(e):
                    t.wait_ge(sm.s_rows2, e + 1)
                    t.wait_ge(sm.s_xn, max(0, e))
                    t.wait_ge(sm.s_bias, max(0, e - 1))
                    t.wait_ge(sm.s_out, 8 * max(0, e - 2))
                    t.matmul(G[:, :], ones_s[0:1, :], rhi_sb[0:1, :], start=True, stop=False)
                    t.matmul(G[:, :], ones_s[0:1, :], rlo_sb[0:1, :],
                             start=False, stop=True).then_inc(sm.s_rbmm2, 1)

                def mr_g(e):
                    t.wait_ge(sm.s_rbcp, e + 1)
                    t.matmul(G[:, :], ones_s[0:1, :], mr_sb[0:1, :],
                             start=True, stop=True).then_inc(sm.s_rbmm2, 1)

                def qproj_g(uu):
                    t.wait_ge(sm.s_xn, uu + 2)
                    t.wait_ge(sm.s_o, 4 * uu)
                    t.wait_ge(sm.s_den, uu)
                    for c in range(DC):
                        ins = t.matmul(PqF[:, 0:512], aq_s[:, c, :], xn[:, (uu + 1) % 2, c, 1:NT + 1],
                                       start=(c == 0), stop=(c == DC - 1))
                    ins.then_inc(sm.s_q, 1)

                def qnorm_g(uu):
                    t.wait_ge(sm.s_q2, uu + 1)
                    t.wait_ge(sm.s_qmaxd, uu)
                    t.matmul(Dp[0:1, :], ones_s[:, 0:1], q2_sb[:, :],
                             start=True, stop=True).then_inc(sm.s_qn, 1)
                    t.wait_ge(sm.s_qb, uu + 1)
                    t.wait_ge(sm.s_xn, min(uu + 2, NL))
                    t.matmul(G[:, 0:1], ones_s[0:1, :], bbf_sb[0:1, :],
                             start=True, stop=True).then_inc(sm.s_bmm, 1)

                def flash(u, ell):
                    t.wait_ge(sm.s_qr, u + 1)
                    t.wait_ge(sm.s_bias, u + 1)
                    t.wait_ge(sm.s_den, u)
                    t.wait_ge(sm.s_o, 4 * u)
                    t.wait_ge(sm.s_oev, u)
                    if u >= 1:
                        t.wait_ge(sm.s_g, 8 * u)
                        t.wait_ge(sm.s_rl, 8 * u)
                        t.wait_ge(sm.s_out, 8 * u)

                    def av_g(gj):
                        t.wait_ge(sm.s_exp, 64 * u + gj + 1)
                        for h in range(2):
                            kc = 2 * gj + h
                            for j in range(4):
                                t.matmul(Pq[:, j, 0:129],
                                         Pr[:, gj % 2, h * 512 + j * 128:h * 512 + (j + 1) * 128],
                                         Vsb[:, kc, :],
                                         start=(kc == 0 and h == 0 and j % 2 == 0),
                                         stop=(kc == KC - 1 and h == 1 and j % 2 == 1)).then_inc(sm.s_av, 1)

                    for gi in range(NG):
                        pg = A if gi % 2 == 0 else Bp
                        t.wait_ge(sm.s_exp, max(0, 64 * u + gi - 1))
                        for h in range(2):
                            kc = 2 * gi + h
                            t.matmul(pg[:, h * 512:(h + 1) * 512],
                                     Ksb[:, kc * 128:(kc + 1) * 128], q_sb[:, :],
                                     start=True, stop=True).then_inc(sm.s_qk, 1)
                        if ell + 1 < NL:
                            if gi == 2:
                                stats_x(ell + 1)
                            if gi == 7:
                                for c2 in range(4):
                                    stats_x2(ell + 1, c2)
                            if gi == 11:
                                for c2 in range(4, DC):
                                    stats_x2(ell + 1, c2)
                            if gi == 18:
                                bcast_g(ell + 1)
                            if gi == 22:
                                mr_g(ell + 1)
                        if gi >= 1:
                            av_g(gi - 1)
                    av_g(NG - 1)

                stats_g(0)
                bcast_g(0)
                mr_g(0)
                stats_g(1)
                bcast_g(1)
                mr_g(1)
                qproj_g(0)
                qnorm_g(0)

                for ell in range(1, NL):
                    u = ell - 1
                    flash(u, ell)
                    t.wait_ge(sm.s_fwk, 16 * (u + 1))
                    for dc in range(DC):
                        pf = (A if dc % 2 == 0 else Bp)
                        t.wait_ge(sm.s_mix, 8 * u + dc + 1)
                        t.wait_ge(sm.s_rl, max(0, 8 * u + dc - 1))
                        for c in range(DC):
                            ins = t.matmul(pf[:, 0:512], fbuf[(3 * u) % 2][:, c, dc, :],
                                           xk_sb[:, c, :], start=(c == 0), stop=(c == DC - 1))
                        ins.then_inc(sm.s_kpre, 1)
                    t.wait_ge(sm.s_o, 4 * (u + 1))
                    t.wait_ge(sm.s_oev, u)
                    for j in range(4):
                        t.transpose(Pq[:, 2 + j // 2, (j % 2) * 128:(j % 2) * 128 + 128],
                                    oq_sb[:, j * 128:(j + 1) * 128],
                                    id_s[:]).then_inc(sm.s_dbm, 1)
                    if u + 1 <= QT - 1:
                        qproj_g(u + 1)
                    t.wait_ge(sm.s_fwr, 16 * (u + 1))
                    for dc in range(DC):
                        pf = (A if dc % 2 == 0 else Bp)
                        t.wait_ge(sm.s_g, max(0, 8 * u + dc - 1))
                        if dc in (0, 1):
                            t.wait_ge(sm.s_rl, 8 * u + 7 + dc)
                        for c in range(DC):
                            ins = t.matmul(pf[:, 0:512], fbuf[(3 * u + 1) % 2][:, c, dc, :],
                                           xr_sb[:, c, :], start=(c == 0), stop=(c == DC - 1))
                        ins.then_inc(sm.s_rpre, 1)
                    if u + 1 <= QT - 1:
                        qnorm_g(u + 1)
                    t.wait_ge(sm.s_fwv, 16 * (u + 1))
                    t.wait_ge(sm.s_oev, u + 1)
                    t.wait_ge(sm.s_kh, 8 * u + 8)
                    t.wait_ge(sm.s_qr, min(u + 2, QT))
                    t.wait_ge(sm.s_bias, min(u + 2, QT))
                    for dc in range(DC):
                        pacc = (A if dc % 2 == 0 else Bp)
                        pff = (PqF[:, 512:1024] if dc % 2 == 0 else G[:, 0:512])
                        t.wait_ge(sm.s_out, max(0, 8 * u + dc - 1))
                        if dc in (0, 1):
                            t.wait_ge(sm.s_g, 8 * u + 7 + dc)
                        t.matmul(pacc[:, 512:1024], ao_s[:, dc * 128:(dc + 1) * 128],
                                 o_sb[:, :], start=True, stop=True).then_inc(sm.s_facc, 1)
                        for c in range(DC):
                            ins = t.matmul(pff, fbuf[(3 * u + 2) % 2][:, c, dc, :],
                                           kh_sb[:, c, :], start=(c == 0), stop=(c == DC - 1))
                        ins.then_inc(sm.s_facc, 1)

            @block.scalar
            def _(se0):
                se = _Ser(se0, sm.s_sser, sst)
                se.wait_ge(sm.s_p1d, 5)

                def sq_c(e, c):
                    sidx = 8 * e + c
                    if c == 0:
                        se.wait_ge(sm.s_xb2 if e % 2 == 0 else sm.s_xb3, 16 * (e // 2 + 1))
                    se.wait_ge(sm.s_sqm2, max(0, sidx - 3))
                    se.activation(out=x2[:, c % 4, :], in_=xbf[:, e % 2, c, :],
                                  func=AF.Square, inc=(sm.s_sq2, 1, sidx + 1), chain=False)

                def sq_g(e):
                    for c in range(DC):
                        sq_c(e, c)

                def rows_g_a(e):
                    se.wait_ge(sm.s_st2, e + 1)
                    se.wait_ge(sm.s_sqm2, 8 * (e + 1))
                    se.wait_ge(sm.s_var2, e)
                    se.wait_ge(sm.s_rows2, e)
                    se.activation(out=m_sb[0:1, :], in_=Dp[32:33, :], func=AF.Copy, scale=1.0 / D)
                    se.activation(out=e2_sb[0:1, :], in_=Dp[64:65, :], func=AF.Copy,
                                  scale=1.0 / D, inc=(sm.s_srow2, 1, e + 1))

                def rows_g_b(e):
                    se.wait_ge(sm.s_var2, e + 1)
                    se.wait_ge(sm.s_den, max(0, e - 2))
                    se.activation(out=var_sb[0:1, :], in_=var_sb[0:1, :], func=AF.Ln,
                                  bias=eps_s[0:1, :])
                    se.activation(out=r_sb[0:1, :], in_=var_sb[0:1, :], func=AF.Exp,
                                  scale=-0.5, inc=(sm.s_fin_2, 1, e + 1))

                def rows_g(e):
                    rows_g_a(e)
                    rows_g_b(e)

                def q2_g(uu):
                    se.wait_ge(sm.s_qr, uu + 1)
                    se.wait_ge(sm.s_qn, uu)
                    se.activation(out=q2_sb[:, :], in_=q_sb[:, :], func=AF.Square,
                                  inc=(sm.s_q2, 1, uu + 1))

                sq_g(0)
                rows_g(0)
                sq_g(1)
                rows_g(1)
                q2_g(0)

                for ell in range(1, NL):
                    u = ell - 1
                    se.wait_ge(sm.s_bias, u + 1)
                    for gi in range(NG):
                        pg = A if gi % 2 == 0 else Bp
                        se.wait_ge(sm.s_qk, 128 * u + 2 * gi + 2)
                        se.wait_ge(sm.s_av, max(0, 8 * (64 * u + gi - 1)))
                        se.activation(out=Pr[:, gi % 2, :], in_=pg[:, :], func=AF.Exp,
                                      scale=RSQ, bias=bias_sb[:, :],
                                      inc=(sm.s_exp, 1, 64 * u + gi + 1), chain=False)
                        if gi == 5 and ell + 1 < NL:
                            for c in range(4):
                                sq_c(ell + 1, c)
                        if gi == 9 and ell + 1 < NL:
                            for c in range(4, DC):
                                sq_c(ell + 1, c)
                        if gi == 13 and ell + 1 < NL:
                            rows_g_a(ell + 1)
                        if gi == 15 and ell + 1 < NL:
                            rows_g_b(ell + 1)
                    se.wait_ge(sm.s_den, u + 1)
                    se.wait_ge(sm.s_dbm, 4 * u)
                    for j in range(4):
                        se.activation(out=oq_sb[:, j * 128:(j + 1) * 128], in_=Pq[:, j, 0:128],
                                      func=AF.Copy, scale=recip_sb[:, j:j + 1],
                                      inc=(sm.s_o, 1, 4 * u + j + 1), chain=False)
                    for dc in range(DC):
                        pf = (A if dc % 2 == 0 else Bp)
                        se.wait_ge(sm.s_kpre, 8 * u + dc + 1)
                        se.wait_ge(sm.s_kh, max(0, 8 * u + dc - 1))
                        se.activation(out=tt_sb[:, dc % 2, :], in_=pf[:, 0:512], func=AF.Relu,
                                      inc=(sm.s_rl, 1, 8 * u + dc + 1), chain=False)
                    se.wait_ge(sm.s_out, 8 * u)
                    for dc in range(DC):
                        pf = (A if dc % 2 == 0 else Bp)
                        se.wait_ge(sm.s_rpre, 8 * u + dc + 1)
                        se.wait_ge(sm.s_kh, min(8 * u + 7 + dc, 8 * u + 8))
                        se.activation(out=tt_sb[:, dc % 2, :], in_=pf[:, 0:512], func=AF.Tanh, scale=0.5)
                        se.activation(out=g_sb[:, dc, :], in_=tt_sb[:, dc % 2, :], func=AF.Copy,
                                      scale=0.5, bias=0.5, inc=(sm.s_g, 1, 8 * u + dc + 1))
                    if u + 1 <= QT - 1:
                        q2_g(u + 1)

            @block.vector
            def _(v0):
                v = _Ser(v0, sm.s_vser, vst)
                v.wait_ge(sm.s_p1d, 5)

                def rows_v(e):
                    v.wait_ge(sm.s_srow2, e + 1)
                    v.wait_ge(sm.s_fin_2, e)
                    v.tensor_tensor(out=var_sb[0:1, :], in0=m_sb[0:1, :], in1=m_sb[0:1, :], op=ALU.mult)
                    v.tensor_tensor(out=var_sb[0:1, :], in0=e2_sb[0:1, :], in1=var_sb[0:1, :],
                                    op=ALU.subtract, inc=(sm.s_var2, 1, e + 1))
                    v.wait_ge(sm.s_fin_2, e + 1)
                    v.wait_ge(sm.s_rbmm2, 2 * e)
                    v.tensor_tensor(out=mr_sb[0:1, :], in0=m_sb[0:1, :], in1=r_sb[0:1, :], op=ALU.mult)
                    v.tensor_copy(rhi_sb[0:1, :], r_sb[0:1, :])
                    v.tensor_tensor(out=rlo_sb[0:1, :], in0=r_sb[0:1, :], in1=rhi_sb[0:1, :],
                                    op=ALU.subtract, inc=(sm.s_rows2, 1, e + 1))

                def xn_v(e):
                    v.wait_ge(sm.s_rbmm2, 2 * e + 1)
                    v.wait_ge(sm.s_xb2 if e % 2 == 0 else sm.s_xb3, 16 * (e // 2 + 1))
                    if e >= 2:
                        v.wait_ge(sm.s_mix, 8 * (e - 1))
                    v.tensor_copy(rb_sb[:, :], G[:, :], inc=(sm.s_rbcp, 1, e + 1))
                    v.wait_ge(sm.s_rbmm2, 2 * (e + 1))
                    for dc in range(DC):
                        v.tensor_tensor(out=t1_sb[:, :], in0=xbf[:, e % 2, dc, :],
                                        in1=rb_sb[:, :], op=ALU.mult)
                        v.tensor_tensor(out=xn[:, e % 2, dc, 1:NT + 1], in0=t1_sb[:, :],
                                        in1=G[:, :], op=ALU.subtract)
                        if e >= 1:
                            v.tensor_copy(xn[:, e % 2, dc, 0:1], xn[:, (e - 1) % 2, dc, NT:NT + 1])
                        else:
                            v.memset(xn[:, 0, dc, 0:1], 0.0)
                    if e == 0:
                        v.wait_ge(sm.s_misc, 48)
                        for dc in range(DC):
                            v.tensor_scalar(out=xn[:, 0, dc, NT:NT + 1], in0=xn[:, 0, dc, NT:NT + 1],
                                            scalar1=halo_s[:, :], scalar2=None, op0=ALU.mult)
                    v.nop(inc=(sm.s_xn, 1, e + 1))

                def rope_v(uu):
                    v.wait_ge(sm.s_q, uu + 1)
                    v.wait_ge(sm.s_tabq, 32 * (uu + 1))
                    v.tensor_tensor(out=p1q[0:64, :], in0=tab[0:64, 0, 0, :], in1=PqF[0:64, 0:512], op=ALU.mult)
                    v.tensor_tensor(out=p2q[0:64, :], in0=tab[0:64, 0, 1, :], in1=PqF[64:128, 0:512], op=ALU.mult)
                    v.tensor_tensor(out=q_sb[0:64, :], in0=p1q[0:64, :], in1=p2q[0:64, :], op=ALU.subtract)
                    v.tensor_tensor(out=p1q[64:128, :], in0=tab[64:128, 0, 0, :], in1=PqF[0:64, 0:512], op=ALU.mult)
                    v.tensor_tensor(out=p2q[64:128, :], in0=tab[64:128, 0, 1, :], in1=PqF[64:128, 0:512], op=ALU.mult)
                    v.tensor_tensor(out=q_sb[64:128, :], in0=p1q[64:128, :], in1=p2q[64:128, :],
                                    op=ALU.add, inc=(sm.s_qr, 1, uu + 1))

                def mix_v(uu):
                    e = uu + 1
                    if uu >= 1:
                        v.wait_ge(sm.s_rpre, 8 * uu)
                        v.wait_ge(sm.s_kpre, 8 * uu)
                    for dc in range(DC):
                        v.tensor_tensor(out=d_sb[:, :], in0=xn[:, e % 2, dc, 1:NT + 1],
                                        in1=xn[:, e % 2, dc, 0:NT], op=ALU.subtract)
                        v.scalar_tensor_tensor(out=xr_sb[:, dc, :], in0=d_sb[:, :],
                                               scalar=mur_s[:, dc:dc + 1], in1=xn[:, e % 2, dc, 0:NT],
                                               op0=ALU.mult, op1=ALU.add)
                        v.scalar_tensor_tensor(out=xk_sb[:, dc, :], in0=d_sb[:, :],
                                               scalar=muk_s[:, dc:dc + 1], in1=xn[:, e % 2, dc, 0:NT],
                                               op0=ALU.mult, op1=ALU.add,
                                               inc=(sm.s_mix, 1, 8 * uu + dc + 1))

                def bias_v(uu):
                    v.wait_ge(sm.s_qn, uu + 1)
                    v.reduce_max(sm_sb[0:1, 1:2], Dp[0:1, :], axis=AX.X,
                                 inc=(sm.s_qmaxd, 1, uu + 1))
                    v.tensor_tensor(out=sm_sb[0:1, 2:3], in0=sm_sb[0:1, 1:2], in1=sm_sb[0:1, 0:1], op=ALU.add)
                    v.tensor_scalar(out=sm_sb[0:1, 3:4], in0=sm_sb[0:1, 2:3],
                                    scalar1=-0.5 * RSQ, scalar2=None, op0=ALU.mult)
                    v.tensor_copy(bbf_sb[0:1, :], sm_sb[0:1, 3:4], inc=(sm.s_qb, 1, uu + 1))
                    v.wait_ge(sm.s_bmm, uu + 1)
                    v.tensor_copy(bias_sb[:, :], G[:, 0:1], inc=(sm.s_bias, 1, uu + 1))

                rows_v(0)
                xn_v(0)
                rows_v(1)
                xn_v(1)
                rope_v(0)
                mix_v(0)
                bias_v(0)

                for ell in range(1, NL):
                    u = ell - 1
                    if ell + 1 < NL:
                        rows_v(ell + 1)
                        xn_v(ell + 1)
                    v.wait_ge(sm.s_av, 512 * (u + 1))
                    v.wait_ge(sm.s_o, 4 * u)
                    v.reciprocal(recip_sb[:, 0:4],
                                 Pq[:, :, 128:129].rearrange("p a n -> p (a n)"),
                                 inc=(sm.s_den, 1, u + 1))
                    if u >= 1:
                        v.wait_ge(sm.s_facc, 16 * u)
                    for dc in range(DC):
                        v.wait_ge(sm.s_rl, 8 * u + dc + 1)
                        v.tensor_tensor(out=kh_sb[:, dc, :], in0=tt_sb[:, dc % 2, :],
                                        in1=tt_sb[:, dc % 2, :],
                                        op=ALU.mult, inc=(sm.s_kh, 1, 8 * u + dc + 1),
                                        chain=False)
                    v.wait_ge(sm.s_dbm, 4 * (u + 1))
                    v.tensor_copy(o_sb.ap().rearrange("p (a n) -> p a n", n=128),
                                  Pq[:, 2:4, 0:256].rearrange("p a (b n) -> p (a b) n", n=128),
                                  inc=(sm.s_oev, 1, u + 1))
                    if u + 1 <= QT - 1:
                        rope_v(u + 1)
                    if u + 1 <= QT - 1:
                        bias_v(u + 1)
                        mix_v(u + 1)
                    for dc in range(DC):
                        pacc = (A if dc % 2 == 0 else Bp)
                        pff = (PqF[:, 512:1024] if dc % 2 == 0 else G[:, 0:512])
                        v.wait_ge(sm.s_facc, 16 * u + 2 * dc + 2)
                        v.wait_ge(sm.s_g, 8 * u + dc + 1)
                        if 8 * u + dc >= 1:
                            v.wait_ge(sm.s_store, 16 * (8 * u + dc))
                        v.tensor_tensor(out=t1_sb[:, :], in0=g_sb[:, dc, :], in1=pff, op=ALU.mult)
                        v.tensor_tensor(out=outs[:, 0, :], in0=t1_sb[:, :], in1=pacc[:, 512:1024],
                                        op=ALU.add, inc=(sm.s_out, 1, 8 * u + dc + 1))
    return nc


_NC_CACHE = None


def kernel(L0_attn_x, L0_attn_wq, L0_attn_wk, L0_attn_wv, L0_attn_wo,
           L0_ffn_W_r, L0_ffn_W_k, L0_ffn_W_v, L0_ffn_mu_r, L0_ffn_mu_k):
    global _NC_CACHE
    in_maps = make_in_maps(L0_attn_x, L0_attn_wq, L0_attn_wk, L0_attn_wv,
                           L0_attn_wo, L0_ffn_W_r, L0_ffn_W_k, L0_ffn_W_v,
                           L0_ffn_mu_r, L0_ffn_mu_k)
    if _NC_CACHE is None:
        _NC_CACHE = build_nc()
    res = run_bass_kernel_spmd(_NC_CACHE, in_maps, core_ids=list(range(8)))

    out = np.empty((B, S, D), dtype=np.float32)
    for core in range(8):
        b, qi = core // 4, core % 4
        yv = res.results[core]["y"]
        out[b, qi * Q:(qi + 1) * Q, :] = yv.transpose(1, 3, 2, 0).reshape(Q, D)
    return out


def make_in_maps(L0_attn_x, L0_attn_wq, L0_attn_wk, L0_attn_wv, L0_attn_wo,
                 L0_ffn_W_r, L0_ffn_W_k, L0_ffn_W_v, L0_ffn_mu_r, L0_ffn_mu_k):
    x = np.asarray(L0_attn_x, dtype=np.float32)
    wq = np.asarray(L0_attn_wq, dtype=np.float32)
    wk = np.asarray(L0_attn_wk, dtype=np.float32)
    wv = np.asarray(L0_attn_wv, dtype=np.float32)
    wo = np.asarray(L0_attn_wo, dtype=np.float32)
    Wr = np.asarray(L0_ffn_W_r, dtype=np.float32)
    Wk = np.asarray(L0_ffn_W_k, dtype=np.float32)
    Wv = np.asarray(L0_ffn_W_v, dtype=np.float32)
    mu_r = np.asarray(L0_ffn_mu_r, dtype=np.float32)
    mu_k = np.asarray(L0_ffn_mu_k, dtype=np.float32)

    try:
        import jax
        import jax.numpy as jnp
        with jax.default_device(jax.devices("cpu")[0]):
            inv_j = 1.0 / 10000.0 ** (jnp.arange(0, HD, 2, dtype=jnp.float32) / HD)
            t_j = jnp.arange(S - 1, dtype=jnp.float32)
            ang_j = t_j[:, None] * inv_j[None, :]
            cosr = np.asarray(jnp.cos(ang_j))
            sinr = np.asarray(jnp.sin(ang_j))
    except Exception:
        inv32 = (1.0 / (np.float32(10000.0) **
                        (np.arange(0, HD, 2, dtype=np.float32) / np.float32(HD)))
                 ).astype(np.float32)
        t32 = np.arange(S - 1, dtype=np.float32)
        ang32 = (t32[:, None] * inv32[None, :]).astype(np.float32)
        cosr = np.cos(ang32).astype(np.float32)
        sinr = np.sin(ang32).astype(np.float32)
    cosf = np.empty((64, S), dtype=np.float32)
    sinf = np.empty((64, S), dtype=np.float32)
    cosf[:, 1:] = cosr.T
    sinf[:, 1:] = sinr.T
    cosf[:, 0] = 1.0
    sinf[:, 0] = 0.0
    CS = np.concatenate([cosf, sinf], axis=0)
    SC = np.concatenate([sinf, cosf], axis=0)

    perm_h = np.concatenate([np.arange(0, HD, 2), np.arange(1, HD, 2)])

    def _wT_pch(w):
        return np.ascontiguousarray(w.T.reshape(DC, 128, HD).transpose(1, 0, 2))

    def _fT_pcdk(w):
        return np.ascontiguousarray(w.T.reshape(DC, 128, DC, 128).transpose(1, 0, 2, 3))

    aq_h = _wT_pch(wq[perm_h])
    ak_h = _wT_pch(wk[perm_h])
    av_h = _wT_pch(wv)
    ao_h = np.ascontiguousarray(wo.T)
    fr_h = _fT_pcdk(Wr)
    fk_h = _fT_pcdk(Wk)
    fv_h = _fT_pcdk(Wv)
    mur_h = np.ascontiguousarray(mu_r.reshape(DC, 128).T)
    muk_h = np.ascontiguousarray(mu_k.reshape(DC, 128).T)
    ones_h = np.ones((128, 128), dtype=np.float32)
    ident_h = np.eye(128, dtype=np.float32)

    in_maps = []
    for core in range(8):
        b, qi = core // 4, core % 4
        q0 = qi * Q
        if qi == 0:
            perm = np.concatenate([np.arange(S - NT, S), np.arange(0, Q), np.arange(Q, S - NT)])
            halo_h = np.zeros((128, 1), dtype=np.float32)
        else:
            perm = np.concatenate([np.arange(q0 - NT, q0 + Q), np.arange(0, q0 - NT),
                                   np.arange(q0 + Q, S)])
            halo_h = np.ones((128, 1), dtype=np.float32)
        xT_h = np.ascontiguousarray(
            x[b].T[:, perm].reshape(DC, 128, ST, NT).transpose(1, 2, 0, 3))
        in_maps.append({
            "xT": xT_h,
            "cs": np.ascontiguousarray(CS[:, perm]),
            "sc": np.ascontiguousarray(SC[:, perm]),
            "csq": np.ascontiguousarray(CS[:, q0:q0 + Q]),
            "scq": np.ascontiguousarray(SC[:, q0:q0 + Q]),
            "aq": aq_h, "ak": ak_h, "av": av_h, "ao": ao_h,
            "fr": fr_h, "fk": fk_h, "fv": fv_h,
            "mur": mur_h, "muk": muk_h,
            "onesd": ones_h, "identd": ident_h, "halo": halo_h,
        })
    return in_maps


# version 15
# speedup vs baseline: 1.1256x; 1.0110x over previous
import math
import numpy as np
import ml_dtypes
import concourse.bass as bass
import concourse.mybir as mybir
from concourse.bass_utils import run_bass_kernel_spmd

F32 = mybir.dt.float32
BF16 = mybir.dt.bfloat16
AF = mybir.ActivationFunctionType
ALU = mybir.AluOpType
AX = mybir.AxisListType

B = 2
S = 16384
D = 1024
HD = 128
NT = 512
ST = S // NT
Q = 4096
QT = Q // NT
DC = D // 128
KC = S // 128
NG = KC // 2
RSQ = 1.0 / math.sqrt(128.0)
EPS = 1e-5


class _Ser:

    def __init__(self, eng, sem, st):
        self._e = eng
        self._sem = sem
        self._st = st

    def wait_ge(self, sem, v):
        self._e.wait_ge(sem, v)

    def __getattr__(self, name):
        f = getattr(self._e, name)

        def g(*a, inc=None, edge=True, chain=True, **kw):
            p = self._st["prev"]
            if p is not None and chain:
                self._e.wait_ge(p[0], p[1])
            ins = f(*a, **kw)
            if inc is None:
                self._st["n"] += 1
                ins.then_inc(self._sem, 1)
                self._st["prev"] = (self._sem, self._st["n"])
            else:
                sem, k, after = inc
                ins.then_inc(sem, k)
                if edge and after is not None:
                    self._st["prev"] = (sem, after)
            return ins
        return g


def build_nc():
    nc = bass.Bass("TRN2", target_bir_lowering=False, dynamic_dma_scratch_size=16384)

    xT = nc.dram_tensor("xT", [128, ST, DC, NT], F32, kind="ExternalInput").ap()
    cs = nc.dram_tensor("cs", [128, S], F32, kind="ExternalInput").ap()
    sc = nc.dram_tensor("sc", [128, S], F32, kind="ExternalInput").ap()
    csq = nc.dram_tensor("csq", [128, Q], F32, kind="ExternalInput").ap()
    scq = nc.dram_tensor("scq", [128, Q], F32, kind="ExternalInput").ap()
    aq = nc.dram_tensor("aq", [128, DC, HD], F32, kind="ExternalInput").ap()
    ak = nc.dram_tensor("ak", [128, DC, HD], F32, kind="ExternalInput").ap()
    av = nc.dram_tensor("av", [128, DC, HD], F32, kind="ExternalInput").ap()
    ao = nc.dram_tensor("ao", [HD, D], F32, kind="ExternalInput").ap()
    fr = nc.dram_tensor("fr", [128, DC, DC, 128], F32, kind="ExternalInput").ap()
    fk = nc.dram_tensor("fk", [128, DC, DC, 128], F32, kind="ExternalInput").ap()
    fv = nc.dram_tensor("fv", [128, DC, DC, 128], F32, kind="ExternalInput").ap()
    mur = nc.dram_tensor("mur", [128, DC], F32, kind="ExternalInput").ap()
    muk = nc.dram_tensor("muk", [128, DC], F32, kind="ExternalInput").ap()
    onesd = nc.dram_tensor("onesd", [128, 128], F32, kind="ExternalInput").ap()
    identd = nc.dram_tensor("identd", [128, 128], F32, kind="ExternalInput").ap()
    halo = nc.dram_tensor("halo", [128, 1], F32, kind="ExternalInput").ap()
    y = nc.dram_tensor("y", [128, QT, DC, NT], F32, kind="ExternalOutput").ap()

    from contextlib import ExitStack
    with ExitStack() as top:
        ec = top.enter_context
        Ksb = ec(nc.sbuf_tensor("Ksb", [128, S], BF16))
        Vsb = ec(nc.sbuf_tensor("Vsb", [128, KC, 129], BF16))
        aq_s = ec(nc.sbuf_tensor("aq_s", [128, DC, 128], BF16))
        ak_s = ec(nc.sbuf_tensor("ak_s", [128, DC, 128], BF16))
        av_s = ec(nc.sbuf_tensor("av_s", [128, DC, 128], BF16))
        ao_s = ec(nc.sbuf_tensor("ao_s", [128, D], BF16))
        mur_s = ec(nc.sbuf_tensor("mur_s", [128, DC], F32))
        muk_s = ec(nc.sbuf_tensor("muk_s", [128, DC], F32))
        ones_s = ec(nc.sbuf_tensor("ones_s", [128, 128], BF16))
        id_s = ec(nc.sbuf_tensor("id_s", [128, 128], F32))
        halo_s = ec(nc.sbuf_tensor("halo_s", [128, 1], F32))
        w1k_s = ec(nc.sbuf_tensor("w1k_s", [1, 128], BF16))
        w1v_s = ec(nc.sbuf_tensor("w1v_s", [1, 128], BF16))
        eps_s = ec(nc.sbuf_tensor("eps_s", [1, 1], F32))
        m_sb = ec(nc.sbuf_tensor("m_sb", [1, NT], F32))
        e2_sb = ec(nc.sbuf_tensor("e2_sb", [1, NT], F32))
        var_sb = ec(nc.sbuf_tensor("var_sb", [1, NT], F32))
        r_sb = ec(nc.sbuf_tensor("r_sb", [1, NT], F32))
        mr_sb = ec(nc.sbuf_tensor("mr_sb", [1, NT], BF16))
        rhi_sb = ec(nc.sbuf_tensor("rhi_sb", [1, NT], BF16))
        rlo_sb = ec(nc.sbuf_tensor("rlo_sb", [1, NT], BF16))
        rb_sb = ec(nc.sbuf_tensor("rb_sb", [128, NT], F32))
        sm_sb = ec(nc.sbuf_tensor("sm_sb", [1, 8], F32))
        bbf_sb = ec(nc.sbuf_tensor("bbf_sb", [1, 1], BF16))
        bias_sb = ec(nc.sbuf_tensor("bias_sb", [128, 1], F32))
        rden_sb, dhi_sb, dlo_sb = r_sb, rhi_sb, rlo_sb
        xbf = ec(nc.sbuf_tensor("xbf", [128, 2, DC, NT], BF16))
        x2 = ec(nc.sbuf_tensor("x2", [128, 4, NT], BF16))
        tab = ec(nc.sbuf_tensor("tab", [128, 1, 2, NT], F32))
        A = ec(nc.psum_tensor("A", [128, 1024], F32))
        Bp = ec(nc.psum_tensor("Bp", [128, 1024], F32))
        Dp = ec(nc.psum_tensor("Dp", [128, 512], F32))
        G = ec(nc.psum_tensor("G", [128, 512], F32))
        Pq = ec(nc.psum_tensor("Pq", [128, 4, 256], F32))
        PqF = Pq.ap().rearrange("p a n -> p (a n)")

        sems = {}
        for name in ["s_w", "s_misc", "s_w1", "s_w1e", "s_init",
                     "s_x0", "s_x1", "s_tab", "s_sq", "s_serows", "s_st", "s_var",
                     "s_fin", "s_fin2", "s_rb", "s_rbe", "s_csr", "s_tek",
                     "s_tev", "s_kp", "s_k2s", "s_k2m", "s_kmax", "s_vs",
                     "s_vt", "s_ve2", "s_p1d",
                     "s_xf", "s_xb2", "s_tabq", "s_fwk", "s_fwr", "s_fwv",
                     "s_st2", "s_sq2",
                     "s_srow2", "s_var2", "s_fin_2", "s_rows2", "s_rbmm2",
                     "s_xn", "s_q", "s_qr", "s_q2", "s_qn", "s_qmaxd",
                     "s_qb", "s_bmm", "s_bias", "s_qk", "s_exp", "s_av",
                     "s_den", "s_dbm", "s_o", "s_mix", "s_kpre", "s_rl",
                     "s_kh", "s_rpre", "s_g", "s_facc", "s_out", "s_store",
                     "s_vser", "s_sser", "s_yini", "s_xb3", "s_oev", "s_sqm", "s_sqm2", "s_rbcp"]:
            sems[name] = ec(nc.semaphore(name))
        sm = type("SM", (), sems)
        vst = {"prev": None, "n": 0}
        sst = {"prev": None, "n": 0}

        block = ec(nc.Block())

        with ExitStack() as p1:
            e1 = p1.enter_context
            csr = e1(nc.sbuf_tensor("csr", [128, NT], F32))
            scr = e1(nc.sbuf_tensor("scr", [128, NT], F32))
            p1s = e1(nc.sbuf_tensor("p1s", [128, NT], F32))
            p2s = e1(nc.sbuf_tensor("p2s", [128, NT], F32))
            vt = e1(nc.sbuf_tensor("vt", [128, NT], F32))
            k2 = e1(nc.sbuf_tensor("k2", [128, NT], BF16))

            @block.gpsimd
            def _(g):
                for wi, (dst, srcw) in enumerate([(aq_s, aq), (ak_s, ak), (av_s, av),
                                                  (ao_s, ao), (ones_s, onesd), (id_s, identd)]):
                    if wi >= 2:
                        g.wait_ge(sm.s_w, 16 * wi)
                    g.dma_start(out=dst[:], in_=srcw).then_inc(sm.s_w, 16)
                for t in range(ST):
                    if t >= 2:
                        g.wait_ge(sm.s_st, t - 1)
                        g.wait_ge(sm.s_tek, t - 1)
                        g.wait_ge(sm.s_tev, t - 1)
                        g.wait_ge(sm.s_sq, 8 * (t - 1))
                    g.dma_start(out=xbf[:, t % 2, :, :],
                                in_=xT[:, t, :, :]).then_inc(
                                    sm.s_x0 if t % 2 == 0 else sm.s_x1, 16)
                g.nop().then_inc(sm.s_p1d, 1)

            @block.sync
            def _(s):
                s.dma_start(out=mur_s[:], in_=mur).then_inc(sm.s_misc, 16)
                s.dma_start(out=muk_s[:], in_=muk).then_inc(sm.s_misc, 16)
                s.dma_start(out=halo_s[:], in_=halo).then_inc(sm.s_misc, 16)
                for t in range(ST):
                    if t >= 1:
                        s.wait_ge(sm.s_csr, t)
                    s.dma_start(out=tab[:, 0, 0, :], in_=cs[:, t * NT:(t + 1) * NT]).then_inc(sm.s_tab, 16)
                    s.dma_start(out=tab[:, 0, 1, :], in_=sc[:, t * NT:(t + 1) * NT]).then_inc(sm.s_tab, 16)
                s.nop().then_inc(sm.s_p1d, 1)

            @block.tensor
            def _(t):
                t.wait_ge(sm.s_w, 96)
                for c in range(DC):
                    ins = t.matmul(Dp[0:1, :128], ones_s[:, 0:1], ak_s[:, c, :],
                                   start=(c == 0), stop=(c == DC - 1))
                ins.then_inc(sm.s_w1, 1)
                for c in range(DC):
                    ins = t.matmul(Dp[32:33, :128], ones_s[:, 0:1], av_s[:, c, :],
                                   start=(c == 0), stop=(c == DC - 1))
                ins.then_inc(sm.s_w1, 1)

                def stats_mms(tt):
                    t.wait_ge(sm.s_x0 if tt % 2 == 0 else sm.s_x1, 16 * (tt // 2 + 1))
                    t.wait_ge(sm.s_serows, tt)
                    if tt == 0:
                        t.wait_ge(sm.s_w1e, 1)
                    for c in range(DC):
                        ins = t.matmul(Dp[0:1, :], ones_s[:, 0:1], xbf[:, tt % 2, c, :],
                                       start=(c == 0), stop=(c == DC - 1))
                    ins.then_inc(sm.s_st, 1)
                    for c in range(DC):
                        t.wait_ge(sm.s_sq, 8 * tt + c + 1)
                        t.matmul(Dp[32:33, :], ones_s[:, 0:1], x2[:, c % 4, :],
                                 start=(c == 0), stop=(c == DC - 1)).then_inc(sm.s_sqm, 1)

                def mid_mms(tt):
                    h0 = (tt % 2) * 512
                    t.wait_ge(sm.s_fin2, tt + 1)
                    t.wait_ge(sm.s_rbe, tt)
                    t.matmul(G[:, :], ones_s[0:1, :], rhi_sb[0:1, :], start=True, stop=False)
                    t.matmul(G[:, :], ones_s[0:1, :], rlo_sb[0:1, :], start=False, stop=True).then_inc(sm.s_rb, 1)
                    t.wait_ge(sm.s_kp, max(0, tt - 1))
                    for c in range(DC):
                        t.matmul(A[:, h0:h0 + 512], ak_s[:, c, :], xbf[:, tt % 2, c, :],
                                 start=(c == 0), stop=False)
                    t.matmul(A[:, h0:h0 + 512], w1k_s[0:1, :], mr_sb[0:1, :],
                             start=False, stop=True).then_inc(sm.s_tek, 1)
                    t.wait_ge(sm.s_vs, max(0, tt - 1))
                    for c in range(DC):
                        t.matmul(Bp[:, h0:h0 + 512], av_s[:, c, :], xbf[:, tt % 2, c, :],
                                 start=(c == 0), stop=False)
                    t.matmul(Bp[:, h0:h0 + 512], w1v_s[0:1, :], mr_sb[0:1, :],
                             start=False, stop=True).then_inc(sm.s_tev, 1)

                def tail_mms(tt):
                    t.wait_ge(sm.s_vs, tt + 1)
                    t.wait_ge(sm.s_ve2, tt)
                    for j in range(4):
                        t.transpose(Pq[:, j, 0:128],
                                    vt[:, j * 128:(j + 1) * 128],
                                    id_s[:]).then_inc(sm.s_vt, 1)
                    t.wait_ge(sm.s_k2s, tt + 1)
                    t.wait_ge(sm.s_kmax, tt)
                    t.matmul(Dp[64:65, :], ones_s[:, 0:1], k2[:, :],
                             start=True, stop=True).then_inc(sm.s_k2m, 1)

                for stage in range(ST + 2):
                    if stage < ST:
                        stats_mms(stage)
                    if 1 <= stage <= ST:
                        mid_mms(stage - 1)
                    if stage >= 2:
                        tail_mms(stage - 2)
                t.wait_ge(sm.s_w1, 2)
                t.wait_ge(sm.s_st, ST)
                t.wait_ge(sm.s_sqm, 8 * ST)
                t.wait_ge(sm.s_rb, ST)
                t.wait_ge(sm.s_tek, ST)
                t.wait_ge(sm.s_tev, ST)
                t.wait_ge(sm.s_vt, 4 * ST)
                t.wait_ge(sm.s_k2m, ST)
                t.nop().then_inc(sm.s_p1d, 1)

            @block.scalar
            def _(se0):
                se = _Ser(se0, sm.s_sser, sst)
                se.wait_ge(sm.s_init, 1)
                for stage in range(ST + 2):
                    if stage < ST:
                        tt = stage
                        se.wait_ge(sm.s_x0 if tt % 2 == 0 else sm.s_x1, 16 * (tt // 2 + 1))
                        for c in range(DC):
                            sidx = 8 * tt + c
                            se.wait_ge(sm.s_sqm, max(0, sidx - 3))
                            se.activation(out=x2[:, c % 4, :], in_=xbf[:, tt % 2, c, :],
                                          func=AF.Square, inc=(sm.s_sq, 1, sidx + 1), chain=False)
                        se.wait_ge(sm.s_st, tt + 1)
                        se.wait_ge(sm.s_sqm, 8 * (tt + 1))
                        se.wait_ge(sm.s_var, tt)
                        se.wait_ge(sm.s_fin2, tt)
                        se.activation(out=m_sb[0:1, :], in_=Dp[0:1, :], func=AF.Copy,
                                      scale=1.0 / D)
                        se.activation(out=e2_sb[0:1, :], in_=Dp[32:33, :], func=AF.Copy,
                                      scale=1.0 / D, inc=(sm.s_serows, 1, tt + 1))
                        se.wait_ge(sm.s_var, tt + 1)
                        se.activation(out=var_sb[0:1, :], in_=var_sb[0:1, :], func=AF.Ln,
                                      bias=eps_s[0:1, :])
                        se.activation(out=r_sb[0:1, :], in_=var_sb[0:1, :], func=AF.Exp,
                                      scale=-0.5, inc=(sm.s_fin, 1, tt + 1))
                    if stage >= 2:
                        tt = stage - 2
                        se.wait_ge(sm.s_kp, tt + 1)
                        se.wait_ge(sm.s_k2m, tt)
                        se.activation(out=k2[:, :], in_=Ksb[:, tt * NT:(tt + 1) * NT],
                                      func=AF.Square, inc=(sm.s_k2s, 1, tt + 1))
                se.nop(inc=(sm.s_p1d, 1, None))

            @block.vector
            def _(v0):
                v = _Ser(v0, sm.s_vser, vst)
                v.memset(sm_sb[0:1, :], 0.0)
                v.memset(Vsb[:, :, 128:129], 1.0)
                v.memset(eps_s[0:1, :], EPS, inc=(sm.s_init, 1, 1))
                v.wait_ge(sm.s_w1, 2)
                v.tensor_scalar(out=w1k_s[0:1, :], in0=Dp[0:1, :128], scalar1=-1.0,
                                scalar2=None, op0=ALU.mult)
                v.tensor_scalar(out=w1v_s[0:1, :], in0=Dp[32:33, :128], scalar1=-1.0,
                                scalar2=None, op0=ALU.mult, inc=(sm.s_w1e, 1, 1))
                for stage in range(ST + 2):
                    if stage < ST:
                        tt = stage
                        v.wait_ge(sm.s_serows, tt + 1)
                        v.wait_ge(sm.s_fin, tt)
                        v.tensor_tensor(out=var_sb[0:1, :], in0=m_sb[0:1, :],
                                        in1=m_sb[0:1, :], op=ALU.mult)
                        v.tensor_tensor(out=var_sb[0:1, :], in0=e2_sb[0:1, :],
                                        in1=var_sb[0:1, :], op=ALU.subtract,
                                        inc=(sm.s_var, 1, tt + 1))
                        v.wait_ge(sm.s_fin, tt + 1)
                        v.wait_ge(sm.s_tek, tt)
                        v.wait_ge(sm.s_tev, tt)
                        v.wait_ge(sm.s_rb, tt)
                        v.tensor_copy(mr_sb[0:1, :], m_sb[0:1, :])
                        v.tensor_copy(rhi_sb[0:1, :], r_sb[0:1, :])
                        v.tensor_tensor(out=rlo_sb[0:1, :], in0=r_sb[0:1, :],
                                        in1=rhi_sb[0:1, :], op=ALU.subtract,
                                        inc=(sm.s_fin2, 1, tt + 1))
                    if 1 <= stage <= ST:
                        tt = stage - 1
                        h0 = (tt % 2) * 512
                        v.wait_ge(sm.s_rb, tt + 1)
                        v.tensor_copy(rb_sb[:, :], G[:, :], inc=(sm.s_rbe, 1, tt + 1))
                        v.wait_ge(sm.s_tab, 32 * (tt + 1))
                        v.tensor_tensor(out=csr[:, :], in0=tab[:, 0, 0, :],
                                        in1=rb_sb[:, :], op=ALU.mult)
                        v.tensor_tensor(out=scr[:, :], in0=tab[:, 0, 1, :],
                                        in1=rb_sb[:, :], op=ALU.mult,
                                        inc=(sm.s_csr, 1, tt + 1))
                        v.wait_ge(sm.s_tek, tt + 1)
                        v.tensor_tensor(out=p1s[0:64, :], in0=csr[0:64, :], in1=A[0:64, h0:h0 + 512], op=ALU.mult)
                        v.tensor_tensor(out=p2s[0:64, :], in0=scr[0:64, :], in1=A[64:128, h0:h0 + 512], op=ALU.mult)
                        v.tensor_tensor(out=Ksb[0:64, tt * NT:(tt + 1) * NT],
                                        in0=p1s[0:64, :], in1=p2s[0:64, :], op=ALU.subtract)
                        v.tensor_tensor(out=p1s[64:128, :], in0=csr[64:128, :], in1=A[0:64, h0:h0 + 512], op=ALU.mult)
                        v.tensor_tensor(out=p2s[64:128, :], in0=scr[64:128, :], in1=A[64:128, h0:h0 + 512], op=ALU.mult)
                        v.tensor_tensor(out=Ksb[64:128, tt * NT:(tt + 1) * NT],
                                        in0=p1s[64:128, :], in1=p2s[64:128, :], op=ALU.add,
                                        inc=(sm.s_kp, 1, tt + 1))
                        v.wait_ge(sm.s_tev, tt + 1)
                        v.wait_ge(sm.s_vt, 4 * tt)
                        v.tensor_tensor(out=vt[:, :], in0=Bp[:, h0:h0 + 512], in1=rb_sb[:, :],
                                        op=ALU.mult, inc=(sm.s_vs, 1, tt + 1))
                    if stage >= 2:
                        tt = stage - 2
                        v.wait_ge(sm.s_vt, 4 * (tt + 1))
                        v.tensor_copy(Vsb[:, tt * 4:(tt + 1) * 4, 0:128], Pq[:, :, 0:128],
                                      inc=(sm.s_ve2, 1, tt + 1))
                        v.wait_ge(sm.s_k2m, tt + 1)
                        v.reduce_max(sm_sb[0:1, 1:2], Dp[64:65, :], axis=AX.X)
                        v.tensor_tensor(out=sm_sb[0:1, 0:1], in0=sm_sb[0:1, 0:1],
                                        in1=sm_sb[0:1, 1:2], op=ALU.max,
                                        inc=(sm.s_kmax, 1, tt + 1))
                v.nop(inc=(sm.s_p1d, 1, None))

        with ExitStack() as p2:
            e2 = p2.enter_context
            fbuf = [e2(nc.sbuf_tensor(f"fbuf{i}", [128, DC, DC, 128], BF16)) for i in range(2)]
            xn = e2(nc.sbuf_tensor("xn", [128, 2, DC, NT + 1], BF16))
            q_sb = e2(nc.sbuf_tensor("q_sb", [128, NT], BF16))
            q2_sb = e2(nc.sbuf_tensor("q2_sb", [128, NT], BF16))
            Pr = e2(nc.sbuf_tensor("Pr", [128, 2, 1024], BF16))
            o_sb = e2(nc.sbuf_tensor("o_sb", [128, NT], BF16))
            oq_sb = e2(nc.sbuf_tensor("oq_sb", [128, NT], F32))
            recip_sb = e2(nc.sbuf_tensor("recip_sb", [128, 4], F32))
            xr_sb = e2(nc.sbuf_tensor("xr_sb", [128, DC, NT], BF16))
            xk_sb = e2(nc.sbuf_tensor("xk_sb", [128, DC, NT], BF16))
            kh_sb = e2(nc.sbuf_tensor("kh_sb", [128, DC, NT], BF16))
            d_sb = e2(nc.sbuf_tensor("d_sb", [128, NT], BF16))
            tt_sb = e2(nc.sbuf_tensor("tt_sb", [128, 1, NT], BF16))
            t1_sb = e2(nc.sbuf_tensor("t1_sb", [128, NT], F32))
            outs = e2(nc.sbuf_tensor("outs", [128, 1, NT], F32))
            g_sb = e2(nc.sbuf_tensor("g_sb", [128, DC, NT], BF16))
            p1q, p2q = t1_sb, rb_sb

            NL = QT + 1

            @block.gpsimd
            def _(g):
                def xload(e):
                    if e >= 2:
                        g.wait_ge(sm.s_st2, e - 1)
                        g.wait_ge(sm.s_sq2, 8 * (e - 1))
                        g.wait_ge(sm.s_xn, e - 1)
                    g.dma_start(out=xbf[:, e % 2, :, :],
                                in_=xT[:, e, :, :]).then_inc(
                                    sm.s_xb2 if e % 2 == 0 else sm.s_xb3, 16)

                g.wait_ge(sm.s_p1d, 5)
                xload(0)
                xload(1)
                xload(2)
                for ell in range(1, NL):
                    if ell + 2 < NL:
                        xload(ell + 2)
                    if ell >= 1:
                        u = ell - 1
                        for i, (w, fsem) in enumerate([(fk, sm.s_fwk), (fr, sm.s_fwr),
                                                       (fv, sm.s_fwv)]):
                            li = 3 * u + i
                            buf = fbuf[li % 2]
                            if i == 0 and u >= 1:
                                g.wait_ge(sm.s_rpre, 8 * u)
                            if i == 1 and u >= 1:
                                g.wait_ge(sm.s_facc, 16 * u)
                            if i == 1:
                                g.wait_ge(sm.s_fwk, 16 * (u + 1))
                            if i == 2:
                                g.wait_ge(sm.s_kpre, 8 * u + 8)
                                g.wait_ge(sm.s_fwr, 16 * (u + 1))
                            g.dma_start(out=buf[:], in_=w).then_inc(fsem, 16)
                    if ell >= 1:
                        u = ell - 1
                        if ell == 1:
                            g.wait_ge(sm.s_yini, 16)
                        for dc in range(DC):
                            g.wait_ge(sm.s_out, 8 * u + dc + 1)
                            if 8 * u + dc >= 1:
                                g.wait_ge(sm.s_store, 16 * (8 * u + dc))
                            g.dma_start(out=y[:, u, dc, :], in_=outs[:, 0, :],
                                        accum_op=ALU.add).then_inc(sm.s_store, 16)
                g.wait_ge(sm.s_store, 16 * 8 * QT)

            @block.sync
            def _(s):
                s.wait_ge(sm.s_p1d, 5)
                s.dma_start(out=y[:, :, :, :], in_=xT[:, 1:QT + 1, :, :]).then_inc(sm.s_yini, 16)
                for ell in range(NL):
                    u = ell - 1
                    if ell >= 1:
                        s.wait_ge(sm.s_qr, u)
                        s.dma_start(out=tab[:, 0, 0, :], in_=csq[:, u * NT:(u + 1) * NT]).then_inc(sm.s_tabq, 16)
                        s.dma_start(out=tab[:, 0, 1, :], in_=scq[:, u * NT:(u + 1) * NT]).then_inc(sm.s_tabq, 16)

            @block.tensor
            def _(t):
                t.wait_ge(sm.s_p1d, 5)

                def stats_x(e):
                    t.wait_ge(sm.s_xb2 if e % 2 == 0 else sm.s_xb3, 16 * (e // 2 + 1))
                    t.wait_ge(sm.s_srow2, e)
                    for c in range(DC):
                        ins = t.matmul(Dp[32:33, :], ones_s[:, 0:1], xbf[:, e % 2, c, :],
                                       start=(c == 0), stop=(c == DC - 1))
                    ins.then_inc(sm.s_st2, 1)

                def stats_x2(e, c):
                    t.wait_ge(sm.s_sq2, 8 * e + c + 1)
                    t.matmul(Dp[64:65, :], ones_s[:, 0:1], x2[:, c % 4, :],
                             start=(c == 0), stop=(c == DC - 1)).then_inc(sm.s_sqm2, 1)

                def stats_g(e):
                    stats_x(e)
                    for c in range(DC):
                        stats_x2(e, c)

                def bcast_g(e):
                    t.wait_ge(sm.s_rows2, e + 1)
                    t.wait_ge(sm.s_xn, max(0, e))
                    t.wait_ge(sm.s_bias, max(0, e - 1))
                    t.wait_ge(sm.s_out, 8 * max(0, e - 2))
                    t.matmul(G[:, :], ones_s[0:1, :], rhi_sb[0:1, :], start=True, stop=False)
                    t.matmul(G[:, :], ones_s[0:1, :], rlo_sb[0:1, :],
                             start=False, stop=True).then_inc(sm.s_rbmm2, 1)

                def mr_g(e):
                    t.wait_ge(sm.s_rbcp, e + 1)
                    t.matmul(G[:, :], ones_s[0:1, :], mr_sb[0:1, :],
                             start=True, stop=True).then_inc(sm.s_rbmm2, 1)

                def qproj_g(uu):
                    t.wait_ge(sm.s_xn, uu + 2)
                    t.wait_ge(sm.s_o, 4 * uu)
                    t.wait_ge(sm.s_den, uu)
                    for c in range(DC):
                        ins = t.matmul(PqF[:, 0:512], aq_s[:, c, :], xn[:, (uu + 1) % 2, c, 1:NT + 1],
                                       start=(c == 0), stop=(c == DC - 1))
                    ins.then_inc(sm.s_q, 1)

                def qnorm_g(uu):
                    t.wait_ge(sm.s_q2, uu + 1)
                    t.wait_ge(sm.s_qmaxd, uu)
                    t.matmul(Dp[0:1, :], ones_s[:, 0:1], q2_sb[:, :],
                             start=True, stop=True).then_inc(sm.s_qn, 1)
                    t.wait_ge(sm.s_qb, uu + 1)
                    t.wait_ge(sm.s_xn, min(uu + 2, NL))
                    t.matmul(G[:, 0:1], ones_s[0:1, :], bbf_sb[0:1, :],
                             start=True, stop=True).then_inc(sm.s_bmm, 1)

                def flash(u, ell):
                    t.wait_ge(sm.s_qr, u + 1)
                    t.wait_ge(sm.s_bias, u + 1)
                    t.wait_ge(sm.s_den, u)
                    t.wait_ge(sm.s_o, 4 * u)
                    t.wait_ge(sm.s_oev, u)
                    if u >= 1:
                        t.wait_ge(sm.s_g, 8 * u)
                        t.wait_ge(sm.s_rl, 8 * u)
                        t.wait_ge(sm.s_out, 8 * u)

                    def av_g(gj):
                        t.wait_ge(sm.s_exp, 64 * u + gj + 1)
                        for h in range(2):
                            kc = 2 * gj + h
                            for j in range(4):
                                t.matmul(Pq[:, j, 0:129],
                                         Pr[:, gj % 2, h * 512 + j * 128:h * 512 + (j + 1) * 128],
                                         Vsb[:, kc, :],
                                         start=(kc == 0 and h == 0 and j % 2 == 0),
                                         stop=(kc == KC - 1 and h == 1 and j % 2 == 1)).then_inc(sm.s_av, 1)

                    for gi in range(NG):
                        pg = A if gi % 2 == 0 else Bp
                        t.wait_ge(sm.s_exp, max(0, 64 * u + gi - 1))
                        for h in range(2):
                            kc = 2 * gi + h
                            t.matmul(pg[:, h * 512:(h + 1) * 512],
                                     Ksb[:, kc * 128:(kc + 1) * 128], q_sb[:, :],
                                     start=True, stop=True).then_inc(sm.s_qk, 1)
                        if ell + 1 < NL:
                            if gi == 2:
                                stats_x(ell + 1)
                            if gi == 7:
                                for c2 in range(4):
                                    stats_x2(ell + 1, c2)
                            if gi == 11:
                                for c2 in range(4, DC):
                                    stats_x2(ell + 1, c2)
                            if gi == 18:
                                bcast_g(ell + 1)
                            if gi == 22:
                                mr_g(ell + 1)
                        if gi >= 1:
                            av_g(gi - 1)
                    av_g(NG - 1)

                stats_g(0)
                bcast_g(0)
                mr_g(0)
                stats_g(1)
                bcast_g(1)
                mr_g(1)
                qproj_g(0)
                qnorm_g(0)

                for ell in range(1, NL):
                    u = ell - 1
                    flash(u, ell)
                    t.wait_ge(sm.s_fwk, 16 * (u + 1))
                    for dc in range(DC):
                        pf = (A if dc % 2 == 0 else Bp)
                        t.wait_ge(sm.s_mix, 8 * u + dc + 1)
                        t.wait_ge(sm.s_rl, max(0, 8 * u + dc - 1))
                        for c in range(DC):
                            ins = t.matmul(pf[:, 0:512], fbuf[(3 * u) % 2][:, c, dc, :],
                                           xk_sb[:, c, :], start=(c == 0), stop=(c == DC - 1))
                        ins.then_inc(sm.s_kpre, 1)
                    t.wait_ge(sm.s_o, 4 * (u + 1))
                    t.wait_ge(sm.s_oev, u)
                    for j in range(4):
                        t.transpose(Pq[:, 2 + j // 2, (j % 2) * 128:(j % 2) * 128 + 128],
                                    oq_sb[:, j * 128:(j + 1) * 128],
                                    id_s[:]).then_inc(sm.s_dbm, 1)
                    if u + 1 <= QT - 1:
                        qproj_g(u + 1)
                    t.wait_ge(sm.s_fwr, 16 * (u + 1))
                    for dc in range(DC):
                        pf = (A if dc % 2 == 0 else Bp)
                        t.wait_ge(sm.s_g, max(0, 8 * u + dc - 1))
                        if dc in (0, 1):
                            t.wait_ge(sm.s_rl, 8 * u + 7 + dc)
                        for c in range(DC):
                            ins = t.matmul(pf[:, 0:512], fbuf[(3 * u + 1) % 2][:, c, dc, :],
                                           xr_sb[:, c, :], start=(c == 0), stop=(c == DC - 1))
                        ins.then_inc(sm.s_rpre, 1)
                    if u + 1 <= QT - 1:
                        qnorm_g(u + 1)
                    t.wait_ge(sm.s_fwv, 16 * (u + 1))
                    t.wait_ge(sm.s_oev, u + 1)
                    t.wait_ge(sm.s_kh, 8 * u + 8)
                    t.wait_ge(sm.s_qr, min(u + 2, QT))
                    t.wait_ge(sm.s_bias, min(u + 2, QT))
                    for dc in range(DC):
                        pacc = (A if dc % 2 == 0 else Bp)
                        pff = (PqF[:, 512:1024] if dc % 2 == 0 else G[:, 0:512])
                        t.wait_ge(sm.s_out, max(0, 8 * u + dc - 1))
                        if dc in (0, 1):
                            t.wait_ge(sm.s_g, 8 * u + 7 + dc)
                        t.matmul(pacc[:, 512:1024], ao_s[:, dc * 128:(dc + 1) * 128],
                                 o_sb[:, :], start=True, stop=True).then_inc(sm.s_facc, 1)
                        for c in range(DC):
                            ins = t.matmul(pff, fbuf[(3 * u + 2) % 2][:, c, dc, :],
                                           kh_sb[:, c, :], start=(c == 0), stop=(c == DC - 1))
                        ins.then_inc(sm.s_facc, 1)

            @block.scalar
            def _(se0):
                se = _Ser(se0, sm.s_sser, sst)
                se.wait_ge(sm.s_p1d, 5)

                def sq_c(e, c):
                    sidx = 8 * e + c
                    if c == 0:
                        se.wait_ge(sm.s_xb2 if e % 2 == 0 else sm.s_xb3, 16 * (e // 2 + 1))
                    se.wait_ge(sm.s_sqm2, max(0, sidx - 3))
                    se.activation(out=x2[:, c % 4, :], in_=xbf[:, e % 2, c, :],
                                  func=AF.Square, inc=(sm.s_sq2, 1, sidx + 1), chain=False)

                def sq_g(e):
                    for c in range(DC):
                        sq_c(e, c)

                def rows_g_a(e):
                    se.wait_ge(sm.s_st2, e + 1)
                    se.wait_ge(sm.s_sqm2, 8 * (e + 1))
                    se.wait_ge(sm.s_var2, e)
                    se.wait_ge(sm.s_rows2, e)
                    se.activation(out=m_sb[0:1, :], in_=Dp[32:33, :], func=AF.Copy, scale=1.0 / D)
                    se.activation(out=e2_sb[0:1, :], in_=Dp[64:65, :], func=AF.Copy,
                                  scale=1.0 / D, inc=(sm.s_srow2, 1, e + 1))

                def rows_g_b(e):
                    se.wait_ge(sm.s_var2, e + 1)
                    se.wait_ge(sm.s_den, max(0, e - 2))
                    se.activation(out=var_sb[0:1, :], in_=var_sb[0:1, :], func=AF.Ln,
                                  bias=eps_s[0:1, :])
                    se.activation(out=r_sb[0:1, :], in_=var_sb[0:1, :], func=AF.Exp,
                                  scale=-0.5, inc=(sm.s_fin_2, 1, e + 1))

                def rows_g(e):
                    rows_g_a(e)
                    rows_g_b(e)

                def q2_g(uu):
                    se.wait_ge(sm.s_qr, uu + 1)
                    se.wait_ge(sm.s_qn, uu)
                    se.activation(out=q2_sb[:, :], in_=q_sb[:, :], func=AF.Square,
                                  inc=(sm.s_q2, 1, uu + 1))

                sq_g(0)
                rows_g(0)
                sq_g(1)
                rows_g(1)
                q2_g(0)

                for ell in range(1, NL):
                    u = ell - 1
                    se.wait_ge(sm.s_bias, u + 1)
                    for gi in range(NG):
                        pg = A if gi % 2 == 0 else Bp
                        se.wait_ge(sm.s_qk, 128 * u + 2 * gi + 2)
                        se.wait_ge(sm.s_av, max(0, 8 * (64 * u + gi - 1)))
                        se.activation(out=Pr[:, gi % 2, :], in_=pg[:, :], func=AF.Exp,
                                      scale=RSQ, bias=bias_sb[:, :],
                                      inc=(sm.s_exp, 1, 64 * u + gi + 1), chain=False)
                        if gi == 5 and ell + 1 < NL:
                            for c in range(4):
                                sq_c(ell + 1, c)
                        if gi == 9 and ell + 1 < NL:
                            for c in range(4, DC):
                                sq_c(ell + 1, c)
                        if gi == 13 and ell + 1 < NL:
                            rows_g_a(ell + 1)
                        if gi == 15 and ell + 1 < NL:
                            rows_g_b(ell + 1)
                    se.wait_ge(sm.s_den, u + 1)
                    se.wait_ge(sm.s_dbm, 4 * u)
                    for j in range(4):
                        se.activation(out=oq_sb[:, j * 128:(j + 1) * 128], in_=Pq[:, j, 0:128],
                                      func=AF.Copy, scale=recip_sb[:, j:j + 1],
                                      inc=(sm.s_o, 1, 4 * u + j + 1), chain=False)
                    for dc in range(DC):
                        pf = (A if dc % 2 == 0 else Bp)
                        se.wait_ge(sm.s_kpre, 8 * u + dc + 1)
                        se.wait_ge(sm.s_kh, 8 * u + dc)
                        se.activation(out=tt_sb[:, 0, :], in_=pf[:, 0:512], func=AF.Relu,
                                      inc=(sm.s_rl, 1, 8 * u + dc + 1))
                    se.wait_ge(sm.s_out, 8 * u)
                    for dc in range(DC):
                        pf = (A if dc % 2 == 0 else Bp)
                        se.wait_ge(sm.s_rpre, 8 * u + dc + 1)
                        se.wait_ge(sm.s_kh, 8 * u + 8)
                        se.activation(out=tt_sb[:, 0, :], in_=pf[:, 0:512], func=AF.Tanh, scale=0.5)
                        se.activation(out=g_sb[:, dc, :], in_=tt_sb[:, 0, :], func=AF.Copy,
                                      scale=0.5, bias=0.5, inc=(sm.s_g, 1, 8 * u + dc + 1))
                    if u + 1 <= QT - 1:
                        q2_g(u + 1)

            @block.vector
            def _(v0):
                v = _Ser(v0, sm.s_vser, vst)
                v.wait_ge(sm.s_p1d, 5)

                def rows_v(e):
                    v.wait_ge(sm.s_srow2, e + 1)
                    v.wait_ge(sm.s_fin_2, e)
                    v.tensor_tensor(out=var_sb[0:1, :], in0=m_sb[0:1, :], in1=m_sb[0:1, :], op=ALU.mult)
                    v.tensor_tensor(out=var_sb[0:1, :], in0=e2_sb[0:1, :], in1=var_sb[0:1, :],
                                    op=ALU.subtract, inc=(sm.s_var2, 1, e + 1))
                    v.wait_ge(sm.s_fin_2, e + 1)
                    v.wait_ge(sm.s_rbmm2, 2 * e)
                    v.tensor_tensor(out=mr_sb[0:1, :], in0=m_sb[0:1, :], in1=r_sb[0:1, :], op=ALU.mult)
                    v.tensor_copy(rhi_sb[0:1, :], r_sb[0:1, :])
                    v.tensor_tensor(out=rlo_sb[0:1, :], in0=r_sb[0:1, :], in1=rhi_sb[0:1, :],
                                    op=ALU.subtract, inc=(sm.s_rows2, 1, e + 1))

                def xn_v(e):
                    v.wait_ge(sm.s_rbmm2, 2 * e + 1)
                    v.wait_ge(sm.s_xb2 if e % 2 == 0 else sm.s_xb3, 16 * (e // 2 + 1))
                    if e >= 2:
                        v.wait_ge(sm.s_mix, 8 * (e - 1))
                    v.tensor_copy(rb_sb[:, :], G[:, :], inc=(sm.s_rbcp, 1, e + 1))
                    v.wait_ge(sm.s_rbmm2, 2 * (e + 1))
                    for dc in range(DC):
                        v.tensor_tensor(out=t1_sb[:, :], in0=xbf[:, e % 2, dc, :],
                                        in1=rb_sb[:, :], op=ALU.mult)
                        v.tensor_tensor(out=xn[:, e % 2, dc, 1:NT + 1], in0=t1_sb[:, :],
                                        in1=G[:, :], op=ALU.subtract)
                        if e >= 1:
                            v.tensor_copy(xn[:, e % 2, dc, 0:1], xn[:, (e - 1) % 2, dc, NT:NT + 1])
                        else:
                            v.memset(xn[:, 0, dc, 0:1], 0.0)
                    if e == 0:
                        v.wait_ge(sm.s_misc, 48)
                        for dc in range(DC):
                            v.tensor_scalar(out=xn[:, 0, dc, NT:NT + 1], in0=xn[:, 0, dc, NT:NT + 1],
                                            scalar1=halo_s[:, :], scalar2=None, op0=ALU.mult)
                    v.nop(inc=(sm.s_xn, 1, e + 1))

                def rope_v(uu):
                    v.wait_ge(sm.s_q, uu + 1)
                    v.wait_ge(sm.s_tabq, 32 * (uu + 1))
                    v.tensor_tensor(out=p1q[0:64, :], in0=tab[0:64, 0, 0, :], in1=PqF[0:64, 0:512], op=ALU.mult)
                    v.tensor_tensor(out=p2q[0:64, :], in0=tab[0:64, 0, 1, :], in1=PqF[64:128, 0:512], op=ALU.mult)
                    v.tensor_tensor(out=q_sb[0:64, :], in0=p1q[0:64, :], in1=p2q[0:64, :], op=ALU.subtract)
                    v.tensor_tensor(out=p1q[64:128, :], in0=tab[64:128, 0, 0, :], in1=PqF[0:64, 0:512], op=ALU.mult)
                    v.tensor_tensor(out=p2q[64:128, :], in0=tab[64:128, 0, 1, :], in1=PqF[64:128, 0:512], op=ALU.mult)
                    v.tensor_tensor(out=q_sb[64:128, :], in0=p1q[64:128, :], in1=p2q[64:128, :],
                                    op=ALU.add, inc=(sm.s_qr, 1, uu + 1))

                def mix_v(uu):
                    e = uu + 1
                    if uu >= 1:
                        v.wait_ge(sm.s_rpre, 8 * uu)
                        v.wait_ge(sm.s_kpre, 8 * uu)
                    for dc in range(DC):
                        v.tensor_tensor(out=d_sb[:, :], in0=xn[:, e % 2, dc, 1:NT + 1],
                                        in1=xn[:, e % 2, dc, 0:NT], op=ALU.subtract)
                        v.scalar_tensor_tensor(out=xr_sb[:, dc, :], in0=d_sb[:, :],
                                               scalar=mur_s[:, dc:dc + 1], in1=xn[:, e % 2, dc, 0:NT],
                                               op0=ALU.mult, op1=ALU.add)
                        v.scalar_tensor_tensor(out=xk_sb[:, dc, :], in0=d_sb[:, :],
                                               scalar=muk_s[:, dc:dc + 1], in1=xn[:, e % 2, dc, 0:NT],
                                               op0=ALU.mult, op1=ALU.add,
                                               inc=(sm.s_mix, 1, 8 * uu + dc + 1))

                def bias_v(uu):
                    v.wait_ge(sm.s_qn, uu + 1)
                    v.reduce_max(sm_sb[0:1, 1:2], Dp[0:1, :], axis=AX.X,
                                 inc=(sm.s_qmaxd, 1, uu + 1))
                    v.tensor_tensor(out=sm_sb[0:1, 2:3], in0=sm_sb[0:1, 1:2], in1=sm_sb[0:1, 0:1], op=ALU.add)
                    v.tensor_scalar(out=sm_sb[0:1, 3:4], in0=sm_sb[0:1, 2:3],
                                    scalar1=-0.5 * RSQ, scalar2=None, op0=ALU.mult)
                    v.tensor_copy(bbf_sb[0:1, :], sm_sb[0:1, 3:4], inc=(sm.s_qb, 1, uu + 1))
                    v.wait_ge(sm.s_bmm, uu + 1)
                    v.tensor_copy(bias_sb[:, :], G[:, 0:1], inc=(sm.s_bias, 1, uu + 1))

                rows_v(0)
                xn_v(0)
                rows_v(1)
                xn_v(1)
                rope_v(0)
                mix_v(0)
                bias_v(0)

                for ell in range(1, NL):
                    u = ell - 1
                    if ell + 1 < NL:
                        rows_v(ell + 1)
                        xn_v(ell + 1)
                    v.wait_ge(sm.s_av, 512 * (u + 1))
                    v.wait_ge(sm.s_o, 4 * u)
                    v.reciprocal(recip_sb[:, 0:4],
                                 Pq[:, :, 128:129].rearrange("p a n -> p (a n)"),
                                 inc=(sm.s_den, 1, u + 1))
                    if u >= 1:
                        v.wait_ge(sm.s_facc, 16 * u)
                    for dc in range(DC):
                        v.wait_ge(sm.s_rl, 8 * u + dc + 1)
                        v.tensor_tensor(out=kh_sb[:, dc, :], in0=tt_sb[:, 0, :],
                                        in1=tt_sb[:, 0, :],
                                        op=ALU.mult, inc=(sm.s_kh, 1, 8 * u + dc + 1),
                                        chain=False)
                    v.wait_ge(sm.s_dbm, 4 * (u + 1))
                    v.tensor_copy(o_sb.ap().rearrange("p (a n) -> p a n", n=128),
                                  Pq[:, 2:4, 0:256].rearrange("p a (b n) -> p (a b) n", n=128),
                                  inc=(sm.s_oev, 1, u + 1))
                    if u + 1 <= QT - 1:
                        rope_v(u + 1)
                    if u + 1 <= QT - 1:
                        bias_v(u + 1)
                        mix_v(u + 1)
                    for dc in range(DC):
                        pacc = (A if dc % 2 == 0 else Bp)
                        pff = (PqF[:, 512:1024] if dc % 2 == 0 else G[:, 0:512])
                        v.wait_ge(sm.s_facc, 16 * u + 2 * dc + 2)
                        v.wait_ge(sm.s_g, 8 * u + dc + 1)
                        if 8 * u + dc >= 1:
                            v.wait_ge(sm.s_store, 16 * (8 * u + dc))
                        v.tensor_tensor(out=t1_sb[:, :], in0=g_sb[:, dc, :], in1=pff, op=ALU.mult)
                        v.tensor_tensor(out=outs[:, 0, :], in0=t1_sb[:, :], in1=pacc[:, 512:1024],
                                        op=ALU.add, inc=(sm.s_out, 1, 8 * u + dc + 1))
    return nc


_NC_CACHE = None


def kernel(L0_attn_x, L0_attn_wq, L0_attn_wk, L0_attn_wv, L0_attn_wo,
           L0_ffn_W_r, L0_ffn_W_k, L0_ffn_W_v, L0_ffn_mu_r, L0_ffn_mu_k):
    global _NC_CACHE
    in_maps = make_in_maps(L0_attn_x, L0_attn_wq, L0_attn_wk, L0_attn_wv,
                           L0_attn_wo, L0_ffn_W_r, L0_ffn_W_k, L0_ffn_W_v,
                           L0_ffn_mu_r, L0_ffn_mu_k)
    if _NC_CACHE is None:
        _NC_CACHE = build_nc()
    res = run_bass_kernel_spmd(_NC_CACHE, in_maps, core_ids=list(range(8)))

    out = np.empty((B, S, D), dtype=np.float32)
    for core in range(8):
        b, qi = core // 4, core % 4
        yv = res.results[core]["y"]
        out[b, qi * Q:(qi + 1) * Q, :] = yv.transpose(1, 3, 2, 0).reshape(Q, D)
    return out


def make_in_maps(L0_attn_x, L0_attn_wq, L0_attn_wk, L0_attn_wv, L0_attn_wo,
                 L0_ffn_W_r, L0_ffn_W_k, L0_ffn_W_v, L0_ffn_mu_r, L0_ffn_mu_k):
    x = np.asarray(L0_attn_x, dtype=np.float32)
    wq = np.asarray(L0_attn_wq, dtype=np.float32)
    wk = np.asarray(L0_attn_wk, dtype=np.float32)
    wv = np.asarray(L0_attn_wv, dtype=np.float32)
    wo = np.asarray(L0_attn_wo, dtype=np.float32)
    Wr = np.asarray(L0_ffn_W_r, dtype=np.float32)
    Wk = np.asarray(L0_ffn_W_k, dtype=np.float32)
    Wv = np.asarray(L0_ffn_W_v, dtype=np.float32)
    mu_r = np.asarray(L0_ffn_mu_r, dtype=np.float32)
    mu_k = np.asarray(L0_ffn_mu_k, dtype=np.float32)

    try:
        import jax
        import jax.numpy as jnp
        with jax.default_device(jax.devices("cpu")[0]):
            inv_j = 1.0 / 10000.0 ** (jnp.arange(0, HD, 2, dtype=jnp.float32) / HD)
            t_j = jnp.arange(S - 1, dtype=jnp.float32)
            ang_j = t_j[:, None] * inv_j[None, :]
            cosr = np.asarray(jnp.cos(ang_j))
            sinr = np.asarray(jnp.sin(ang_j))
    except Exception:
        inv32 = (1.0 / (np.float32(10000.0) **
                        (np.arange(0, HD, 2, dtype=np.float32) / np.float32(HD)))
                 ).astype(np.float32)
        t32 = np.arange(S - 1, dtype=np.float32)
        ang32 = (t32[:, None] * inv32[None, :]).astype(np.float32)
        cosr = np.cos(ang32).astype(np.float32)
        sinr = np.sin(ang32).astype(np.float32)
    cosf = np.empty((64, S), dtype=np.float32)
    sinf = np.empty((64, S), dtype=np.float32)
    cosf[:, 1:] = cosr.T
    sinf[:, 1:] = sinr.T
    cosf[:, 0] = 1.0
    sinf[:, 0] = 0.0
    CS = np.concatenate([cosf, sinf], axis=0)
    SC = np.concatenate([sinf, cosf], axis=0)

    perm_h = np.concatenate([np.arange(0, HD, 2), np.arange(1, HD, 2)])

    def _wT_pch(w):
        return np.ascontiguousarray(w.T.reshape(DC, 128, HD).transpose(1, 0, 2))

    def _fT_pcdk(w):
        return np.ascontiguousarray(w.T.reshape(DC, 128, DC, 128).transpose(1, 0, 2, 3))

    aq_h = _wT_pch(wq[perm_h])
    ak_h = _wT_pch(wk[perm_h])
    av_h = _wT_pch(wv)
    ao_h = np.ascontiguousarray(wo.T)
    fr_h = _fT_pcdk(Wr)
    fk_h = _fT_pcdk(Wk)
    fv_h = _fT_pcdk(Wv)
    mur_h = np.ascontiguousarray(mu_r.reshape(DC, 128).T)
    muk_h = np.ascontiguousarray(mu_k.reshape(DC, 128).T)
    ones_h = np.ones((128, 128), dtype=np.float32)
    ident_h = np.eye(128, dtype=np.float32)

    in_maps = []
    for core in range(8):
        b, qi = core // 4, core % 4
        q0 = qi * Q
        if qi == 0:
            perm = np.concatenate([np.arange(S - NT, S), np.arange(0, Q), np.arange(Q, S - NT)])
            halo_h = np.zeros((128, 1), dtype=np.float32)
        else:
            perm = np.concatenate([np.arange(q0 - NT, q0 + Q), np.arange(0, q0 - NT),
                                   np.arange(q0 + Q, S)])
            halo_h = np.ones((128, 1), dtype=np.float32)
        xT_h = np.ascontiguousarray(
            x[b].T[:, perm].reshape(DC, 128, ST, NT).transpose(1, 2, 0, 3))
        in_maps.append({
            "xT": xT_h,
            "cs": np.ascontiguousarray(CS[:, perm]),
            "sc": np.ascontiguousarray(SC[:, perm]),
            "csq": np.ascontiguousarray(CS[:, q0:q0 + Q]),
            "scq": np.ascontiguousarray(SC[:, q0:q0 + Q]),
            "aq": aq_h, "ak": ak_h, "av": av_h, "ao": ao_h,
            "fr": fr_h, "fk": fk_h, "fv": fv_h,
            "mur": mur_h, "muk": muk_h,
            "onesd": ones_h, "identd": ident_h, "halo": halo_h,
        })
    return in_maps
